# version 08
# baseline (speedup 1.0000x reference)
import numpy as np
import ml_dtypes
import concourse.bass as bass
import concourse.mybir as mybir
from concourse.bass_utils import run_bass_kernel_spmd

F32 = mybir.dt.float32
BF16 = mybir.dt.bfloat16
AF = mybir.ActivationFunctionType
ALU = mybir.AluOpType
AX = mybir.AxisListType
BFNP = ml_dtypes.bfloat16

NCORES = 8


CTR_LOG = []


class Ctr:

    def __init__(self, nc, name):
        self.h = nc.alloc_semaphore(name)
        self.v = 0
        CTR_LOG.append(self)


def feat_ap(dram_ap, T, chunk0=0, nch=1, col0=0, ncols=None):
    if ncols is None:
        ncols = T
    return bass.AP(dram_ap.tensor, dram_ap.offset + chunk0 * 128 * T + col0,
                   [[T, 128], [T * 128, nch], [1, ncols]])


def emit_ffn(nc, aps, D=2048, I=5632, T=4096, NQ=4, tag="f0", down_only=False):
    NK = D // 128
    NI = I // 128
    NTB = T // 512
    ND = D // 128
    NDq = ND // NQ

    resid, dwT = aps["resid"], aps["dwT"]
    ds, hscr, out = aps["ds"], aps["hscr"], aps["out"]
    if down_only:
        _emit_ffn_b2(nc, resid, dwT, ds, hscr, out, D, I, T, NQ, tag)
        return
    xn, gwT, uwT = aps["xn"], aps["gwT"], aps["uwT"]
    gs, us = aps["gs"], aps["us"]

    with (
        nc.sbuf_tensor(f"{tag}xns", [128, NK * T], BF16) as xns,
        nc.sbuf_tensor(f"{tag}gw0", [128, D], BF16) as gw0,
        nc.sbuf_tensor(f"{tag}gw1", [128, D], BF16) as gw1,
        nc.sbuf_tensor(f"{tag}uw0", [128, D], BF16) as uw0,
        nc.sbuf_tensor(f"{tag}uw1", [128, D], BF16) as uw1,
        nc.sbuf_tensor(f"{tag}hb0", [128, T], BF16) as hb0,
        nc.sbuf_tensor(f"{tag}hb1", [128, T], BF16) as hb1,
        nc.sbuf_tensor(f"{tag}gss", [128, NI], F32) as gss,
        nc.sbuf_tensor(f"{tag}uss", [128, NI], F32) as uss,
        nc.sbuf_tensor(f"{tag}sil0", [128, 512], F32) as sil0,
        nc.sbuf_tensor(f"{tag}sil1", [128, 512], F32) as sil1,
        nc.psum_tensor(f"{tag}p0", [128, 512], F32) as p0,
        nc.psum_tensor(f"{tag}p1", [128, 512], F32) as p1,
        nc.psum_tensor(f"{tag}p2", [128, 512], F32) as p2,
        nc.psum_tensor(f"{tag}p3", [128, 512], F32) as p3,
        nc.Block() as blk,
    ):
        wbufs = [(gw0, uw0), (gw1, uw1)]
        hbufs = [hb0, hb1]
        sils = [sil0, sil1]
        psum = [p0, p1, p2, p3]
        ld = [Ctr(nc, f"{tag}ld0"), Ctr(nc, f"{tag}ld1")]
        xld = Ctr(nc, f"{tag}xld")
        mm = Ctr(nc, f"{tag}mm")
        evA = Ctr(nc, f"{tag}evA")
        evD = Ctr(nc, f"{tag}evD")
        hst = [Ctr(nc, f"{tag}hst0"), Ctr(nc, f"{tag}hst1")]

        @blk.gpsimd
        def _(g):
            g.dma_start(xns[:], feat_ap(xn, T, nch=NK)).then_inc(xld.h, 16)
            g.dma_start(gss[:], bass.AP(gs.tensor, gs.offset, [[NI, 128], [1, NI]])
                        ).then_inc(xld.h, 16)
            g.dma_start(uss[:], bass.AP(us.tensor, us.offset, [[NI, 128], [1, NI]])
                        ).then_inc(xld.h, 16)
            xld.v = 48
            for ic in range(NI):
                if ic >= 2:
                    g.wait_ge(mm.h, (ic - 1) * NTB * 2)
                gw, uw = wbufs[ic % 2]
                g.dma_start(gw[:], feat_ap(gwT, I, col0=ic * 128, ncols=128, nch=NK)
                            ).then_inc(ld[ic % 2].h, 16)
                g.dma_start(uw[:], feat_ap(uwT, I, col0=ic * 128, ncols=128, nch=NK)
                            ).then_inc(ld[ic % 2].h, 16)
                ld[ic % 2].v += 32

        @blk.tensor
        def _(t):
            t.wait_ge(xld.h, xld.v)
            for ic in range(NI):
                gw, uw = wbufs[ic % 2]
                t.wait_ge(ld[ic % 2].h, 32 * (ic // 2 + 1))
                for tb in range(NTB):
                    j = ic * NTB + tb
                    for half, w in ((0, gw), (1, uw)):
                        n = 2 * j + half
                        bank = psum[n % 4]
                        if n >= 4:
                            if half == 0:
                                t.wait_ge(evA.h, (n - 4) // 2 + 1)
                            else:
                                t.wait_ge(evD.h, (n - 5) // 2 + 1)
                        ins = None
                        for k in range(NK):
                            ins = t.matmul(bank[:],
                                           w[:, k * 128:(k + 1) * 128],
                                           xns[:, k * T + tb * 512: k * T + tb * 512 + 512],
                                           start=(k == 0), stop=(k == NK - 1))
                        ins.then_inc(mm.h, 1)
            mm.v = 2 * NI * NTB

        @blk.scalar
        def _(s):
            s.wait_ge(xld.h, 48)
            for ic in range(NI):
                for tb in range(NTB):
                    j = ic * NTB + tb
                    if j >= 2:
                        s.wait_ge(evD.h, j - 1)
                    s.wait_ge(mm.h, 2 * j + 1)
                    s.activation(sils[j % 2][:], psum[(2 * j) % 4][:], AF.Silu,
                                 scale=gss[:, ic:ic + 1]).then_inc(evA.h, 1)
            evA.v = NI * NTB

        @blk.vector
        def _(v):
            for ic in range(NI):
                hbuf = hbufs[ic % 2]
                if ic >= 2:
                    v.wait_ge(hst[ic % 2].h, 16 * (ic // 2))
                for tb in range(NTB):
                    j = ic * NTB + tb
                    v.wait_ge(mm.h, 2 * j + 2)
                    v.wait_ge(evA.h, j + 1)
                    v.scalar_tensor_tensor(hbuf[:, tb * 512:(tb + 1) * 512],
                                           psum[(2 * j + 1) % 4][:], uss[:, ic:ic + 1],
                                           sils[j % 2][:], ALU.mult, ALU.mult
                                           ).then_inc(evD.h, 1)
            evD.v = NI * NTB

        @blk.sync
        def _(s):
            for ic in range(NI):
                s.wait_ge(evD.h, NTB * (ic + 1))
                s.dma_start(hscr[ic * 128:(ic + 1) * 128, :], hbufs[ic % 2][:]
                            ).then_inc(hst[ic % 2].h, 16)
                hst[ic % 2].v += 16
            s.wait_ge(hst[0].h, hst[0].v)
            s.wait_ge(hst[1].h, hst[1].v)

    _emit_ffn_b2(nc, resid, dwT, ds, hscr, out, D, I, T, NQ, tag)


def _emit_ffn_b2(nc, resid, dwT, ds, hscr, out, D, I, T, NQ, tag):
    NK = D // 128
    NI = I // 128
    NTB = T // 512
    ND = D // 128
    NDq = ND // NQ
    with (
        nc.sbuf_tensor(f"{tag}dw", [128, NI * NDq * 128], BF16) as dw,
        nc.sbuf_tensor(f"{tag}hs0", [128, NI * 512], BF16) as hs0,
        nc.sbuf_tensor(f"{tag}hs1", [128, NI * 512], BF16) as hs1,
        nc.sbuf_tensor(f"{tag}rs0", [128, NDq * 512], F32) as rs0,
        nc.sbuf_tensor(f"{tag}rs1", [128, NDq * 512], F32) as rs1,
        nc.sbuf_tensor(f"{tag}os0", [128, NDq * 512], F32) as os0,
        nc.sbuf_tensor(f"{tag}os1", [128, NDq * 512], F32) as os1,
        nc.sbuf_tensor(f"{tag}dss", [128, ND], F32) as dss,
        nc.psum_tensor(f"{tag}q0", [128, 512], F32) as q0,
        nc.psum_tensor(f"{tag}q1", [128, 512], F32) as q1,
        nc.psum_tensor(f"{tag}q2", [128, 512], F32) as q2,
        nc.psum_tensor(f"{tag}q3", [128, 512], F32) as q3,
        nc.Block() as blk,
    ):
        psum = [q0, q1, q2, q3]
        hbufs = [hs0, hs1]
        rbufs = [rs0, rs1]
        obufs = [os0, os1]
        wld = Ctr(nc, f"{tag}wld")
        hld = [Ctr(nc, f"{tag}hld0"), Ctr(nc, f"{tag}hld1")]
        rld = [Ctr(nc, f"{tag}rld0"), Ctr(nc, f"{tag}rld1")]
        mm2 = Ctr(nc, f"{tag}mm2")
        ev2 = Ctr(nc, f"{tag}ev2")
        ost = [Ctr(nc, f"{tag}ost0"), Ctr(nc, f"{tag}ost1")]

        @blk.gpsimd
        def _(g):
            g.dma_start(dss[:], bass.AP(ds.tensor, ds.offset, [[ND, 128], [1, ND]])
                        ).then_inc(wld.h, 16)
            wld.v += 16
            nb = 0
            for q in range(NQ):
                src = bass.AP(dwT.tensor, dwT.offset + q * NDq * 128,
                              [[D, 128], [D * 128, NI], [128, NDq], [1, 128]])
                dst = bass.AP(dw, 0, [[NI * NDq * 128, 128], [NDq * 128, NI],
                                      [128, NDq], [1, 128]])
                if q >= 1:
                    g.wait_ge(mm2.h, q * NTB * NDq)
                g.dma_start(dst, src).then_inc(wld.h, 16)
                wld.v += 16
                for tb in range(NTB):
                    if nb >= 2:
                        g.wait_ge(mm2.h, (nb - 1) * NDq)
                        g.wait_ge(ev2.h, (nb - 1) * NDq)
                    g.dma_start(hbufs[nb % 2][:],
                                feat_ap(hscr, T, col0=tb * 512, ncols=512, nch=NI)
                                ).then_inc(hld[nb % 2].h, 16)
                    hld[nb % 2].v += 16
                    g.dma_start(rbufs[nb % 2][:],
                                feat_ap(resid, T, chunk0=q * NDq, nch=NDq,
                                        col0=tb * 512, ncols=512)
                                ).then_inc(rld[nb % 2].h, 16)
                    rld[nb % 2].v += 16
                    nb += 1

        @blk.tensor
        def _(t):
            n = 0
            nb = 0
            for q in range(NQ):
                t.wait_ge(wld.h, 16 * (q + 2))
                for tb in range(NTB):
                    t.wait_ge(hld[nb % 2].h, 16 * (nb // 2 + 1))
                    for dc in range(NDq):
                        bank = psum[n % 4]
                        if n >= 4:
                            t.wait_ge(ev2.h, n - 3)
                        ins = None
                        for icc in range(NI):
                            ins = t.matmul(bank[:],
                                           dw[:, icc * NDq * 128 + dc * 128:
                                              icc * NDq * 128 + dc * 128 + 128],
                                           hbufs[nb % 2][:, icc * 512:(icc + 1) * 512],
                                           start=(icc == 0), stop=(icc == NI - 1))
                        ins.then_inc(mm2.h, 1)
                        n += 1
                    nb += 1
            mm2.v = n

        @blk.vector
        def _(v):
            n = 0
            nb = 0
            for q in range(NQ):
                for tb in range(NTB):
                    v.wait_ge(rld[nb % 2].h, 16 * (nb // 2 + 1))
                    if nb >= 2:
                        v.wait_ge(ost[nb % 2].h, 16 * (nb // 2))
                    for dc in range(NDq):
                        v.wait_ge(mm2.h, n + 1)
                        v.scalar_tensor_tensor(obufs[nb % 2][:, dc * 512:(dc + 1) * 512],
                                               psum[n % 4][:],
                                               dss[:, q * NDq + dc:q * NDq + dc + 1],
                                               rbufs[nb % 2][:, dc * 512:(dc + 1) * 512],
                                               ALU.mult, ALU.add).then_inc(ev2.h, 1)
                        n += 1
                    nb += 1
            ev2.v = n

        @blk.sync
        def _(s):
            nb = 0
            for q in range(NQ):
                for tb in range(NTB):
                    s.wait_ge(ev2.h, (nb + 1) * NDq)
                    dst = feat_ap(out, T, chunk0=q * NDq, nch=NDq, col0=tb * 512, ncols=512)
                    s.dma_start(dst, obufs[nb % 2][:]).then_inc(ost[nb % 2].h, 16)
                    ost[nb % 2].v += 16
                    nb += 1
            s.wait_ge(ost[0].h, ost[0].v)
            s.wait_ge(ost[1].h, ost[1].v)


def kernel(**inputs):
    raise NotImplementedError("assembled later")




def emit_ln(nc, aps, D=2048, T=4096, tag="ln0"):
    NK = D // 128
    NTB = T // 512
    x, xn = aps["x"], aps["xn"]
    with (
        nc.sbuf_tensor(f"{tag}xb", [128, NK * 512], BF16) as xb,
        nc.sbuf_tensor(f"{tag}xs", [128, NK * 512], F32) as xs,
        nc.sbuf_tensor(f"{tag}xnb", [128, NK * 512], BF16) as xnb,
        nc.sbuf_tensor(f"{tag}sq0", [128, 512], BF16) as sq0,
        nc.sbuf_tensor(f"{tag}sq1", [128, 512], BF16) as sq1,
        nc.sbuf_tensor(f"{tag}onesc", [128, 1], BF16) as onesc,
        nc.sbuf_tensor(f"{tag}ones1", [1, 128], F32) as ones1,
        nc.sbuf_tensor(f"{tag}rows", [1, 4 * 512], F32) as rows,
        nc.sbuf_tensor(f"{tag}abc", [128, 512], F32) as abc,
        nc.sbuf_tensor(f"{tag}bbc", [128, 512], F32) as bbc,
        nc.sbuf_tensor(f"{tag}t3", [128, 512], F32) as t3,
        nc.psum_tensor(f"{tag}pa", [128, 512], F32) as pa,
        nc.psum_tensor(f"{tag}pb", [128, 512], F32) as pb,
        nc.psum_tensor(f"{tag}pc", [128, 512], F32) as pc,
        nc.psum_tensor(f"{tag}pd", [128, 512], F32) as pd,
        nc.Block() as blk,
    ):
        sqs = [sq0, sq1]
        R_MU = rows[0:1, 0:512]
        R_MU2 = rows[0:1, 512:1024]
        R_V = rows[0:1, 1024:1536]
        R_A = rows[0:1, 1536:2048]
        xldb = Ctr(nc, f"{tag}xldb")
        xlds = Ctr(nc, f"{tag}xlds")
        evq = Ctr(nc, f"{tag}evq")
        mmq = Ctr(nc, f"{tag}mmq")
        evrow = Ctr(nc, f"{tag}evrow")
        evsq = Ctr(nc, f"{tag}evsq")
        mmb = Ctr(nc, f"{tag}mmb")
        evx = Ctr(nc, f"{tag}evx")
        xst = Ctr(nc, f"{tag}xst")
        dvc = Ctr(nc, f"{tag}dvc")

        @blk.gpsimd
        def _(g):
            for qb in range(NTB):
                if qb >= 1:
                    g.wait_ge(mmq.h, NK * qb)
                    g.wait_ge(evq.h, NK * qb)
                    g.wait_ge(evx.h, qb)
                g.dma_start(xb[:], feat_ap(x, T, nch=NK, col0=qb * 512, ncols=512)
                            ).then_inc(xldb.h, 16)
                g.dma_start(xs[:], feat_ap(x, T, nch=NK, col0=qb * 512, ncols=512)
                            ).then_inc(xlds.h, 16)
                xldb.v += 16
                xlds.v += 16

        @blk.vector
        def _(v):
            v.memset(onesc[:], 1.0)
            v.memset(ones1[:], 1.0).then_inc(evq.h, 1)
            for qb in range(NTB):
                v.wait_ge(xldb.h, 16 * (qb + 1))
                for k in range(NK):
                    j = qb * NK + k
                    if j >= 2:
                        v.wait_ge(mmq.h, j - 1)
                    v.tensor_mul(sqs[j % 2][:], xb[:, k * 512:(k + 1) * 512],
                                 xb[:, k * 512:(k + 1) * 512]).then_inc(evq.h, 1)
                v.wait_ge(mmq.h, NK * (qb + 1))
                v.tensor_scalar_mul(R_MU, pa[0:1, :], 1.0 / D)
                v.tensor_scalar_mul(R_MU2, pb[0:1, :], 1.0 / D).then_inc(evrow.h, 1)
                v.wait_ge(evrow.h, 3 * qb + 1)
                v.scalar_tensor_tensor(R_V, R_MU, 1.0, R_MU, ALU.mult, ALU.mult
                                       ).then_inc(dvc.h, 1)
                dvc.v += 1
                v.wait_ge(dvc.h, dvc.v)
                v.scalar_tensor_tensor(R_V, R_MU2, 1e-5, R_V, ALU.add, ALU.subtract
                                       ).then_inc(evrow.h, 1)
                v.wait_ge(evsq.h, qb + 1)
                v.reciprocal(R_A, R_V).then_inc(dvc.h, 1)
                dvc.v += 1
                v.wait_ge(dvc.h, dvc.v)
                v.scalar_tensor_tensor(R_MU, R_MU, 1.0, R_A, ALU.mult, ALU.mult
                                       ).then_inc(evrow.h, 1)
                v.wait_ge(mmb.h, 2 * (qb + 1))
                v.wait_ge(xlds.h, 16 * (qb + 1))
                if qb >= 1:
                    v.wait_ge(xst.h, 16 * qb)
                v.tensor_copy(abc[:], pc[:])
                v.tensor_copy(bbc[:], pd[:]).then_inc(dvc.h, 1)
                dvc.v += 1
                v.wait_ge(dvc.h, dvc.v)
                for k in range(NK):
                    v.tensor_mul(t3[:], xs[:, k * 512:(k + 1) * 512], abc[:]
                                 ).then_inc(dvc.h, 1)
                    dvc.v += 1
                    v.wait_ge(dvc.h, dvc.v)
                    ins = v.tensor_sub(xnb[:, k * 512:(k + 1) * 512], t3[:], bbc[:])
                    if k == NK - 1:
                        ins.then_inc(evx.h, 1)
                        evx.v += 1
                    else:
                        ins.then_inc(dvc.h, 1)
                        dvc.v += 1
                        v.wait_ge(dvc.h, dvc.v)

        @blk.scalar
        def _(s):
            for qb in range(NTB):
                s.wait_ge(evrow.h, 3 * qb + 2)
                s.sqrt(R_V, R_V).then_inc(evsq.h, 1)

        @blk.tensor
        def _(t):
            t.wait_ge(evq.h, 1)
            for qb in range(NTB):
                if qb >= 1:
                    t.wait_ge(evrow.h, 3 * qb - 2)
                t.wait_ge(xldb.h, 16 * (qb + 1))
                for k in range(NK):
                    j = qb * NK + k
                    t.wait_ge(evq.h, j + 2)
                    t.matmul(pa[0:1, :], onesc[:], xb[:, k * 512:(k + 1) * 512],
                             start=(k == 0), stop=(k == NK - 1))
                    ins = t.matmul(pb[0:1, :], onesc[:], sqs[j % 2][:],
                                   start=(k == 0), stop=(k == NK - 1))
                    ins.then_inc(mmq.h, 1)
                    mmq.v += 1
                t.wait_ge(evrow.h, 3 * (qb + 1))
                if qb >= 1:
                    t.wait_ge(evx.h, qb)
                t.matmul(pc[:], ones1[:, :], R_A, start=True, stop=True).then_inc(mmb.h, 1)
                t.matmul(pd[:], ones1[:, :], R_MU, start=True, stop=True).then_inc(mmb.h, 1)
                mmb.v += 2

        @blk.sync
        def _(s):
            for qb in range(NTB):
                s.wait_ge(evx.h, qb + 1)
                s.dma_start(feat_ap(xn, T, nch=NK, col0=qb * 512, ncols=512), xnb[:]
                            ).then_inc(xst.h, 16)
                xst.v += 16
            s.wait_ge(xst.h, xst.v)

MBIAS = 30.0


def emit_attn0(nc, aps, NH=16, T=4096, tag="a0"):
    D = NH * 128
    NK = D // 128
    NTB = T // 512
    NW = NK + 2
    x, xn, wqT, wkT, wvT = aps["x"], aps["xn"], aps["wqT"], aps["wkT"], aps["wvT"]
    cosq, sinq, cosk, sink = aps["cosq"], aps["sinq"], aps["cosk"], aps["sink"]
    cmask, x1p = aps["cmask"], aps["x1p"]

    tiles = []
    ndiag = 0
    for qb in range(NTB):
        NKT = 4 * (qb + 1)
        for h in range(NH):
            for kt in range(NKT):
                if kt >= NKT - 4:
                    tiles.append((qb, h, kt, NKT, ndiag))
                    ndiag += 1
                else:
                    tiles.append((qb, h, kt, NKT, None))

    from contextlib import ExitStack
    es = ExitStack()
    wq = es.enter_context(nc.sbuf_tensor(f"{tag}wq", [128, NK * NW * 128], BF16))
    xnb = es.enter_context(nc.sbuf_tensor(f"{tag}xnb", [128, NK * 512], BF16))
    xs = es.enter_context(nc.sbuf_tensor(f"{tag}xs", [128, NK * 512], F32))
    qs = es.enter_context(nc.sbuf_tensor(f"{tag}qs", [128, NH * 512], BF16))
    kc = es.enter_context(nc.sbuf_tensor(f"{tag}kc", [128, T], BF16))
    vt = es.enter_context(nc.sbuf_tensor(f"{tag}vt", [128, T], BF16))
    vtmp = es.enter_context(nc.sbuf_tensor(f"{tag}vtmp", [128, 512], BF16))
    rope = es.enter_context(nc.sbuf_tensor(f"{tag}rope", [64, 4 * 512], F32))
    pb0 = es.enter_context(nc.sbuf_tensor(f"{tag}pb0", [128, 512], BF16))
    pb1 = es.enter_context(nc.sbuf_tensor(f"{tag}pb1", [128, 512], BF16))
    t1 = es.enter_context(nc.sbuf_tensor(f"{tag}t1", [64, 512], F32))
    t2 = es.enter_context(nc.sbuf_tensor(f"{tag}t2", [64, 512], F32))
    t3 = es.enter_context(nc.sbuf_tensor(f"{tag}t3", [128, 512], F32))
    rrow = es.enter_context(nc.sbuf_tensor(f"{tag}rrow", [1, 512], F32))
    rbc = es.enter_context(nc.sbuf_tensor(f"{tag}rbc", [128, 512], F32))
    ones1 = es.enter_context(nc.sbuf_tensor(f"{tag}ones1", [1, 128], F32))
    onesc = es.enter_context(nc.sbuf_tensor(f"{tag}onesc", [128, 1], BF16))
    mbc = es.enter_context(nc.sbuf_tensor(f"{tag}mbc", [128, 1], F32))
    cm = es.enter_context(nc.sbuf_tensor(f"{tag}cm", [128, 4 * 512], BF16))
    idn = es.enter_context(nc.sbuf_tensor(f"{tag}idn", [128, 128], BF16))
    ps0 = es.enter_context(nc.psum_tensor(f"{tag}s0", [128, 512], F32))
    ps1 = es.enter_context(nc.psum_tensor(f"{tag}s1", [128, 512], F32))
    ps2 = es.enter_context(nc.psum_tensor(f"{tag}s2", [128, 512], F32))
    ps3 = es.enter_context(nc.psum_tensor(f"{tag}s3", [128, 512], F32))
    psO = es.enter_context(nc.psum_tensor(f"{tag}sO", [128, 512], F32))
    psden = es.enter_context(nc.psum_tensor(f"{tag}sden", [128, 512], F32))
    pst = es.enter_context(nc.psum_tensor(f"{tag}st", [128, 128], BF16))
    with es, nc.Block() as blk:
        psum = [ps0, ps1, ps2, ps3]
        pbufs = [pb0, pb1]
        wld = Ctr(nc, f"{tag}wld")
        xnl = Ctr(nc, f"{tag}xnl")
        xsl = Ctr(nc, f"{tag}xsl")
        rl = Ctr(nc, f"{tag}rl")
        ini = Ctr(nc, f"{tag}ini")
        mmQ = Ctr(nc, f"{tag}mmQ")
        evQ = Ctr(nc, f"{tag}evQ")
        mmT = Ctr(nc, f"{tag}mmT")
        evT = Ctr(nc, f"{tag}evT")
        mmA = Ctr(nc, f"{tag}mmA")
        evE = Ctr(nc, f"{tag}evE")
        evM = Ctr(nc, f"{tag}evM")
        mmO = Ctr(nc, f"{tag}mmO")
        evR = Ctr(nc, f"{tag}evR")
        mmR = Ctr(nc, f"{tag}mmR")
        evN = Ctr(nc, f"{tag}evN")
        st = Ctr(nc, f"{tag}st")
        dvc = Ctr(nc, f"{tag}dvc")

        @blk.gpsimd
        def _(g):
            srcq = bass.AP(wqT.tensor, wqT.offset,
                           [[NH * 128, 128], [NH * 128 * 128, NK], [128, NH], [1, 128]])
            dstq = bass.AP(wq, 0, [[NK * NW * 128, 128], [NW * 128, NK], [128, NK], [1, 128]])
            g.dma_start(dstq, srcq).then_inc(wld.h, 16)
            srck = bass.AP(wkT.tensor, wkT.offset, [[128, 128], [128 * 128, NK], [1, 128]])
            dstk = bass.AP(wq, NK * 128, [[NK * NW * 128, 128], [NW * 128, NK], [1, 128]])
            g.dma_start(dstk, srck).then_inc(wld.h, 16)
            srcv = bass.AP(wvT.tensor, wvT.offset, [[128, 128], [128 * 128, NK], [1, 128]])
            dstv = bass.AP(wq, (NK + 1) * 128, [[NK * NW * 128, 128], [NW * 128, NK], [1, 128]])
            g.dma_start(dstv, srcv).then_inc(wld.h, 16)
            g.dma_start(cm[:], bass.AP(cmask.tensor, cmask.offset,
                                       [[4 * 512, 128], [1, 4 * 512]])).then_inc(wld.h, 16)
            g.dma_start(idn[:], bass.AP(aps["ident"].tensor, aps["ident"].offset,
                                        [[128, 128], [1, 128]])).then_inc(wld.h, 16)
            wld.v = 80
            for qb in range(NTB):
                if qb >= 1:
                    g.wait_ge(st.h, 16 * qb)
                    g.wait_ge(mmQ.h, NW * qb)
                    g.wait_ge(evQ.h, NW * qb)
                g.dma_start(xnb[:], feat_ap(xn, T, nch=NK, col0=qb * 512, ncols=512)
                            ).then_inc(xnl.h, 16)
                xnl.v += 16
                g.dma_start(xs[:], feat_ap(x, T, nch=NK, col0=qb * 512, ncols=512)
                            ).then_inc(xsl.h, 16)
                xsl.v += 16
                for i, tb in enumerate((cosq, sinq, cosk, sink)):
                    g.dma_start(rope[:, i * 512:(i + 1) * 512],
                                bass.AP(tb.tensor, tb.offset + qb * 512, [[T, 64], [1, 512]])
                                ).then_inc(rl.h, 16)
                rl.v += 64

        @blk.tensor
        def _(t):
            t.wait_ge(wld.h, wld.v)
            t.wait_ge(ini.h, 2)
            atc = 0
            for qb in range(NTB):
                NKT = 4 * (qb + 1)
                hbase = qb * NH
                t.wait_ge(xnl.h, 16 * (qb + 1))
                if qb >= 1:
                    t.wait_ge(evN.h, qb * NH)
                for j in range(NW):
                    n = qb * NW + j
                    if n >= 4:
                        t.wait_ge(evQ.h, n - 3)
                    bank = psum[n % 4]
                    ins = None
                    for k in range(NK):
                        ins = t.matmul(bank[:],
                                       wq[:, k * NW * 128 + j * 128: k * NW * 128 + j * 128 + 128],
                                       xnb[:, k * 512:(k + 1) * 512],
                                       start=(k == 0), stop=(k == NK - 1))
                    ins.then_inc(mmQ.h, 1)
                    mmQ.v += 1
                t.wait_ge(evQ.h, (qb + 1) * NW)
                for i in range(4):
                    if qb > 0 or i > 0:
                        t.wait_ge(evT.h, qb * 4 + i)
                    t.transpose(pst[:], vtmp[:, i * 128:(i + 1) * 128], idn[:]
                                ).then_inc(mmT.h, 1)
                    mmT.v += 1
                t.wait_ge(evT.h, (qb + 1) * 4)
                for h in range(NH):
                    hh = hbase + h
                    t.wait_ge(evR.h, hh)
                    t.wait_ge(evN.h, hh)
                    for kt in range(NKT):
                        a = atc + kt
                        if a >= 2:
                            t.wait_ge(evE.h, a - 1)
                        t.matmul(psum[a % 2][:], kc[:, kt * 128:(kt + 1) * 128],
                                 qs[:, h * 512:(h + 1) * 512], start=True, stop=True
                                 ).then_inc(mmA.h, 1)
                        mmA.v += 1
                        if kt >= 1:
                            b = a - 1
                            bkt = kt - 1
                            t.wait_ge(evE.h, b + 1)
                            if tiles[b][4] is not None:
                                t.wait_ge(evM.h, tiles[b][4] + 1)
                            t.matmul(psden[0:1, :], onesc[:], pbufs[b % 2][:],
                                     start=(bkt == 0), stop=False)
                            t.matmul(psO[:], vt[:, bkt * 128:(bkt + 1) * 128],
                                     pbufs[b % 2][:],
                                     start=(bkt == 0), stop=(bkt == NKT - 1)
                                     ).then_inc(mmO.h, 1)
                            mmO.v += 1
                    b = atc + NKT - 1
                    bkt = NKT - 1
                    t.wait_ge(evE.h, b + 1)
                    if tiles[b][4] is not None:
                        t.wait_ge(evM.h, tiles[b][4] + 1)
                    t.matmul(psden[0:1, :], onesc[:], pbufs[b % 2][:],
                             start=(bkt == 0), stop=True)
                    ins = t.matmul(psO[:], vt[:, bkt * 128:(bkt + 1) * 128],
                                   pbufs[b % 2][:], start=(bkt == 0), stop=True)
                    ins.then_inc(mmO.h, 1)
                    mmO.v += 1
                    atc += NKT
                    t.wait_ge(evR.h, hh + 1)
                    t.matmul(ps2[:], ones1[:, :], rrow[0:1, :], start=True, stop=True
                             ).then_inc(mmR.h, 1)
                    mmR.v += 1

        @blk.scalar
        def _(s):
            for a, (qb, h, kt, NKT, di) in enumerate(tiles):
                s.wait_ge(mmA.h, a + 1)
                if a >= 2:
                    s.wait_ge(mmO.h, a - 1)
                s.activation(pbufs[a % 2][:], psum[a % 2][:], AF.Exp,
                             bias=mbc[:, 0:1]).then_inc(evE.h, 1)
            evE.v = len(tiles)

        @blk.vector
        def _(v):
            v.memset(ones1[:], 1.0)
            v.memset(mbc[:], -MBIAS)
            v.memset(onesc[:], 1.0).then_inc(ini.h, 2)
            ini.v = 2
            atc = 0
            for qb in range(NTB):
                NKT = 4 * (qb + 1)
                hbase = qb * NH
                v.wait_ge(rl.h, 64 * (qb + 1))
                for j in range(NW):
                    n = qb * NW + j
                    v.wait_ge(mmQ.h, n + 1)
                    bank = psum[n % 4]
                    if j <= NK:
                        co = 0 if j < NK else 2 * 512
                        so = 512 if j < NK else 3 * 512
                        if j < NK:
                            dstE = qs[0:64, j * 512:(j + 1) * 512]
                            dstO = qs[64:128, j * 512:(j + 1) * 512]
                        else:
                            dstE = kc[0:64, qb * 512:(qb + 1) * 512]
                            dstO = kc[64:128, qb * 512:(qb + 1) * 512]
                        E = bank[0:64, :]
                        O = bank[64:128, :]
                        v.wait_ge(evQ.h, evQ.v)
                        v.tensor_mul(t1[:], E, rope[:, co:co + 512])
                        v.tensor_mul(t2[:], O, rope[:, so:so + 512]).then_inc(dvc.h, 1)
                        dvc.v += 1
                        v.wait_ge(dvc.h, dvc.v)
                        v.tensor_sub(dstE, t1[:], t2[:]).then_inc(dvc.h, 1)
                        dvc.v += 1
                        v.wait_ge(dvc.h, dvc.v)
                        v.tensor_mul(t1[:], O, rope[:, co:co + 512])
                        v.tensor_mul(t2[:], E, rope[:, so:so + 512]).then_inc(dvc.h, 1)
                        dvc.v += 1
                        v.wait_ge(dvc.h, dvc.v)
                        ins = v.tensor_add(dstO, t1[:], t2[:])
                    else:
                        ins = v.tensor_copy(vtmp[:], bank[:])
                    ins.then_inc(evQ.h, 1)
                    evQ.v += 1
                for i in range(4):
                    v.wait_ge(mmT.h, qb * 4 + i + 1)
                    v.tensor_copy(vt[:, (qb * 4 + i) * 128:(qb * 4 + i) * 128 + 128],
                                  pst[:]).then_inc(evT.h, 1)
                    evT.v += 1
                v.wait_ge(xsl.h, 16 * (qb + 1))
                for h in range(NH):
                    hh = hbase + h
                    for kt in range(NKT - 4, NKT):
                        a = atc + kt
                        di = tiles[a][4]
                        o = kt - (NKT - 4)
                        v.wait_ge(evE.h, a + 1)
                        v.tensor_mul(pbufs[a % 2][:], pbufs[a % 2][:],
                                     cm[:, o * 512:(o + 1) * 512]).then_inc(evM.h, 1)
                        evM.v += 1
                    atc += NKT
                    v.wait_ge(mmO.h, atc)
                    v.reciprocal(rrow[0:1, :], psden[0:1, :]).then_inc(evR.h, 1)
                    evR.v += 1
                    v.wait_ge(mmR.h, hh + 1)
                    v.wait_ge(evN.h, hh)
                    v.tensor_copy(rbc[:], ps2[:]).then_inc(dvc.h, 1)
                    dvc.v += 1
                    v.wait_ge(dvc.h, dvc.v)
                    v.tensor_mul(t3[:], psO[:], rbc[:]).then_inc(dvc.h, 1)
                    dvc.v += 1
                    v.wait_ge(dvc.h, dvc.v)
                    v.tensor_add(xs[:, h * 512:(h + 1) * 512],
                                 xs[:, h * 512:(h + 1) * 512], t3[:]).then_inc(evN.h, 1)
                    evN.v += 1

        @blk.sync
        def _(s):
            for qb in range(NTB):
                s.wait_ge(evN.h, (qb + 1) * NH)
                s.dma_start(feat_ap(x1p, T, nch=NK, col0=qb * 512, ncols=512), xs[:]
                            ).then_inc(st.h, 16)
                st.v += 16
            s.wait_ge(st.h, st.v)

class Sched:
    ENGINES = ("gpsimd", "tensor", "scalar", "vector", "sync")

    def __init__(self, nc, tag):
        self.nc = nc
        self.tag = tag
        self.ops = {e: [] for e in self.ENGINES}
        self._ctrs = []

    def ctr(self, name):
        c = Ctr(self.nc, f"{self.tag}{name}")
        self._ctrs.append(c)
        return c

    def add(self, eng, fn, waits=(), inc=None, inc_n=1):
        post = None
        if inc is not None:
            inc.v += inc_n
            post = inc.v
        self.ops[eng].append((tuple(waits), fn, inc, inc_n))
        return post

    def add_raw(self, eng, fn, waits=(), inc=None, inc_n=1):
        self.ops[eng].append((tuple(waits), fn, inc, inc_n))

    def emit(self, blk):
        for eng in self.ENGINES:
            if not self.ops[eng]:
                continue
            def body(e, _ops=self.ops[eng]):
                for waits, fn, inc, inc_n in _ops:
                    for c, val in waits:
                        if val > 0:
                            e.wait_ge(c.h, val)
                    ins = fn(e)
                    if inc is not None:
                        ins.then_inc(inc.h, inc_n)
            getattr(blk, eng)(body)


def emit_l1proj(nc, aps, NH=16, T=4096, tag="c1"):
    D = NH * 128
    NK = D // 128
    NTB = T // 512
    R = 4
    xn = aps["xn1"]
    qo, ko, vo, bsg = aps["q"], aps["k"], aps["v"], aps["bsg"]
    f1s, g1s = aps["f1s"], aps["g1s"]
    qw, kw, vw, bw, f1w, g1w = (aps[s] for s in ("qw", "kw", "vw", "bw", "f1w", "g1w"))
    cvw = aps["cvw"]

    chunks = []
    cv = 0
    for c in range(NH):
        chunks.append((qw, D, c, qo, c, "qn", cv)); cv += 1
    for c in range(NH * R):
        chunks.append((kw, NH * R * 128, c, ko, c, "kn", cv)); cv += 1
    for c in range(NH * R):
        chunks.append((vw, NH * R * 128, c, vo, c, "v", cv)); cv += 1
    chunks.append((bw, 64, 0, bsg, 0, "b", None))
    chunks.append((f1w, 128, 0, f1s, 0, "plain", None))
    chunks.append((g1w, 128, 0, g1s, 0, "plain", None))
    NCV = NH + 2 * NH * R

    from contextlib import ExitStack
    es = ExitStack()
    xns = es.enter_context(nc.sbuf_tensor(f"{tag}xns", [128, NK * T], BF16))
    wb = [es.enter_context(nc.sbuf_tensor(f"{tag}wb{j}", [128, D], BF16)) for j in range(2)]
    cb = es.enter_context(nc.sbuf_tensor(f"{tag}cb", [128, T + 3], BF16))
    t4 = es.enter_context(nc.sbuf_tensor(f"{tag}t4", [128, T], F32))
    t5 = es.enter_context(nc.sbuf_tensor(f"{tag}t5", [128, T], BF16))
    ysil = es.enter_context(nc.sbuf_tensor(f"{tag}ysil", [128, T], BF16))
    ob = [es.enter_context(nc.sbuf_tensor(f"{tag}ob{j}", [128, T], BF16)) for j in range(2)]
    nln = es.enter_context(nc.sbuf_tensor(f"{tag}nln", [1, 512], F32))
    nrm = es.enter_context(nc.sbuf_tensor(f"{tag}nrm", [1, 2 * 512], BF16))
    cvs = es.enter_context(nc.sbuf_tensor(f"{tag}cvs", [128, NCV * 4], F32))
    onesc = es.enter_context(nc.sbuf_tensor(f"{tag}onesc", [128, 1], BF16))
    ones1 = es.enter_context(nc.sbuf_tensor(f"{tag}ones1", [1, 128], BF16))
    cbank = es.enter_context(nc.sbuf_tensor(f"{tag}cbank", [128, 4], F32))
    psum = [es.enter_context(nc.psum_tensor(f"{tag}p{j}", [128, 512], F32)) for j in range(4)]
    pn = es.enter_context(nc.psum_tensor(f"{tag}pn", [128, 512], F32))
    pg = [es.enter_context(nc.psum_tensor(f"{tag}pg{j}", [128, 512], F32)) for j in range(2)]
    with es, nc.Block() as blk:
        S = Sched(nc, tag)
        xld, cld = S.ctr("xld"), S.ctr("cld")
        wl = [S.ctr("wl0"), S.ctr("wl1")]
        mm, evc, gcv = S.ctr("mm"), S.ctr("evc"), S.ctr("gcv")
        dvc, acc = S.ctr("dvc"), S.ctr("acc")
        mmn, mnb = S.ctr("mmn"), S.ctr("mnb")
        ost = [S.ctr("ost0"), S.ctr("ost1")]

        S.add("gpsimd", lambda g: g.dma_start(xns[:], feat_ap(xn, T, nch=NK)),
              inc=xld, inc_n=16)
        S.add("gpsimd", lambda g: g.dma_start(
            cvs[:], bass.AP(cvw.tensor, cvw.offset, [[NCV * 4, 128], [1, NCV * 4]])),
            inc=cld, inc_n=16)
        S.add("vector", lambda v: v.memset(onesc[:], 1.0))
        S.add("vector", lambda v: v.memset(cbank[:, 0:1], 1e-6))
        S.add("vector", lambda v: v.memset(cbank[:, 1:2], -0.5 * float(np.log(128.0))))
        S.add("vector", lambda v: v.memset(cbank[:, 2:3], 0.0))
        S.add("vector", lambda v: v.memset(ones1[:], 1.0), inc=dvc)

        st_count = [0, 0]
        cb_free = None
        t4_free = None
        ysil_free = None
        t5_free = None
        nrm_free = {}
        exp_acc = {}
        pg_free = [None, None]
        deferred_pe = []

        for i, (wap, wcols, wc, oap, oc, kind, cvc) in enumerate(chunks):
            M = 64 if kind == "b" else 128
            par = i % 2
            ncols = 64 if kind == "b" else 128
            wwaits = [(mm, (i - 1) * NTB)] if i >= 2 else []

            def wload(g, wap=wap, wcols=wcols, wc=wc, M=M, par=par):
                dst = bass.AP(wb[par], 0, [[D, 128], [M, NK], [1, M]])
                return g.dma_start(dst,
                                   bass.AP(wap.tensor, wap.offset + wc * 128,
                                           [[wcols, 128], [wcols * 128, NK], [1, M]]))
            S.add("gpsimd", wload, waits=wwaits, inc=wl[par], inc_n=16)

            for tb in range(NTB):
                n = i * NTB + tb
                waits = [(xld, 16), (dvc, 1)] if n == 0 else []
                waits.append((wl[par], 16 * (i // 2 + 1)))
                if n >= 4:
                    waits.append((evc, n - 3))

                def pmm(t, tb=tb, n=n, M=M, par=par):
                    ins = None
                    for kk in range(NK):
                        ins = t.matmul(psum[n % 4][0:M, :],
                                       wb[par][:, kk * M:kk * M + M],
                                       xns[:, kk * T + tb * 512: kk * T + tb * 512 + 512],
                                       start=(kk == 0), stop=(kk == NK - 1))
                    return ins
                S.add("tensor", pmm, waits=waits, inc=mm)
                if tb == 0:
                    for w_, f_, c_ in deferred_pe:
                        S.add_raw("tensor", f_, waits=w_, inc=c_)
                    deferred_pe = []

            guards = [(mm, i * NTB + 1), (dvc, dvc.v)]
            if cb_free is not None:
                guards.append(cb_free)
            S.add("vector", lambda v, M=M: v.memset(cb[0:M, 0:3], 0.0),
                  waits=guards, inc=dvc)
            for tb in range(NTB):
                n = i * NTB + tb
                S.add("vector",
                      lambda v, M=M, tb=tb, n=n: v.tensor_copy(
                          cb[0:M, 3 + tb * 512:3 + (tb + 1) * 512], psum[n % 4][0:M, :]),
                      waits=[(mm, n + 1), (dvc, dvc.v)], inc=evc)

            if kind in ("qn", "kn", "v"):
                vwaits = [(evc, (i + 1) * NTB), (cld, 16), (dvc, dvc.v)]
                if t4_free is not None:
                    vwaits.append((acc, t4_free))
                S.add("vector", lambda v, cvc=cvc: v.tensor_scalar(
                    t4[:], cb[:, 3:T + 3], cvs[:, 4 * cvc + 3:4 * cvc + 4], None, ALU.mult),
                    waits=vwaits, inc=dvc)
                for tap in (2, 1, 0):
                    S.add("vector", lambda v, cvc=cvc, tap=tap: v.scalar_tensor_tensor(
                        t4[:], cb[:, tap:T + tap], cvs[:, 4 * cvc + tap:4 * cvc + tap + 1],
                        t4[:], ALU.mult, ALU.add), waits=[(dvc, dvc.v)], inc=dvc)
                conv_done = dvc.v
                cb_free = (dvc, conv_done)

                if kind == "v":
                    swaits = [(dvc, conv_done)]
                    if st_count[par]:
                        swaits.append((ost[par], 16 * st_count[par]))
                    S.add("scalar", lambda s, par=par: s.activation(
                        ob[par][:], t4[:], AF.Silu),
                        waits=swaits + [(acc, acc.v)], inc=acc)
                    t4_free = acc.v
                    out_ready = (acc, acc.v)
                else:
                    swaits = [(dvc, conv_done)]
                    if ysil_free is not None:
                        swaits.append((dvc, ysil_free))
                    S.add("scalar", lambda s: s.activation(ysil[:], t4[:], AF.Silu),
                          waits=swaits + [(acc, acc.v)], inc=acc)
                    t4_free = acc.v
                    sq_waits = []
                    if t5_free is not None:
                        sq_waits.append((mmn, t5_free))
                    S.add("scalar", lambda s: s.activation(t5[:], ysil[:], AF.Square),
                          waits=sq_waits + [(acc, acc.v)], inc=acc)
                    sq_done = acc.v
                    ln_prev = 0
                    for tb in range(NTB):
                        w_ = [(acc, sq_done)]
                        if ln_prev:
                            w_.append((acc, ln_prev))
                        def smm(t, tb=tb):
                            return t.matmul(pn[0:1, :], onesc[:],
                                            t5[:, tb * 512:(tb + 1) * 512],
                                            start=True, stop=True)
                        deferred_pe.append((w_, smm, mmn))
                        mmn.v += 1
                        mv = mmn.v
                        def lnop(s, tb=tb):
                            return s.activation(nln[0:1, :],
                                                pn[0:1, :], AF.Ln, bias=cbank[0:1, 0:1])
                        S.add("scalar", lnop, waits=[(mmn, mv), (acc, acc.v)], inc=acc)
                        bcol = 1 if kind == "qn" else 2
                        ew = [(acc, acc.v)]
                        if nrm_free.get(tb % 2):
                            ew.append((mnb, nrm_free[tb % 2]))
                        S.add("scalar", lambda s, tb=tb, bcol=bcol: s.activation(
                            nrm[0:1, (tb % 2) * 512:(tb % 2) * 512 + 512],
                            nln[0:1, :], AF.Exp, scale=-0.5,
                            bias=cbank[0:1, bcol:bcol + 1]), waits=ew, inc=acc)
                        exp_acc[tb] = acc.v
                        ln_prev = acc.v
                    t5_free = mmn.v
                    for tb in range(NTB):
                        w_ = [(acc, exp_acc[tb])]
                        if pg_free[tb % 2] is not None:
                            w_.append((dvc, pg_free[tb % 2]))
                        def bmm(t, tb=tb):
                            return t.matmul(pg[tb % 2][:, :], ones1[0:1, :],
                                            nrm[0:1, (tb % 2) * 512:(tb % 2) * 512 + 512],
                                            start=True, stop=True)
                        deferred_pe.append((w_, bmm, mnb))
                        mnb.v += 1
                        nrm_free[tb % 2] = mnb.v
                        bv = mnb.v
                        stw = [(mnb, bv), (dvc, dvc.v)]
                        if st_count[par]:
                            stw.append((ost[par], 16 * st_count[par]))
                        S.add("vector", lambda v, tb=tb, par=par: v.tensor_mul(
                            ob[par][:, tb * 512:(tb + 1) * 512],
                            ysil[:, tb * 512:(tb + 1) * 512], pg[tb % 2][:, :]),
                            waits=stw, inc=dvc)
                        pg_free[tb % 2] = dvc.v
                    ysil_free = dvc.v
                    out_ready = (dvc, dvc.v)
            elif kind == "b":
                swaits = [(evc, (i + 1) * NTB)]
                if st_count[par]:
                    swaits.append((ost[par], 16 * st_count[par]))
                S.add("scalar", lambda s, par=par: s.activation(
                    ob[par][0:64, :], cb[0:64, 0:T], AF.Sigmoid),
                    waits=swaits + [(acc, acc.v)], inc=acc)
                cb_free = (acc, acc.v)
                out_ready = (acc, acc.v)
            else:
                pwaits = [(evc, (i + 1) * NTB), (dvc, dvc.v)]
                if st_count[par]:
                    pwaits.append((ost[par], 16 * st_count[par]))
                S.add("vector", lambda v, par=par: v.tensor_copy(
                    ob[par][:, 0:T], cb[:, 0:T]), waits=pwaits, inc=dvc)
                cb_free = (dvc, dvc.v)
                out_ready = (dvc, dvc.v)

            rows = 64 if kind == "b" else 128

            def store(sy, oap=oap, oc=oc, par=par, rows=rows):
                dst = bass.AP(oap.tensor, oap.offset + oc * 128 * T, [[T, rows], [1, T]])
                return sy.dma_start(dst, ob[par][0:rows, 0:T])
            S.add("sync", store, waits=[out_ready], inc=ost[par], inc_n=16)
            st_count[par] += 1

        for w_, f_, c_ in deferred_pe:
            S.add_raw("tensor", f_, waits=w_, inc=c_)
        S.add("sync", lambda sy: sy.nop(),
              waits=[(ost[0], 16 * st_count[0]), (ost[1], 16 * st_count[1])])
        S.emit(blk)

def emit_scan(nc, aps, NH=16, T=4096, NHORN=7, tag="s2", max_stage=99, ncc_cap=None,
              sub_stage=99):
    H, R, D = NH, 4, 128
    NT = T * R
    C = 128
    CT = C // R
    NCC = NT // C
    SCG = 4
    TSC = CT * SCG
    qai, kai, vai, gdi, bsgi = (aps[s] for s in ("q", "k", "v", "gd", "bsg"))
    gsig, ldf, mlog, osc = aps["gsig"], aps["ldf"], aps["mlog"], aps["osc"]
    wmxs = aps["wmxs"]
    msku = aps["masku"]
    pall = aps["pall"]
    i4t = aps["i4t"]
    onw = aps["onw"]

    from contextlib import ExitStack
    es = ExitStack()
    sb = lambda n, sh, dt: es.enter_context(nc.sbuf_tensor(f"{tag}{n}", sh, dt))
    ps = lambda n, sh, dt: es.enter_context(nc.psum_tensor(f"{tag}{n}", sh, dt))
    kslab = sb("kslab", [128, 64 * TSC], BF16)
    vslab = sb("vslab", [128, 64 * TSC], BF16)
    qslab = sb("qslab", [128, 16 * TSC], BF16)
    gdslab = sb("gdslab", [128, 16 * (TSC // 2)], F32)
    Ssb = sb("S", [128, 2048], F32)
    Sbf = sb("Sbf", [128, 2048], BF16)
    Gf = sb("Gf", [128, 2048], F32)
    Ge = sb("Ge", [128, 16 * 129], F32)
    dA = sb("dA", [128, 2048], F32)
    kscn = sb("kscn", [128, 2048], BF16)
    vscn = sb("vscn", [128, 2048], BF16)
    EL16 = sb("EL16", [128, 2048], BF16)
    ELI = sb("ELI", [128, 2048], BF16)
    f_kl16 = sb("fkl16", [128, 2048], BF16)
    f_kl32 = sb("fkl32", [128, 2048], BF16)
    f_kl64 = sb("fkl64", [128, 2048], BF16)
    f_ks = sb("fks", [128, 2048], BF16)
    f_kli = sb("fkli", [128, 2048], BF16)
    f_kr16 = sb("fkr16", [128, 2048], BF16)
    f_kr32 = sb("fkr32", [128, 2048], BF16)
    f_kr64 = sb("fkr64", [128, 2048], BF16)
    f_kg = sb("fkg", [128, 2048], BF16)
    f_ql16 = sb("fql16", [128, 2048], BF16)
    f_ql32 = sb("fql32", [128, 2048], BF16)
    f_ql64 = sb("fql64", [128, 2048], BF16)
    f_qs = sb("fqs", [128, 2048], BF16)
    pc16 = sb("pc16", [128, 128], F32)
    pcr = sb("pcr", [128, 7 * 128], F32)
    TTm = sb("TT", [128, 2048], BF16)
    BTm = sb("BT", [128, 2048], BF16)
    Vtok = sb("Vtok", [128, 2048], BF16)
    KGt = sb("KGt", [128, 2048], BF16)
    Rsb = sb("R", [128, 2048], BF16)
    Xsb = sb("X", [128, 2048], BF16)
    Osb = sb("Osb", [128, 2048], BF16)
    o1 = sb("o1", [32, 2048], BF16)
    rr16 = sb("rr16", [32, 16], F32)
    ofeat = sb("ofeat", [128, 512], BF16)
    onesf = sb("onesf", [128, 1], F32)
    cbank = sb("cbank", [128, 4], F32)
    gsl = sb("gsl", [128, 512], BF16)
    bscn = sb("bscn", [16, 128], BF16)
    bsrt = sb("bsrt", [16, 128], F32)
    msksu = sb("msksu", [128, 5 * 2048], mybir.dt.uint8)
    onehs = sb("onehs", [16, 2048], BF16)
    palls = sb("palls", [128, 32], BF16)
    i4ts = sb("i4ts", [4, 128], BF16)
    mixm = sb("mixm", [128, 16 * 32], BF16)
    wcol = sb("wcol", [128, 16], F32)
    mls = sb("mls", [16, 4], F32)
    wmx = sb("wmx", [16, 4], F32)
    wmxT = sb("wmxT", [4, 16], BF16)
    ldfs = sb("ldfs", [128, 16], F32)
    onwc = sb("onwc", [128, 1], F32)
    ident = sb("ident", [128, 128], BF16)
    ECt = sb("ECt", [128, 16], F32)
    trow = sb("trow", [16, 8], F32)
    psGA = ps("psGA", [128, 2048], F32)
    psT = ps("psT", [128, 2048], BF16)
    psM = ps("psM", [32, 1024], F32)

    with es, nc.Block() as blk:
        S = Sched(nc, tag)
        dvc = S.ctr("dvc")
        gpc = S.ctr("gpc")
        acc = S.ctr("acc")
        pec = S.ctr("pec")
        dld = S.ctr("dld")
        cl0 = S.ctr("cl0")
        bld = S.ctr("bld")
        gld = S.ctr("gld")
        wst = S.ctr("wst")
        wst2 = S.ctr("wst2")
        ost = S.ctr("ost")

        def dve(fn, waits=(), tail=None):
            w = list(waits) + [(dvc, dvc.v)]
            S.add("vector", fn, waits=w, inc=tail or dvc)

        def gpo(fn, waits=(), tail=None):
            w = list(waits) + [(gpc, gpc.v)]
            S.add("gpsimd", fn, waits=w, inc=tail or gpc)

        def act(fn, waits=(), tail=None):
            w = list(waits) + [(acc, acc.v)]
            S.add("scalar", fn, waits=w, inc=tail or acc)

        S.add("sync", lambda s: s.dma_start(msksu[:], bass.AP(
            msku.tensor, msku.offset, [[2048, 128], [2048 * 128, 5], [1, 2048]])),
            inc=cl0, inc_n=16)
        S.add("sync", lambda s: s.dma_start(onehs[:], bass.AP(
            aps["oneh16"].tensor, aps["oneh16"].offset, [[2048, 16], [1, 2048]])),
            inc=cl0, inc_n=16)
        S.add("sync", lambda s: s.dma_start(palls[:], bass.AP(
            pall.tensor, pall.offset, [[32, 128], [1, 32]])), inc=cl0, inc_n=16)
        S.add("sync", lambda s: s.dma_start(i4ts[:], bass.AP(
            i4t.tensor, i4t.offset, [[128, 4], [1, 128]])), inc=cl0, inc_n=16)
        S.add("sync", lambda s: s.dma_start(ldfs[:], bass.AP(
            ldf.tensor, ldf.offset, [[16, 128], [1, 16]])), inc=cl0, inc_n=16)
        S.add("sync", lambda s: s.dma_start(mls[:], bass.AP(
            mlog.tensor, mlog.offset, [[4, 16], [1, 4]])), inc=cl0, inc_n=16)
        S.add("sync", lambda s: s.dma_start(onwc[:], bass.AP(
            onw.tensor, onw.offset, [[1, 128], [1, 1]])), inc=cl0, inc_n=16)
        S.add("sync", lambda s: s.dma_start(ident[:], bass.AP(
            aps["ident"].tensor, aps["ident"].offset, [[128, 128], [1, 128]])),
            inc=cl0, inc_n=16)
        CL = cl0.v

        dve(lambda v: v.memset(Ssb[:], 0.0), waits=[])
        dve(lambda v: v.memset(Sbf[:], 0.0))
        dve(lambda v: v.memset(Ge[:], 0.0))
        dve(lambda v: v.tensor_copy(
            bass.AP(Gf, 1, [[2048, 128], [128, 16], [4, 32], [1, 3]]),
            bass.AP(ldfs, 0, [[16, 128], [1, 16], [0, 32], [0, 3]])),
            waits=[(cl0, CL)])
        dve(lambda v: v.tensor_reduce(trow[:, 0:1], mls[:], AX.X, ALU.max,
                                      negate=True), waits=[(cl0, CL)])
        act(lambda s: s.activation(wmx[:], mls[:], AF.Exp, bias=trow[:, 0:1]),
            waits=[(dvc, dvc.v)])
        dve(lambda v: v.tensor_reduce(trow[:, 1:2], wmx[:], AX.X, ALU.add),
            waits=[(acc, acc.v)])
        dve(lambda v: v.reciprocal(trow[:, 2:3], trow[:, 1:2]))
        dve(lambda v: v.tensor_scalar(wmx[:], wmx[:], trow[:, 2:3], None, ALU.mult))
        S.add("sync", lambda s: s.dma_start(
            bass.AP(wmxs.tensor, wmxs.offset, [[4, 16], [1, 4]]), wmx[:]),
            waits=[(dvc, dvc.v)], inc=wst, inc_n=16)
        S.add("gpsimd", lambda g: g.dma_start(
            wmxT[:], bass.AP(wmxs.tensor, wmxs.offset, [[1, 4], [4, 16]]),
            allow_slow_non_contiguous=True),
            waits=[(wst, 16)], inc=wst2, inc_n=16)
        S.add("tensor", lambda t: t.matmul(psGA[:, 0:16], i4ts[:], wmxT[:],
                                           start=True, stop=True),
              waits=[(wst2, 16), (cl0, CL)], inc=pec)
        dve(lambda v: v.tensor_copy(wcol[:], psGA[:, 0:16]), waits=[(pec, pec.v)])
        dve(lambda v: v.tensor_mul(
            mixm[:], bass.AP(palls, 0, [[32, 128], [0, 16], [1, 32]]),
            bass.AP(wcol, 0, [[16, 128], [1, 16], [0, 32]])))
        dve(lambda v: v.memset(onesf[:], 1.0))
        dve(lambda v: v.memset(cbank[:, 0:1], 1e-5))
        mix_ready = dvc.v

        prev_s_done = 0
        prev_gf = 0
        NCCr = min(NCC, ncc_cap) if ncc_cap else NCC
        for cc in range(NCCr):
            sg, tc = cc // SCG, cc % SCG
            if tc == 0:
                w = [(dvc, prev_s_done)] if cc > 0 else [(dvc, mix_ready)]
                for part in range(4):
                    S.add("sync", lambda s, sg=sg, part=part: s.dma_start(
                        bass.AP(kslab, part * 16 * TSC,
                                [[64 * TSC, 128], [TSC, 16], [1, TSC]]),
                        bass.AP(kai.tensor, kai.offset + sg * TSC + part * 16 * 128 * T,
                                [[T, 128], [128 * T, 16], [1, TSC]])),
                        waits=(w if part == 0 else ()), inc=dld, inc_n=16)
                    S.add("sync", lambda s, sg=sg, part=part: s.dma_start(
                        bass.AP(vslab, part * 16 * TSC,
                                [[64 * TSC, 128], [TSC, 16], [1, TSC]]),
                        bass.AP(vai.tensor, vai.offset + sg * TSC + part * 16 * 128 * T,
                                [[T, 128], [128 * T, 16], [1, TSC]])),
                        inc=dld, inc_n=16)
                for part in range(2):
                    S.add("sync", lambda s, sg=sg, part=part: s.dma_start(
                        bass.AP(qslab, part * 8 * TSC,
                                [[16 * TSC, 128], [TSC, 8], [1, TSC]]),
                        bass.AP(qai.tensor, qai.offset + sg * TSC + part * 8 * 128 * T,
                                [[T, 128], [128 * T, 8], [1, TSC]])),
                        inc=dld, inc_n=16)

            if cc % 2 == 0:
                wgd = [(dvc, prev_gf)] if cc > 0 else []
                for part in range(2):
                    S.add("sync", lambda s, cc=cc, part=part: s.dma_start(
                        bass.AP(gdslab, part * 8 * (TSC // 2),
                                [[16 * (TSC // 2), 128], [TSC // 2, 8], [1, TSC // 2]]),
                        bass.AP(gdi.tensor, gdi.offset + cc * CT + part * 8 * 128 * T,
                                [[T, 128], [128 * T, 8], [1, TSC // 2]])),
                        waits=(wgd if part == 0 else ()), inc=dld, inc_n=16)
            SL = dld.v
            S.add("gpsimd", lambda g, cc=cc: g.dma_start(bsrt[:], bass.AP(
                bsgi.tensor, bsgi.offset + cc * CT,
                [[4 * T, 16], [T, 4], [1, CT]])),
                waits=[(dvc, dvc.v)], inc=bld, inc_n=16)
            BL = bld.v
            for part in range(4):
                S.add("gpsimd", lambda g, cc=cc, part=part: g.dma_start(
                    bass.AP(gsl, part * 4 * CT, [[16 * CT, 128], [CT, 4], [1, CT]]),
                    bass.AP(gsig.tensor, gsig.offset + cc * CT + part * 4 * 128 * T,
                            [[T, 128], [128 * T, 4], [1, CT]])),
                    waits=([(dvc, dvc.v)] if part == 0 else ()), inc=gld, inc_n=16)
            GL = gld.v

            kview = bass.AP(kslab, tc * CT,
                            [[64 * TSC, 128], [4 * TSC, 16], [1, CT], [TSC, 4]])
            vview = bass.AP(vslab, tc * CT,
                            [[64 * TSC, 128], [4 * TSC, 16], [1, CT], [TSC, 4]])
            qview = bass.AP(qslab, tc * CT,
                            [[16 * TSC, 128], [TSC, 16], [1, CT], [0, 4]])
            gdview = bass.AP(gdslab, (cc % 2) * CT,
                             [[16 * (TSC // 2), 128], [TSC // 2, 16], [1, CT]])

            dve(lambda v, kview=kview: v.tensor_copy(kscn[:], kview),
                waits=[(dld, SL)])
            dve(lambda v, vview=vview: v.tensor_copy(vscn[:], vview),
                waits=[(dld, SL)])
            dve(lambda v, gdview=gdview: v.tensor_copy(
                bass.AP(Gf, 0, [[2048, 128], [128, 16], [4, 32]]), gdview),
                waits=[(dld, SL)])
            dve(lambda v: v.tensor_copy(
                bscn[:], bass.AP(bsrt, 0, [[128, 16], [1, 32], [32, 4]])),
                waits=[(bld, BL)])
            gf_done = dvc.v
            prev_gf = gf_done

            if max_stage < 2:
                prev_s_done = dvc.v
                continue
            ones_col = bass.AP(onesf, 0, [[1, 128], [0, 128]])
            for h in range(NH):
                fn = (lambda e, h=h: e.tensor_tensor_scan(
                    Ge[:, h * 129 + 1:h * 129 + 129],
                    Gf[:, h * 128:(h + 1) * 128], ones_col,
                    0.0, ALU.add, ALU.mult))
                dve(fn, waits=[(dvc, gf_done)])
            scans_d, scans_g = dvc.v, gpc.v

            GV = bass.AP(Ge, 1, [[2064, 128], [129, 16], [1, 128]])
            PV16 = bass.AP(Ge, 0, [[2064, 128], [129, 16], [16, 8], [0, 16]])
            gE16 = bass.AP(Ge, 16, [[2064, 128], [129, 16], [16, 8]])
            gP16 = bass.AP(Ge, 0, [[2064, 128], [129, 16], [16, 8]])
            gE32 = bass.AP(Ge, 32, [[2064, 128], [129, 16], [32, 4], [0, 2]])
            gP32 = bass.AP(Ge, 0, [[2064, 128], [129, 16], [32, 4], [0, 2]])
            gE64 = bass.AP(Ge, 64, [[2064, 128], [129, 16], [64, 2], [0, 4]])
            gP64 = bass.AP(Ge, 0, [[2064, 128], [129, 16], [64, 2], [0, 4]])
            gEC = bass.AP(Ge, 128, [[2064, 128], [129, 16], [0, 8]])

            if max_stage < 3:
                prev_s_done = dvc.v
                continue
            dve(lambda v: v.tensor_sub(dA[:], GV, PV16), waits=[(dvc, scans_d)])
            dve(lambda v: v.tensor_scalar_max(dA[:], dA[:], -80.0))
            d16_done = dvc.v
            act(lambda s: s.activation(EL16[:], dA[:], AF.Exp), waits=[(dvc, d16_done)])
            act(lambda s: s.activation(ELI[:], dA[:], AF.Exp, scale=-1.0))
            e16_done = acc.v
            prs = [(gE16, gP16), (gE32, gP16), (gE64, gP16),
                   (gP16, gP32), (gP16, gP64), (gEC, gP16), (gP16, None)]
            for ii, (a_, b_) in enumerate(prs):
                if b_ is None:
                    dve(lambda v, ii=ii, a_=a_: v.tensor_copy(
                        pcr[:, ii * 128:ii * 128 + 128], a_))
                else:
                    dve(lambda v, ii=ii, a_=a_, b_=b_: v.tensor_sub(
                        pcr[:, ii * 128:ii * 128 + 128], a_, b_))
            pr_done = dvc.v
            act(lambda s: s.activation(pcr[:], pcr[:], AF.Exp),
                waits=[(dvc, pr_done)])
            pr_exp = acc.v

            def RVi(ii):
                return bass.AP(pcr, ii * 128, [[7 * 128, 128], [8, 16], [1, 8], [0, 16]])

            if max_stage < 4:
                prev_s_done = dvc.v
                continue
            dve(lambda v: v.tensor_mul(f_kl16[:], kscn[:], EL16[:]),
                waits=[(acc, e16_done)])
            gpo(lambda g, qview=qview: g.tensor_mul(f_ql16[:], qview, EL16[:]),
                waits=[(acc, e16_done), (dld, SL)])
            for h in range(NH):
                S.add("tensor", lambda t, h=h: t.matmul(
                    psGA[:, h * 128:(h + 1) * 128], onehs[:, h * 128:(h + 1) * 128],
                    bscn[:], start=True, stop=True),
                    waits=([(dvc, gf_done), (cl0, CL), (pec, pec.v)] if h == 0 else []),
                    inc=(pec if h == NH - 1 else None))
            bb_mm = pec.v
            dve(lambda v: v.scalar_tensor_tensor(f_kli[:], psGA[:], 1.0, kscn[:],
                                                 ALU.mult, ALU.mult),
                waits=[(pec, bb_mm)])
            dve(lambda v: v.tensor_mul(f_kli[:], f_kli[:], ELI[:]))
            kli_done = dvc.v
            dve(lambda v: v.tensor_mul(f_kr16[:], f_kli[:], RVi(0)),
                waits=[(acc, pr_exp)])
            dve(lambda v: v.tensor_mul(f_kr32[:], f_kli[:], RVi(1)))
            dve(lambda v: v.tensor_mul(f_kr64[:], f_kli[:], RVi(2)))
            dve(lambda v: v.tensor_mul(f_kg[:], f_kli[:], RVi(5)))
            gpo(lambda g: g.tensor_mul(f_kl32[:], f_kl16[:], RVi(3)),
                waits=[(acc, pr_exp), (dvc, dvc.v)])
            gpo(lambda g: g.tensor_mul(f_kl64[:], f_kl16[:], RVi(4)))
            gpo(lambda g: g.tensor_mul(f_ql32[:], f_ql16[:], RVi(3)))
            gpo(lambda g: g.tensor_mul(f_ql64[:], f_ql16[:], RVi(4)))
            gpo(lambda g: g.tensor_mul(f_ks[:], f_kl16[:], RVi(6)))
            gpo(lambda g: g.tensor_mul(f_qs[:], f_ql16[:], RVi(6)))
            fields_d, fields_g = dvc.v, gpc.v

            if max_stage < 5:
                prev_s_done = dvc.v
                continue
            pieces = [(f_kli, f_kl16, 0, TTm), (f_kr16, f_kl16, 2, TTm),
                      (f_kr32, f_kl32, 3, TTm), (f_kr64, f_kl64, 4, TTm),
                      (f_kli, f_ql16, 1, BTm), (f_kr16, f_ql16, 2, BTm),
                      (f_kr32, f_ql32, 3, BTm), (f_kr64, f_ql64, 4, BTm)]
            pv_done = []
            for pi, (lf, rf, mi, dst) in enumerate(pieces):
                for h in range(NH):
                    w = []
                    if h == 0:
                        w = [(dvc, fields_d), (gpc, fields_g)]
                        if pi >= 1:
                            w.append((dvc, pv_done[pi - 1]))
                        elif cc > 0:
                            w.append((pec, pec.v))
                    S.add("tensor", lambda t, h=h, lf=lf, rf=rf: t.matmul(
                        psGA[:, h * 128:(h + 1) * 128],
                        lf[:, h * 128:(h + 1) * 128], rf[:, h * 128:(h + 1) * 128],
                        start=True, stop=True), waits=w,
                        inc=(pec if h == NH - 1 else None))
                pmm = pec.v
                mview = msksu[:, mi * 2048:(mi + 1) * 2048]
                if pi == 0 or pi == 4:
                    dve(lambda v, dst=dst: v.memset(dst[:], 0.0), waits=[(pec, pmm)])
                dve(lambda v, dst=dst, mview=mview: v.copy_predicated(
                    dst[:], mview, psGA[:]), waits=[(pec, pmm)])
                pv_done.append(dvc.v)
            tb_done = dvc.v

            if max_stage < 6:
                prev_s_done = dvc.v
                continue
            for h in range(NH):
                w = [(pec, pec.v), (dvc, tb_done)] if h == 0 else []
                S.add("tensor", lambda t, h=h: t.transpose(
                    psT[:, h * 128:(h + 1) * 128],
                    vscn[:, h * 128:(h + 1) * 128],
                    ident[:]), waits=w, inc=(pec if h == NH - 1 else None))
            vt_mm = pec.v
            dve(lambda v: v.tensor_copy(Vtok[:], psT[:]), waits=[(pec, vt_mm)])
            vt_done = dvc.v
            for h in range(NH):
                w = [(dvc, vt_done)] if h == 0 else []
                S.add("tensor", lambda t, h=h: t.transpose(
                    psT[:, h * 128:(h + 1) * 128], f_kg[:, h * 128:(h + 1) * 128],
                    ident[:]), waits=w, inc=(pec if h == NH - 1 else None))
            kg_mm = pec.v
            dve(lambda v: v.tensor_copy(KGt[:], psT[:]), waits=[(pec, kg_mm)])
            kg_done = dvc.v

            if max_stage < 7:
                prev_s_done = dvc.v
                continue
            for h in range(NH):
                w = [(dvc, kg_done)] if h == 0 else []
                S.add("tensor", lambda t, h=h: t.matmul(
                    psGA[:, h * 128:(h + 1) * 128],
                    f_ks[:, h * 128:(h + 1) * 128], Sbf[:, h * 128:(h + 1) * 128],
                    start=True, stop=True),
                    waits=w, inc=(pec if h == NH - 1 else None))
            c_mm = pec.v
            dve(lambda v: v.scalar_tensor_tensor(
                Rsb[:], psGA[:], -1.0, Vtok[:], ALU.mult, ALU.add), waits=[(pec, c_mm)])
            dve(lambda v: v.tensor_copy(Xsb[:], Rsb[:]))
            r_done = dvc.v

            if max_stage < 8:
                prev_s_done = dvc.v
                continue
            xprev = r_done
            for it in range(NHORN):
                for h in range(NH):
                    w = [(dvc, xprev)] if h == 0 else []
                    S.add("tensor", lambda t, h=h: t.matmul(
                        psGA[:, h * 128:(h + 1) * 128],
                        TTm[:, h * 128:(h + 1) * 128],
                        Xsb[:, h * 128:(h + 1) * 128], start=True, stop=True),
                        waits=w, inc=(pec if h == NH - 1 else None))
                hm = pec.v
                dve(lambda v: v.scalar_tensor_tensor(
                    Xsb[:], psGA[:], -1.0, Rsb[:], ALU.mult, ALU.add),
                    waits=[(pec, hm)])
                xprev = dvc.v

            if max_stage < 9:
                prev_s_done = dvc.v
                continue
            for h in range(NH):
                w = [(dvc, xprev)] if h == 0 else []
                S.add("tensor", lambda t, h=h: (
                    t.matmul(psGA[:, h * 128:(h + 1) * 128],
                             f_qs[:, h * 128:(h + 1) * 128],
                             Sbf[:, h * 128:(h + 1) * 128],
                             start=True, stop=False),
                    t.matmul(psGA[:, h * 128:(h + 1) * 128],
                             BTm[:, h * 128:(h + 1) * 128],
                             Xsb[:, h * 128:(h + 1) * 128],
                             start=False, stop=True))[-1],
                    waits=w, inc=(pec if h == NH - 1 else None))
            o_mm = pec.v
            dve(lambda v: v.tensor_copy(Osb[:], psGA[:]),
                waits=[(pec, o_mm)])
            o_done = dvc.v

            if max_stage < 10:
                prev_s_done = dvc.v
                continue
            for h in range(NH):
                w = [(dvc, o_done)] if h == 0 else []
                S.add("tensor", lambda t, h=h: t.matmul(
                    psGA[:, h * 128:(h + 1) * 128],
                    KGt[:, h * 128:(h + 1) * 128],
                    Xsb[:, h * 128:(h + 1) * 128], start=True, stop=True),
                    waits=w, inc=(pec if h == NH - 1 else None))
            su_mm = pec.v
            ECexp = bass.AP(Ge, 128, [[2064, 128], [129, 16], [0, 128]])
            gpo(lambda g: g.tensor_copy(pc16[:, 0:16],
                                        bass.AP(Ge, 128, [[2064, 128], [129, 16]])),
                waits=[(dvc, o_done)])
            S.add("scalar", lambda s: s.activation(ECt[:], pc16[:, 0:16], AF.Exp),
                  waits=[(gpc, gpc.v), (acc, acc.v)], inc=acc)
            ec_done = acc.v
            gpo(lambda g: g.tensor_mul(
                Ssb[:], Ssb[:], bass.AP(ECt, 0, [[16, 128], [1, 16], [0, 128]])),
                waits=[(acc, ec_done)])
            sdec = gpc.v
            dve(lambda v: v.tensor_add(Ssb[:], Ssb[:], psGA[:]),
                waits=[(gpc, sdec), (pec, su_mm)])
            dve(lambda v: v.tensor_copy(Sbf[:], Ssb[:]))
            s_done = dvc.v

            if max_stage < 11:
                prev_s_done = dvc.v
                continue
            for h in range(NH):
                w = [(dvc, o_done)] if h == 0 else []
                if h == 8 and sub_stage >= 1:
                    w = [(acc, sq_a), (dvc, o1a)]
                S.add("tensor", lambda t, h=h: t.matmul(
                    psM[:, (h % 8) * 128:(h % 8) * 128 + 128],
                    mixm[:, h * 32:(h + 1) * 32],
                    Osb[:, h * 128:(h + 1) * 128], start=True, stop=True),
                    waits=w, inc=(pec if h % 8 == 7 else None))
                if h == 7 and sub_stage >= 1:
                    m1 = pec.v
                    dve(lambda v: v.tensor_copy(o1[:, 0:1024], psM[:]),
                        waits=[(pec, m1)])
                    o1a = dvc.v
                    act(lambda s: s.activation(dA[0:32, 0:1024], o1[:, 0:1024],
                                               AF.Square), waits=[(dvc, o1a)])
                    sq_a = acc.v
                if h == NH - 1 and sub_stage < 1:
                    prev_s_done = dvc.v
            if sub_stage < 1:
                continue
            m2 = pec.v
            dve(lambda v: v.tensor_copy(o1[:, 1024:2048], psM[:]),
                waits=[(pec, m2)])
            o1_done = dvc.v
            act(lambda s: s.activation(dA[0:32, 1024:2048], o1[:, 1024:2048],
                                       AF.Square), waits=[(dvc, o1_done)])
            sq_b = acc.v
            if sub_stage < 2:
                prev_s_done = dvc.v
                continue
            dve(lambda v: v.tensor_reduce(
                rr16[:, 0:16], bass.AP(dA, 0, [[2048, 32], [128, 16], [1, 128]]),
                AX.X, ALU.add), waits=[(acc, sq_b)])
            act(lambda s: s.activation(rr16[:, 0:16], rr16[:, 0:16], AF.Sqrt,
                                       scale=1.0 / 128, bias=cbank[0:32, 0:1]),
                waits=[(dvc, dvc.v)])
            dve(lambda v: v.reciprocal(rr16[:, 0:16], rr16[:, 0:16]),
                waits=[(acc, acc.v)])
            dve(lambda v: v.tensor_mul(
                o1[:], o1[:], bass.AP(rr16, 0, [[16, 32], [1, 16], [0, 128]])))
            on_done = dvc.v
            if sub_stage < 3:
                prev_s_done = dvc.v
                continue
            for h in range(NH):
                w = [(dvc, on_done)] if h == 0 else []
                S.add("tensor", lambda t, h=h: t.transpose(
                    psT[:, h * 32:(h + 1) * 32], o1[0:32, h * 128:(h + 1) * 128],
                    ident[0:32, 0:32]), waits=w, inc=(pec if h == NH - 1 else None))
            ot_mm = pec.v
            dve(lambda v: v.scalar_tensor_tensor(
                ofeat[:], psT[:, 0:512], onwc[:, 0:1], gsl[:], ALU.mult, ALU.mult),
                waits=[(pec, ot_mm), (gld, GL), (ost, ost.v)])
            of_done = dvc.v
            if sub_stage < 4:
                prev_s_done = dvc.v
                continue
            S.add("sync", lambda s, cc=cc: s.dma_start(
                bass.AP(osc.tensor, osc.offset + cc * CT,
                        [[T, 128], [128 * T, 16], [1, CT]]), ofeat[:]),
                waits=[(dvc, of_done)], inc=ost, inc_n=16)
            prev_s_done = s_done

        S.add("sync", lambda s: s.nop(), waits=[(ost, ost.v)])
        S.emit(blk)

def emit_l1bg(nc, aps, NH=16, T=4096, tag="cb"):
    D = NH * 128
    NTB = T // 512
    f1s, g1s, f2wT, g2wT = (aps[s] for s in ("f1s", "g1s", "f2wT", "g2wT"))
    dtb, alog, g2b, gd, gsig, ldf = (aps[s] for s in
                                     ("dtb", "alog", "g2b", "gd", "gsig", "ldf"))
    from contextlib import ExitStack
    es = ExitStack()
    sb = lambda n, sh, dt: es.enter_context(nc.sbuf_tensor(f"{tag}{n}", sh, dt))
    ps = lambda n, sh, dt: es.enter_context(nc.psum_tensor(f"{tag}{n}", sh, dt))
    f1o = sb("f1o", [128, T], BF16)
    g1o = sb("g1o", [128, T], BF16)
    f2s = sb("f2s", [128, D], BF16)
    g2s = sb("g2s", [128, D], BF16)
    dtbs = sb("dtbs", [128, NH], F32)
    g2bs = sb("g2bs", [128, NH], F32)
    abc = sb("abc", [128, NH], F32)
    arow = sb("arow", [1, NH], F32)
    cb = sb("cbk", [128, 2], F32)
    hb = sb("hb", [128, 512], F32)
    eb = sb("eb", [128, 512], F32)
    rb = sb("rb", [128, 512], F32)
    gdb = sb("gdb", [128, T], F32)
    gsb = sb("gsb", [128, T], BF16)
    lf1 = sb("lf1", [128, NH], F32)
    lf2 = sb("lf2", [128, NH], F32)
    ones1 = sb("ones1", [1, 128], F32)
    psum = [ps(f"p{j}", [128, 512], F32) for j in range(4)]
    with es, nc.Block() as blk:
        S = Sched(nc, tag)
        ld = S.ctr("ld")
        mm = S.ctr("mm")
        dvc = S.ctr("dvc")
        acc = S.ctr("acc")
        pec = S.ctr("pec")
        ost = S.ctr("ost")
        ostg = S.ctr("ostg")
        osts = S.ctr("osts")

        S.add("gpsimd", lambda g: g.dma_start(f1o[:], bass.AP(
            f1s.tensor, f1s.offset, [[T, 128], [1, T]])), inc=ld, inc_n=16)
        S.add("gpsimd", lambda g: g.dma_start(g1o[:], bass.AP(
            g1s.tensor, g1s.offset, [[T, 128], [1, T]])), inc=ld, inc_n=16)
        S.add("gpsimd", lambda g: g.dma_start(f2s[:], bass.AP(
            f2wT.tensor, f2wT.offset, [[D, 128], [1, D]])), inc=ld, inc_n=16)
        S.add("gpsimd", lambda g: g.dma_start(g2s[:], bass.AP(
            g2wT.tensor, g2wT.offset, [[D, 128], [1, D]])), inc=ld, inc_n=16)
        S.add("gpsimd", lambda g: g.dma_start(dtbs[:], bass.AP(
            dtb.tensor, dtb.offset, [[NH, 128], [1, NH]])), inc=ld, inc_n=16)
        S.add("gpsimd", lambda g: g.dma_start(g2bs[:], bass.AP(
            g2b.tensor, g2b.offset, [[NH, 128], [1, NH]])), inc=ld, inc_n=16)
        S.add("gpsimd", lambda g: g.dma_start(arow[:], bass.AP(
            alog.tensor, alog.offset, [[NH, 1], [1, NH]])), inc=ld, inc_n=16)
        LD = ld.v
        S.add("vector", lambda v: v.memset(ones1[:], 1.0), inc=dvc)
        S.add("vector", lambda v: v.memset(cb[:, 0:1], 1.0), waits=[(dvc, 1)], inc=dvc)
        S.add("vector", lambda v: v.memset(cb[:, 1:2], -10000.0),
              waits=[(dvc, 2)], inc=dvc)
        S.add("scalar", lambda s: s.activation(arow[:], arow[:], AF.Exp),
              waits=[(ld, LD)], inc=acc)
        S.add("tensor", lambda t: t.matmul(psum[0][:, 0:NH], ones1[:], arow[:],
                                           start=True, stop=True),
              waits=[(acc, 1), (dvc, 3)], inc=pec)
        S.add("vector", lambda v: v.tensor_copy(abc[:], psum[0][:, 0:NH]),
              waits=[(pec, 1), (dvc, 3)], inc=dvc)
        ab_done = dvc.v

        relu_acc = {}
        sig_acc = {}

        def dve(fn, waits=()):
            S.add("vector", fn, waits=list(waits) + [(dvc, dvc.v)], inc=dvc)

        def act(fn, waits=()):
            S.add("scalar", fn, waits=list(waits) + [(acc, acc.v)], inc=acc)

        act(lambda s: s.activation(lf1[:], dtbs[:], AF.Abs, bias=cb[:, 1:2]),
            waits=[(ld, LD), (dvc, ab_done)])
        act(lambda s: s.activation(lf1[:], lf1[:], AF.Exp, scale=-1.0))
        act(lambda s: s.activation(lf1[:], lf1[:], AF.Ln, bias=cb[:, 0:1]))
        act(lambda s: s.activation(lf2[:], dtbs[:], AF.Relu, bias=cb[:, 1:2]))
        lfa = acc.v
        dve(lambda v: v.tensor_add(lf1[:], lf1[:], lf2[:]), waits=[(acc, lfa)])
        dve(lambda v: v.scalar_tensor_tensor(lf1[:], lf1[:], -1.0, abc[:],
                                             ALU.mult, ALU.mult))
        lf_done = dvc.v
        S.add("sync", lambda s: s.dma_start(bass.AP(
            ldf.tensor, ldf.offset, [[NH, 128], [1, NH]]), lf1[:]),
            waits=[(dvc, lf_done)], inc=ost, inc_n=16)

        for c in range(NH):
            for tb in range(NTB):
                n = c * NTB + tb
                w = [(ld, LD), (dvc, ab_done)] if n == 0 else []
                if n >= 2:
                    w.append((acc, relu_acc[n - 2]))
                S.add("tensor", lambda t, c=c, tb=tb, n=n: t.matmul(
                    psum[n % 2][:], f2s[:, c * 128:(c + 1) * 128],
                    f1o[:, tb * 512:(tb + 1) * 512], start=True, stop=True),
                    waits=w, inc=pec)
                pm = pec.v
                act(lambda s, c=c, tb=tb, n=n: s.activation(
                    hb[:], psum[n % 2][:], AF.Abs, bias=dtbs[:, c:c + 1]),
                    waits=[(pec, pm)] + ([(dvc, dvc.v)] if True else []))
                act(lambda s: s.activation(eb[:], hb[:], AF.Exp, scale=-1.0))
                act(lambda s: s.activation(eb[:], eb[:], AF.Ln, bias=cb[:, 0:1]))
                act(lambda s, c=c, n=n: s.activation(
                    rb[:], psum[n % 2][:], AF.Relu, bias=dtbs[:, c:c + 1]))
                relu_acc[n] = acc.v
                aa = acc.v
                dve(lambda v: v.tensor_add(rb[:], rb[:], eb[:]), waits=[(acc, aa)])
                dve(lambda v, c=c, tb=tb: v.scalar_tensor_tensor(
                    gdb[:, tb * 512:(tb + 1) * 512], rb[:], -1.0,
                    bass.AP(abc, c, [[NH, 128], [0, 512]]), ALU.mult, ALU.mult))
            gdc = dvc.v
            S.add("sync", lambda s, c=c: s.dma_start(
                bass.AP(gd.tensor, gd.offset + c * 128 * T, [[T, 128], [1, T]]),
                gdb[:]), waits=[(dvc, gdc)], inc=ostg, inc_n=16)
            gd_st = ostg.v
            for tb in range(NTB):
                n = c * NTB + tb
                S.add("tensor", lambda t, c=c, tb=tb, n=n: t.matmul(
                    psum[2 + n % 2][:], g2s[:, c * 128:(c + 1) * 128],
                    g1o[:, tb * 512:(tb + 1) * 512], start=True, stop=True),
                    waits=([(acc, sig_acc[n - 2])] if n >= 2 else []), inc=pec)
                pm = pec.v
                act(lambda s, c=c, tb=tb, n=n: s.activation(
                    gsb[:, tb * 512:(tb + 1) * 512], psum[2 + n % 2][:],
                    AF.Sigmoid, bias=g2bs[:, c:c + 1]),
                    waits=[(pec, pm)] + ([(osts, gs_st)] if c > 0 and tb == 0 else []))
                sig_acc[n] = acc.v
            gsc = acc.v
            S.add("sync", lambda s, c=c: s.dma_start(
                bass.AP(gsig.tensor, gsig.offset + c * 128 * T, [[T, 128], [1, T]]),
                gsb[:]), waits=[(acc, gsc)], inc=osts, inc_n=16)
            gs_st = osts.v
            dve(lambda v: v.engine_nop(), waits=[(ostg, gd_st)])
        S.add("sync", lambda s: s.nop(),
              waits=[(ost, ost.v), (ostg, ostg.v), (osts, osts.v)])
        S.emit(blk)

def host_prep(ins, T=4096, NH=16):
    D = NH * 128
    p = {}
    ct = lambda a: np.ascontiguousarray(a)
    perm = np.concatenate([np.arange(0, 128, 2), np.arange(1, 128, 2)])
    Wq, Wk, Wv = ins["L0_attn_Wq"], ins["L0_attn_Wk"], ins["L0_attn_Wv"]
    p["wqT"] = ct(np.concatenate([Wq[:, h * 128 + perm] for h in range(NH)], axis=1))
    p["wkT"] = ct(Wk[:, perm])
    p["wvT"] = ct(Wv)
    inv = 1.0 / (10000.0 ** (np.arange(0, 128, 2, dtype=np.float32) / 128))
    f = np.outer(np.arange(T, dtype=np.float32), inv)
    p["cosk"] = ct(np.cos(f).T.astype(np.float32))
    p["sink"] = ct(np.sin(f).T.astype(np.float32))
    sc = np.float32(1.0 / np.sqrt(128.0))
    p["cosq"] = ct(p["cosk"] * sc)
    p["sinq"] = ct(p["sink"] * sc)
    cm = np.zeros((128, 4 * 512), dtype=np.float32)
    for o in range(4):
        for kk in range(128):
            cm[kk, o * 512 + o * 128 + kk:(o + 1) * 512] = 1.0
    p["cmask"] = cm.astype(BFNP)
    p["ident"] = np.eye(128, dtype=np.float32).astype(BFNP)
    for L in (0, 1):
        p[f"gwT{L}"] = ct(ins[f"L{L}_ffn_gate_w"].T)
        p[f"uwT{L}"] = ct(ins[f"L{L}_ffn_up_w"].T)
        p[f"dwT{L}"] = ct(ins[f"L{L}_ffn_down_w"].T)
        p[f"gs{L}"] = ct(ins[f"L{L}_ffn_gate_s"].reshape(-1, 128).T)
        p[f"us{L}"] = ct(ins[f"L{L}_ffn_up_s"].reshape(-1, 128).T)
        p[f"ds{L}"] = ct(ins[f"L{L}_ffn_down_s"].reshape(-1, 128).T)
    p["qw"] = ct(ins["L1_attn_q_proj_w"].T)
    p["kw"] = ct(ins["L1_attn_k_proj_w"].T)
    p["vw"] = ct(ins["L1_attn_v_proj_w"].T)
    p["bw"] = ct(ins["L1_attn_b_proj_w"].T)
    p["f1w"] = ct(ins["L1_attn_f1_w"].T)
    p["g1w"] = ct(ins["L1_attn_g1_w"].T)
    cv = np.concatenate([ins["L1_attn_q_conv_w"], ins["L1_attn_k_conv_w"],
                         ins["L1_attn_v_conv_w"]], axis=0)
    ncv = cv.shape[0] // 128
    p["cvw"] = ct(cv.reshape(ncv, 128, 4).transpose(1, 0, 2).reshape(128, ncv * 4))
    p["f2wT"] = ct(ins["L1_attn_f2_w"].T)
    p["g2wT"] = ct(ins["L1_attn_g2_w"].T)
    p["dtb"] = ct(ins["L1_attn_dt_bias"].reshape(NH, 128).T)
    p["g2b"] = ct(ins["L1_attn_g2_b"].reshape(NH, 128).T)
    p["alog"] = ct(ins["L1_attn_A_log"].reshape(1, NH))
    p["mlog"] = ct(ins["L1_attn_micro_logits"])
    p["onw"] = ct(ins["L1_attn_o_norm_w"].reshape(128, 1))
    p["opT"] = ct(ins["L1_attn_o_proj_w"].T)
    p["dsone"] = np.ones((128, NH), dtype=np.float32)
    mk = np.zeros((5, 128, 128), np.float32)
    mi = np.arange(128)[:, None]
    mj = np.arange(128)[None, :]
    mk[0] = (mi < mj) & (mi // 16 == mj // 16)
    mk[1] = (mi <= mj) & (mi // 16 == mj // 16)
    mk[2] = (mi // 16 + 1 == mj // 16) & (mi // 32 == mj // 32)
    mk[3] = (mi // 32 + 1 == mj // 32) & (mi // 64 == mj // 64)
    mk[4] = (mi // 64 == 0) & (mj // 64 == 1)
    p["masku"] = np.tile(mk, (1, 1, NH)).reshape(5 * 128, NH * 128).astype(np.uint8)
    oneh = np.zeros((NH, NH * 128), np.float32)
    for h in range(NH):
        oneh[h, h * 128:(h + 1) * 128] = 1.0
    p["oneh16"] = oneh.astype(BFNP)
    pall = np.zeros((128, 32), np.float32)
    pall[np.arange(128), np.arange(128) // 4] = 1.0
    p["pall"] = pall.astype(BFNP)
    i4 = np.zeros((4, 128), np.float32)
    i4[np.arange(128) % 4, np.arange(128)] = 1.0
    p["i4t"] = i4.astype(BFNP)
    return p


def _d(nc, name, arr_shape, dt, kind):
    return nc.dram_tensor(name, list(arr_shape), dt, kind=kind).ap()


def _recycled(f, nc, *args, **kw):
    mark = len(CTR_LOG)
    f(nc, *args, **kw)
    nc.clear_and_free_semaphores([c.h for c in CTR_LOG[mark:]])


def build_A(T, NH=16):
    D = NH * 128
    nc = bass.Bass("TRN2", target_bir_lowering=False)
    aps = {
        "x": _d(nc, "x", (D, T), F32, "ExternalInput"),
        "xn": _d(nc, "xn0", (D, T), BF16, "ExternalOutput"),
        "wqT": _d(nc, "wqT", (D, NH * 128), F32, "ExternalInput"),
        "wkT": _d(nc, "wkT", (D, 128), F32, "ExternalInput"),
        "wvT": _d(nc, "wvT", (D, 128), F32, "ExternalInput"),
        "cosq": _d(nc, "cosq", (64, T), F32, "ExternalInput"),
        "sinq": _d(nc, "sinq", (64, T), F32, "ExternalInput"),
        "cosk": _d(nc, "cosk", (64, T), F32, "ExternalInput"),
        "sink": _d(nc, "sink", (64, T), F32, "ExternalInput"),
        "cmask": _d(nc, "cmask", (128, 2048), BF16, "ExternalInput"),
        "ident": _d(nc, "ident", (128, 128), BF16, "ExternalInput"),
        "x1p": _d(nc, "x1p", (D, T), F32, "ExternalOutput"),
    }
    emit_ln(nc, {"x": aps["x"], "xn": aps["xn"]}, D=D, T=T, tag="ln0")
    _recycled(emit_attn0, nc, aps, NH=NH, T=T, tag="a0")
    return nc


def build_ffn(T, L, D=2048, I=5632):
    nc = bass.Bass("TRN2", target_bir_lowering=False)
    aps = {
        "xn": _d(nc, "xn", (D, T), BF16, "ExternalInput"),
        "resid": _d(nc, "resid", (D, T), F32, "ExternalInput"),
        "gwT": _d(nc, "gwT", (D, I), F32, "ExternalInput"),
        "uwT": _d(nc, "uwT", (D, I), F32, "ExternalInput"),
        "dwT": _d(nc, "dwT", (I, D), F32, "ExternalInput"),
        "gs": _d(nc, "gs", (128, I // 128), F32, "ExternalInput"),
        "us": _d(nc, "us", (128, I // 128), F32, "ExternalInput"),
        "ds": _d(nc, "ds", (128, D // 128), F32, "ExternalInput"),
        "hscr": _d(nc, "hscr", (I, T), BF16, "Internal"),
        "out": _d(nc, "out", (D, T), F32, "ExternalOutput"),
    }
    _recycled(emit_ffn, nc, aps, D=D, I=I, T=T, NQ=4, tag="f0")
    return nc


def build_C1(T, NH=16):
    D = NH * 128
    nc = bass.Bass("TRN2", target_bir_lowering=False)
    aps = {
        "x": _d(nc, "x1", (D, T), F32, "ExternalInput"),
        "xn": _d(nc, "xn1", (D, T), BF16, "ExternalOutput"),
        "qw": _d(nc, "qw", (D, D), F32, "ExternalInput"),
        "kw": _d(nc, "kw", (D, 4 * D), F32, "ExternalInput"),
        "vw": _d(nc, "vw", (D, 4 * D), F32, "ExternalInput"),
        "bw": _d(nc, "bw", (D, 64), F32, "ExternalInput"),
        "f1w": _d(nc, "f1w", (D, 128), F32, "ExternalInput"),
        "g1w": _d(nc, "g1w", (D, 128), F32, "ExternalInput"),
        "cvw": _d(nc, "cvw", (128, (NH + 8 * NH) * 4), F32, "ExternalInput"),
        "f2wT": _d(nc, "f2wT", (128, D), F32, "ExternalInput"),
        "g2wT": _d(nc, "g2wT", (128, D), F32, "ExternalInput"),
        "dtb": _d(nc, "dtb", (128, NH), F32, "ExternalInput"),
        "g2b": _d(nc, "g2b", (128, NH), F32, "ExternalInput"),
        "alog": _d(nc, "alog", (1, NH), F32, "ExternalInput"),
        "q": _d(nc, "q", (D, T), BF16, "ExternalOutput"),
        "k": _d(nc, "k", (4 * D, T), BF16, "ExternalOutput"),
        "v": _d(nc, "v", (4 * D, T), BF16, "ExternalOutput"),
        "bsg": _d(nc, "bsg", (64, T), BF16, "ExternalOutput"),
        "f1s": _d(nc, "f1s", (128, T), BF16, "Internal"),
        "g1s": _d(nc, "g1s", (128, T), BF16, "Internal"),
        "gd": _d(nc, "gd", (D, T), F32, "ExternalOutput"),
        "gsig": _d(nc, "gsig", (D, T), BF16, "ExternalOutput"),
        "ldf": _d(nc, "ldf", (128, NH), F32, "ExternalOutput"),
    }
    emit_ln(nc, {"x": aps["x"], "xn": aps["xn"]}, D=D, T=T, tag="ln1")
    aps["xn1"] = aps["xn"]
    _recycled(emit_l1proj, nc, aps, NH=NH, T=T, tag="c1")
    _recycled(emit_l1bg, nc, aps, NH=NH, T=T, tag="cb")
    return nc


def build_C2(T, NH=16):
    D = NH * 128
    nc = bass.Bass("TRN2", target_bir_lowering=False)
    aps = {
        "q": _d(nc, "q", (D, T), BF16, "ExternalInput"),
        "k": _d(nc, "k", (4 * D, T), BF16, "ExternalInput"),
        "v": _d(nc, "v", (4 * D, T), BF16, "ExternalInput"),
        "gd": _d(nc, "gd", (D, T), F32, "ExternalInput"),
        "bsg": _d(nc, "bsg", (64, T), BF16, "ExternalInput"),
        "gsig": _d(nc, "gsig", (D, T), BF16, "ExternalInput"),
        "ldf": _d(nc, "ldf", (128, NH), F32, "ExternalInput"),
        "mlog": _d(nc, "mlog", (NH, 4), F32, "ExternalInput"),
        "onw": _d(nc, "onw", (128, 1), F32, "ExternalInput"),
        "masku": _d(nc, "masku", (5 * 128, NH * 128), mybir.dt.uint8, "ExternalInput"),
        "oneh16": _d(nc, "oneh16", (NH, NH * 128), BF16, "ExternalInput"),
        "pall": _d(nc, "pall", (128, 32), BF16, "ExternalInput"),
        "i4t": _d(nc, "i4t", (4, 128), BF16, "ExternalInput"),
        "ident": _d(nc, "ident", (128, 128), BF16, "ExternalInput"),
        "wmxs": _d(nc, "wmxs", (NH, 4), F32, "Internal"),
        "osc": _d(nc, "osc", (D, T), BF16, "ExternalOutput"),
    }
    _recycled(emit_scan, nc, aps, NH=NH, T=T, tag="s2")
    return nc


def build_C3(T, D=2048):
    nc = bass.Bass("TRN2", target_bir_lowering=False)
    aps = {
        "resid": _d(nc, "resid", (D, T), F32, "ExternalInput"),
        "dwT": _d(nc, "dwT", (D, D), F32, "ExternalInput"),
        "ds": _d(nc, "ds", (128, D // 128), F32, "ExternalInput"),
        "hscr": _d(nc, "hscr", (D, T), BF16, "ExternalInput"),
        "out": _d(nc, "out", (D, T), F32, "ExternalOutput"),
    }
    _recycled(emit_ffn, nc, aps, D=D, I=D, T=T, NQ=2, tag="c3", down_only=True)
    return nc


def run_pipeline(inputs, runner, T=4096, NH=16, ncores=NCORES):
    D = NH * 128
    prep = host_prep(inputs, T=T, NH=NH)
    B = inputs["L0_attn_x"].shape[0]
    xs = [np.ascontiguousarray(inputs["L0_attn_x"][b].T.astype(np.float32))
          for b in range(B)]

    ncA = build_A(T, NH)
    mapsA = [dict(x=xs[b], wqT=prep["wqT"], wkT=prep["wkT"], wvT=prep["wvT"],
                  cosq=prep["cosq"], sinq=prep["sinq"], cosk=prep["cosk"],
                  sink=prep["sink"], cmask=prep["cmask"], ident=prep["ident"])
             for b in range(B)]
    resA = runner(ncA, mapsA, ["xn0", "x1p"])

    ncB = build_ffn(T, 0)
    mapsB = [dict(xn=resA[b]["xn0"], resid=resA[b]["x1p"], gwT=prep["gwT0"],
                  uwT=prep["uwT0"], dwT=prep["dwT0"], gs=prep["gs0"],
                  us=prep["us0"], ds=prep["ds0"]) for b in range(B)]
    resB = runner(ncB, mapsB, ["out"])

    ncC1 = build_C1(T, NH)
    mapsC1 = [dict(x1=resB[b]["out"], qw=prep["qw"], kw=prep["kw"], vw=prep["vw"],
                   bw=prep["bw"], f1w=prep["f1w"], g1w=prep["g1w"], cvw=prep["cvw"],
                   f2wT=prep["f2wT"], g2wT=prep["g2wT"], dtb=prep["dtb"],
                   g2b=prep["g2b"], alog=prep["alog"]) for b in range(B)]
    resC1 = runner(ncC1, mapsC1, ["xn1", "q", "k", "v", "bsg", "gd", "gsig", "ldf"])

    ncC2 = build_C2(T, NH)
    mapsC2 = [dict(q=resC1[b]["q"], k=resC1[b]["k"], v=resC1[b]["v"],
                   gd=resC1[b]["gd"], bsg=resC1[b]["bsg"], gsig=resC1[b]["gsig"],
                   ldf=resC1[b]["ldf"], mlog=prep["mlog"], onw=prep["onw"],
                   masku=prep["masku"], oneh16=prep["oneh16"],
                   pall=prep["pall"], i4t=prep["i4t"],
                   ident=prep["ident"]) for b in range(B)]
    resC2 = runner(ncC2, mapsC2, ["osc"])

    ncC3 = build_C3(T)
    mapsC3 = [dict(resid=resB[b]["out"], dwT=prep["opT"], ds=prep["dsone"],
                   hscr=resC2[b]["osc"]) for b in range(B)]
    resC3 = runner(ncC3, mapsC3, ["out"])

    ncB1 = build_ffn(T, 1)
    mapsB1 = [dict(xn=resC1[b]["xn1"], resid=resC3[b]["out"], gwT=prep["gwT1"],
                   uwT=prep["uwT1"], dwT=prep["dwT1"], gs=prep["gs1"],
                   us=prep["us1"], ds=prep["ds1"]) for b in range(B)]
    resB1 = runner(ncB1, mapsB1, ["out"])

    out = np.stack([resB1[b]["out"].T for b in range(B)], axis=0)
    return np.ascontiguousarray(out.astype(np.float32))


def kernel_multilaunch(**inputs):
    def runner(nc, in_maps, outnames):
        res = run_bass_kernel_spmd(nc, in_maps, core_ids=list(range(len(in_maps))))
        return res.results
    return run_pipeline(inputs, runner, T=4096, NH=16)


def kernel(**inputs):
    return kernel_fused(**inputs)

def build_full(T=4096, NH=16, I=5632):
    D = NH * 128
    nc = bass.Bass("TRN2", target_bir_lowering=False)
    EI, EO, IN = "ExternalInput", "ExternalOutput", "Internal"
    a = {}
    a["x"] = _d(nc, "x", (D, T), F32, EI)
    for n, sh in (("wqT", (D, D)), ("wkT", (D, 128)), ("wvT", (D, 128)),
                  ("cosq", (64, T)), ("sinq", (64, T)), ("cosk", (64, T)),
                  ("sink", (64, T)),
                  ("gwT0", (D, I)), ("uwT0", (D, I)), ("dwT0", (I, D)),
                  ("gs0", (128, I // 128)), ("us0", (128, I // 128)), ("ds0", (128, D // 128)),
                  ("gwT1", (D, I)), ("uwT1", (D, I)), ("dwT1", (I, D)),
                  ("gs1", (128, I // 128)), ("us1", (128, I // 128)), ("ds1", (128, D // 128)),
                  ("qw", (D, D)), ("kw", (D, 4 * D)), ("vw", (D, 4 * D)),
                  ("bw", (D, 64)), ("f1w", (D, 128)), ("g1w", (D, 128)),
                  ("cvw", (128, (NH + 8 * NH) * 4)), ("f2wT", (128, D)),
                  ("g2wT", (128, D)), ("dtb", (128, NH)), ("g2b", (128, NH)),
                  ("alog", (1, NH)), ("mlog", (NH, 4)), ("onw", (128, 1)),
                  ("opT", (D, D)), ("dsone", (128, NH))):
        a[n] = _d(nc, n, sh, F32, EI)
    for n, sh, dt in (("cmask", (128, 2048), BF16), ("ident", (128, 128), BF16),
                      ("masku", (5 * 128, NH * 128), mybir.dt.uint8),
                      ("oneh16", (NH, NH * 128), BF16),
                      ("pall", (128, 32), BF16), ("i4t", (4, 128), BF16)):
        a[n] = _d(nc, n, sh, dt, EI)
    a["xn0"] = _d(nc, "xn0", (D, T), BF16, IN)
    a["x1p"] = _d(nc, "x1p", (D, T), F32, IN)
    a["x1"] = _d(nc, "x1", (D, T), F32, IN)
    a["h0"] = _d(nc, "h0", (I, T), BF16, IN)
    a["xn1"] = _d(nc, "xn1", (D, T), BF16, IN)
    a["q"] = _d(nc, "q", (D, T), BF16, IN)
    a["k"] = _d(nc, "k", (4 * D, T), BF16, IN)
    a["v"] = _d(nc, "v", (4 * D, T), BF16, IN)
    a["bsg"] = _d(nc, "bsg", (64, T), BF16, IN)
    a["f1s"] = _d(nc, "f1s", (128, T), BF16, IN)
    a["g1s"] = _d(nc, "g1s", (128, T), BF16, IN)
    a["gd"] = _d(nc, "gd", (D, T), F32, IN)
    a["gsig"] = _d(nc, "gsig", (D, T), BF16, IN)
    a["ldf"] = _d(nc, "ldf", (128, NH), F32, IN)
    a["wmxs"] = _d(nc, "wmxs", (NH, 4), F32, IN)
    a["osc"] = _d(nc, "osc", (D, T), BF16, IN)
    a["x2p"] = _d(nc, "x2p", (D, T), F32, IN)
    a["h1"] = _d(nc, "h1", (I, T), BF16, IN)
    a["out"] = _d(nc, "out", (D, T), F32, EO)

    _recycled(emit_ln, nc, {"x": a["x"], "xn": a["xn0"]}, D=D, T=T, tag="ln0")
    _recycled(emit_attn0, nc, {"x": a["x"], "xn": a["xn0"], "wqT": a["wqT"], "wkT": a["wkT"],
                    "wvT": a["wvT"], "cosq": a["cosq"], "sinq": a["sinq"],
                    "cosk": a["cosk"], "sink": a["sink"], "cmask": a["cmask"],
                    "ident": a["ident"], "x1p": a["x1p"]}, NH=NH, T=T, tag="a0")
    _recycled(emit_ffn, nc, {"xn": a["xn0"], "resid": a["x1p"], "gwT": a["gwT0"],
                  "uwT": a["uwT0"], "dwT": a["dwT0"], "gs": a["gs0"],
                  "us": a["us0"], "ds": a["ds0"], "hscr": a["h0"],
                  "out": a["x1"]}, D=D, I=I, T=T, NQ=4, tag="f0")
    _recycled(emit_ln, nc, {"x": a["x1"], "xn": a["xn1"]}, D=D, T=T, tag="ln1")
    _recycled(emit_l1proj, nc, {"xn1": a["xn1"], "q": a["q"], "k": a["k"], "v": a["v"],
                     "bsg": a["bsg"], "f1s": a["f1s"], "g1s": a["g1s"],
                     "qw": a["qw"], "kw": a["kw"], "vw": a["vw"], "bw": a["bw"],
                     "f1w": a["f1w"], "g1w": a["g1w"], "cvw": a["cvw"]},
                NH=NH, T=T, tag="c1")
    _recycled(emit_l1bg, nc, {"f1s": a["f1s"], "g1s": a["g1s"], "f2wT": a["f2wT"],
                   "g2wT": a["g2wT"], "dtb": a["dtb"], "alog": a["alog"],
                   "g2b": a["g2b"], "gd": a["gd"], "gsig": a["gsig"],
                   "ldf": a["ldf"]}, NH=NH, T=T, tag="cb")
    _recycled(emit_scan, nc, {"q": a["q"], "k": a["k"], "v": a["v"], "gd": a["gd"],
                   "bsg": a["bsg"], "gsig": a["gsig"], "ldf": a["ldf"],
                   "mlog": a["mlog"], "onw": a["onw"], "masku": a["masku"],
                   "oneh16": a["oneh16"], "pall": a["pall"], "i4t": a["i4t"],
                   "ident": a["ident"], "wmxs": a["wmxs"], "osc": a["osc"]},
              NH=NH, T=T, tag="s2")
    _recycled(emit_ffn, nc, {"resid": a["x1"], "dwT": a["opT"], "ds": a["dsone"],
                  "hscr": a["osc"], "out": a["x2p"]}, D=D, I=D, T=T, NQ=2,
             tag="c3", down_only=True)
    _recycled(emit_ffn, nc, {"xn": a["xn1"], "resid": a["x2p"], "gwT": a["gwT1"],
                  "uwT": a["uwT1"], "dwT": a["dwT1"], "gs": a["gs1"],
                  "us": a["us1"], "ds": a["ds1"], "hscr": a["h1"],
                  "out": a["out"]}, D=D, I=I, T=T, NQ=4, tag="f1")
    return nc


def kernel_fused(**inputs):
    T, NH = 4096, 16
    prep = host_prep(inputs, T=T, NH=NH)
    B = inputs["L0_attn_x"].shape[0]
    xs = [np.ascontiguousarray(inputs["L0_attn_x"][b].T.astype(np.float32))
          for b in range(B)]
    nc = build_full(T=T, NH=NH)
    names = ("wqT wkT wvT cosq sinq cosk sink cmask ident gwT0 uwT0 dwT0 gs0 us0 "
             "ds0 gwT1 uwT1 dwT1 gs1 us1 ds1 qw kw vw bw f1w g1w cvw f2wT g2wT "
             "dtb g2b alog mlog onw opT dsone masku oneh16 pall i4t").split()
    base = {n: prep[n] for n in names}
    in_maps = [dict(base, x=xs[b]) for b in range(B)]
    res = run_bass_kernel_spmd(nc, in_maps, core_ids=list(range(B)))
    out = np.stack([res.results[b]["out"].T for b in range(B)], axis=0)
    return np.ascontiguousarray(out.astype(np.float32))


# version 09
# speedup vs baseline: 1.0840x; 1.0840x over previous
import numpy as np
import ml_dtypes
import concourse.bass as bass
import concourse.mybir as mybir
from concourse.bass_utils import run_bass_kernel_spmd

F32 = mybir.dt.float32
BF16 = mybir.dt.bfloat16
AF = mybir.ActivationFunctionType
ALU = mybir.AluOpType
AX = mybir.AxisListType
BFNP = ml_dtypes.bfloat16

NCORES = 8


CTR_LOG = []


class Ctr:

    def __init__(self, nc, name):
        self.h = nc.alloc_semaphore(name)
        self.v = 0
        CTR_LOG.append(self)


def feat_ap(dram_ap, T, chunk0=0, nch=1, col0=0, ncols=None):
    if ncols is None:
        ncols = T
    return bass.AP(dram_ap.tensor, dram_ap.offset + chunk0 * 128 * T + col0,
                   [[T, 128], [T * 128, nch], [1, ncols]])


def emit_ffn(nc, aps, D=2048, I=5632, T=4096, NQ=4, tag="f0", down_only=False):
    NK = D // 128
    NI = I // 128
    NTB = T // 512
    ND = D // 128
    NDq = ND // NQ

    resid, dwT = aps["resid"], aps["dwT"]
    ds, hscr, out = aps["ds"], aps["hscr"], aps["out"]
    if down_only:
        _emit_ffn_b2(nc, resid, dwT, ds, hscr, out, D, I, T, NQ, tag)
        return
    xn, gwT, uwT = aps["xn"], aps["gwT"], aps["uwT"]
    gs, us = aps["gs"], aps["us"]

    with (
        nc.sbuf_tensor(f"{tag}xns", [128, NK * T], BF16) as xns,
        nc.sbuf_tensor(f"{tag}gw0", [128, D], BF16) as gw0,
        nc.sbuf_tensor(f"{tag}gw1", [128, D], BF16) as gw1,
        nc.sbuf_tensor(f"{tag}uw0", [128, D], BF16) as uw0,
        nc.sbuf_tensor(f"{tag}uw1", [128, D], BF16) as uw1,
        nc.sbuf_tensor(f"{tag}hb0", [128, T], BF16) as hb0,
        nc.sbuf_tensor(f"{tag}hb1", [128, T], BF16) as hb1,
        nc.sbuf_tensor(f"{tag}gss", [128, NI], F32) as gss,
        nc.sbuf_tensor(f"{tag}uss", [128, NI], F32) as uss,
        nc.sbuf_tensor(f"{tag}sil0", [128, 512], F32) as sil0,
        nc.sbuf_tensor(f"{tag}sil1", [128, 512], F32) as sil1,
        nc.psum_tensor(f"{tag}p0", [128, 512], F32) as p0,
        nc.psum_tensor(f"{tag}p1", [128, 512], F32) as p1,
        nc.psum_tensor(f"{tag}p2", [128, 512], F32) as p2,
        nc.psum_tensor(f"{tag}p3", [128, 512], F32) as p3,
        nc.Block() as blk,
    ):
        wbufs = [(gw0, uw0), (gw1, uw1)]
        hbufs = [hb0, hb1]
        sils = [sil0, sil1]
        psum = [p0, p1, p2, p3]
        ld = [Ctr(nc, f"{tag}ld0"), Ctr(nc, f"{tag}ld1")]
        xld = Ctr(nc, f"{tag}xld")
        mm = Ctr(nc, f"{tag}mm")
        evA = Ctr(nc, f"{tag}evA")
        evD = Ctr(nc, f"{tag}evD")
        hst = [Ctr(nc, f"{tag}hst0"), Ctr(nc, f"{tag}hst1")]

        @blk.gpsimd
        def _(g):
            g.dma_start(xns[:], feat_ap(xn, T, nch=NK)).then_inc(xld.h, 16)
            g.dma_start(gss[:], bass.AP(gs.tensor, gs.offset, [[NI, 128], [1, NI]])
                        ).then_inc(xld.h, 16)
            g.dma_start(uss[:], bass.AP(us.tensor, us.offset, [[NI, 128], [1, NI]])
                        ).then_inc(xld.h, 16)
            xld.v = 48
            for ic in range(NI):
                if ic >= 2:
                    g.wait_ge(mm.h, (ic - 1) * NTB * 2)
                gw, uw = wbufs[ic % 2]
                g.dma_start(gw[:], feat_ap(gwT, I, col0=ic * 128, ncols=128, nch=NK)
                            ).then_inc(ld[ic % 2].h, 16)
                g.dma_start(uw[:], feat_ap(uwT, I, col0=ic * 128, ncols=128, nch=NK)
                            ).then_inc(ld[ic % 2].h, 16)
                ld[ic % 2].v += 32

        @blk.tensor
        def _(t):
            t.wait_ge(xld.h, xld.v)
            for ic in range(NI):
                gw, uw = wbufs[ic % 2]
                t.wait_ge(ld[ic % 2].h, 32 * (ic // 2 + 1))
                for tb in range(NTB):
                    j = ic * NTB + tb
                    for half, w in ((0, gw), (1, uw)):
                        n = 2 * j + half
                        bank = psum[n % 4]
                        if n >= 4:
                            if half == 0:
                                t.wait_ge(evA.h, (n - 4) // 2 + 1)
                            else:
                                t.wait_ge(evD.h, (n - 5) // 2 + 1)
                        ins = None
                        for k in range(NK):
                            ins = t.matmul(bank[:],
                                           w[:, k * 128:(k + 1) * 128],
                                           xns[:, k * T + tb * 512: k * T + tb * 512 + 512],
                                           start=(k == 0), stop=(k == NK - 1))
                        ins.then_inc(mm.h, 1)
            mm.v = 2 * NI * NTB

        @blk.scalar
        def _(s):
            s.wait_ge(xld.h, 48)
            for ic in range(NI):
                for tb in range(NTB):
                    j = ic * NTB + tb
                    if j >= 2:
                        s.wait_ge(evD.h, j - 1)
                    s.wait_ge(mm.h, 2 * j + 1)
                    s.activation(sils[j % 2][:], psum[(2 * j) % 4][:], AF.Silu,
                                 scale=gss[:, ic:ic + 1]).then_inc(evA.h, 1)
            evA.v = NI * NTB

        @blk.vector
        def _(v):
            for ic in range(NI):
                hbuf = hbufs[ic % 2]
                if ic >= 2:
                    v.wait_ge(hst[ic % 2].h, 16 * (ic // 2))
                for tb in range(NTB):
                    j = ic * NTB + tb
                    v.wait_ge(mm.h, 2 * j + 2)
                    v.wait_ge(evA.h, j + 1)
                    v.scalar_tensor_tensor(hbuf[:, tb * 512:(tb + 1) * 512],
                                           psum[(2 * j + 1) % 4][:], uss[:, ic:ic + 1],
                                           sils[j % 2][:], ALU.mult, ALU.mult
                                           ).then_inc(evD.h, 1)
            evD.v = NI * NTB

        @blk.sync
        def _(s):
            for ic in range(NI):
                s.wait_ge(evD.h, NTB * (ic + 1))
                s.dma_start(hscr[ic * 128:(ic + 1) * 128, :], hbufs[ic % 2][:]
                            ).then_inc(hst[ic % 2].h, 16)
                hst[ic % 2].v += 16
            s.wait_ge(hst[0].h, hst[0].v)
            s.wait_ge(hst[1].h, hst[1].v)

    _emit_ffn_b2(nc, resid, dwT, ds, hscr, out, D, I, T, NQ, tag)


def _emit_ffn_b2(nc, resid, dwT, ds, hscr, out, D, I, T, NQ, tag):
    NK = D // 128
    NI = I // 128
    NTB = T // 512
    ND = D // 128
    NDq = ND // NQ
    with (
        nc.sbuf_tensor(f"{tag}dw", [128, NI * NDq * 128], BF16) as dw,
        nc.sbuf_tensor(f"{tag}hs0", [128, NI * 512], BF16) as hs0,
        nc.sbuf_tensor(f"{tag}hs1", [128, NI * 512], BF16) as hs1,
        nc.sbuf_tensor(f"{tag}rs0", [128, NDq * 512], F32) as rs0,
        nc.sbuf_tensor(f"{tag}rs1", [128, NDq * 512], F32) as rs1,
        nc.sbuf_tensor(f"{tag}os0", [128, NDq * 512], F32) as os0,
        nc.sbuf_tensor(f"{tag}os1", [128, NDq * 512], F32) as os1,
        nc.sbuf_tensor(f"{tag}dss", [128, ND], F32) as dss,
        nc.psum_tensor(f"{tag}q0", [128, 512], F32) as q0,
        nc.psum_tensor(f"{tag}q1", [128, 512], F32) as q1,
        nc.psum_tensor(f"{tag}q2", [128, 512], F32) as q2,
        nc.psum_tensor(f"{tag}q3", [128, 512], F32) as q3,
        nc.Block() as blk,
    ):
        psum = [q0, q1, q2, q3]
        hbufs = [hs0, hs1]
        rbufs = [rs0, rs1]
        obufs = [os0, os1]
        wld = Ctr(nc, f"{tag}wld")
        hld = [Ctr(nc, f"{tag}hld0"), Ctr(nc, f"{tag}hld1")]
        rld = [Ctr(nc, f"{tag}rld0"), Ctr(nc, f"{tag}rld1")]
        mm2 = Ctr(nc, f"{tag}mm2")
        ev2 = Ctr(nc, f"{tag}ev2")
        ost = [Ctr(nc, f"{tag}ost0"), Ctr(nc, f"{tag}ost1")]

        @blk.gpsimd
        def _(g):
            g.dma_start(dss[:], bass.AP(ds.tensor, ds.offset, [[ND, 128], [1, ND]])
                        ).then_inc(wld.h, 16)
            wld.v += 16
            nb = 0
            for q in range(NQ):
                src = bass.AP(dwT.tensor, dwT.offset + q * NDq * 128,
                              [[D, 128], [D * 128, NI], [128, NDq], [1, 128]])
                dst = bass.AP(dw, 0, [[NI * NDq * 128, 128], [NDq * 128, NI],
                                      [128, NDq], [1, 128]])
                if q >= 1:
                    g.wait_ge(mm2.h, q * NTB * NDq)
                g.dma_start(dst, src).then_inc(wld.h, 16)
                wld.v += 16
                for tb in range(NTB):
                    if nb >= 2:
                        g.wait_ge(mm2.h, (nb - 1) * NDq)
                        g.wait_ge(ev2.h, (nb - 1) * NDq)
                    g.dma_start(hbufs[nb % 2][:],
                                feat_ap(hscr, T, col0=tb * 512, ncols=512, nch=NI)
                                ).then_inc(hld[nb % 2].h, 16)
                    hld[nb % 2].v += 16
                    g.dma_start(rbufs[nb % 2][:],
                                feat_ap(resid, T, chunk0=q * NDq, nch=NDq,
                                        col0=tb * 512, ncols=512)
                                ).then_inc(rld[nb % 2].h, 16)
                    rld[nb % 2].v += 16
                    nb += 1

        @blk.tensor
        def _(t):
            n = 0
            nb = 0
            for q in range(NQ):
                t.wait_ge(wld.h, 16 * (q + 2))
                for tb in range(NTB):
                    t.wait_ge(hld[nb % 2].h, 16 * (nb // 2 + 1))
                    for dc in range(NDq):
                        bank = psum[n % 4]
                        if n >= 4:
                            t.wait_ge(ev2.h, n - 3)
                        ins = None
                        for icc in range(NI):
                            ins = t.matmul(bank[:],
                                           dw[:, icc * NDq * 128 + dc * 128:
                                              icc * NDq * 128 + dc * 128 + 128],
                                           hbufs[nb % 2][:, icc * 512:(icc + 1) * 512],
                                           start=(icc == 0), stop=(icc == NI - 1))
                        ins.then_inc(mm2.h, 1)
                        n += 1
                    nb += 1
            mm2.v = n

        @blk.vector
        def _(v):
            n = 0
            nb = 0
            for q in range(NQ):
                for tb in range(NTB):
                    v.wait_ge(rld[nb % 2].h, 16 * (nb // 2 + 1))
                    if nb >= 2:
                        v.wait_ge(ost[nb % 2].h, 16 * (nb // 2))
                    for dc in range(NDq):
                        v.wait_ge(mm2.h, n + 1)
                        v.scalar_tensor_tensor(obufs[nb % 2][:, dc * 512:(dc + 1) * 512],
                                               psum[n % 4][:],
                                               dss[:, q * NDq + dc:q * NDq + dc + 1],
                                               rbufs[nb % 2][:, dc * 512:(dc + 1) * 512],
                                               ALU.mult, ALU.add).then_inc(ev2.h, 1)
                        n += 1
                    nb += 1
            ev2.v = n

        @blk.sync
        def _(s):
            nb = 0
            for q in range(NQ):
                for tb in range(NTB):
                    s.wait_ge(ev2.h, (nb + 1) * NDq)
                    dst = feat_ap(out, T, chunk0=q * NDq, nch=NDq, col0=tb * 512, ncols=512)
                    s.dma_start(dst, obufs[nb % 2][:]).then_inc(ost[nb % 2].h, 16)
                    ost[nb % 2].v += 16
                    nb += 1
            s.wait_ge(ost[0].h, ost[0].v)
            s.wait_ge(ost[1].h, ost[1].v)


def kernel(**inputs):
    raise NotImplementedError("assembled later")




def emit_ln(nc, aps, D=2048, T=4096, tag="ln0"):
    NK = D // 128
    NTB = T // 512
    x, xn = aps["x"], aps["xn"]
    with (
        nc.sbuf_tensor(f"{tag}xb", [128, NK * 512], BF16) as xb,
        nc.sbuf_tensor(f"{tag}xs", [128, NK * 512], F32) as xs,
        nc.sbuf_tensor(f"{tag}xnb", [128, NK * 512], BF16) as xnb,
        nc.sbuf_tensor(f"{tag}sq0", [128, 512], BF16) as sq0,
        nc.sbuf_tensor(f"{tag}sq1", [128, 512], BF16) as sq1,
        nc.sbuf_tensor(f"{tag}onesc", [128, 1], BF16) as onesc,
        nc.sbuf_tensor(f"{tag}ones1", [1, 128], F32) as ones1,
        nc.sbuf_tensor(f"{tag}rows", [1, 4 * 512], F32) as rows,
        nc.sbuf_tensor(f"{tag}abc", [128, 512], F32) as abc,
        nc.sbuf_tensor(f"{tag}bbc", [128, 512], F32) as bbc,
        nc.sbuf_tensor(f"{tag}t3", [128, 512], F32) as t3,
        nc.psum_tensor(f"{tag}pa", [128, 512], F32) as pa,
        nc.psum_tensor(f"{tag}pb", [128, 512], F32) as pb,
        nc.psum_tensor(f"{tag}pc", [128, 512], F32) as pc,
        nc.psum_tensor(f"{tag}pd", [128, 512], F32) as pd,
        nc.Block() as blk,
    ):
        sqs = [sq0, sq1]
        R_MU = rows[0:1, 0:512]
        R_MU2 = rows[0:1, 512:1024]
        R_V = rows[0:1, 1024:1536]
        R_A = rows[0:1, 1536:2048]
        xldb = Ctr(nc, f"{tag}xldb")
        xlds = Ctr(nc, f"{tag}xlds")
        evq = Ctr(nc, f"{tag}evq")
        mmq = Ctr(nc, f"{tag}mmq")
        evrow = Ctr(nc, f"{tag}evrow")
        evsq = Ctr(nc, f"{tag}evsq")
        mmb = Ctr(nc, f"{tag}mmb")
        evx = Ctr(nc, f"{tag}evx")
        xst = Ctr(nc, f"{tag}xst")
        dvc = Ctr(nc, f"{tag}dvc")

        @blk.gpsimd
        def _(g):
            for qb in range(NTB):
                if qb >= 1:
                    g.wait_ge(mmq.h, NK * qb)
                    g.wait_ge(evq.h, NK * qb)
                    g.wait_ge(evx.h, qb)
                g.dma_start(xb[:], feat_ap(x, T, nch=NK, col0=qb * 512, ncols=512)
                            ).then_inc(xldb.h, 16)
                g.dma_start(xs[:], feat_ap(x, T, nch=NK, col0=qb * 512, ncols=512)
                            ).then_inc(xlds.h, 16)
                xldb.v += 16
                xlds.v += 16

        @blk.vector
        def _(v):
            v.memset(onesc[:], 1.0)
            v.memset(ones1[:], 1.0).then_inc(evq.h, 1)
            for qb in range(NTB):
                v.wait_ge(xldb.h, 16 * (qb + 1))
                for k in range(NK):
                    j = qb * NK + k
                    if j >= 2:
                        v.wait_ge(mmq.h, j - 1)
                    v.tensor_mul(sqs[j % 2][:], xb[:, k * 512:(k + 1) * 512],
                                 xb[:, k * 512:(k + 1) * 512]).then_inc(evq.h, 1)
                v.wait_ge(mmq.h, NK * (qb + 1))
                v.tensor_scalar_mul(R_MU, pa[0:1, :], 1.0 / D)
                v.tensor_scalar_mul(R_MU2, pb[0:1, :], 1.0 / D).then_inc(evrow.h, 1)
                v.wait_ge(evrow.h, 3 * qb + 1)
                v.scalar_tensor_tensor(R_V, R_MU, 1.0, R_MU, ALU.mult, ALU.mult
                                       ).then_inc(dvc.h, 1)
                dvc.v += 1
                v.wait_ge(dvc.h, dvc.v)
                v.scalar_tensor_tensor(R_V, R_MU2, 1e-5, R_V, ALU.add, ALU.subtract
                                       ).then_inc(evrow.h, 1)
                v.wait_ge(evsq.h, qb + 1)
                v.reciprocal(R_A, R_V).then_inc(dvc.h, 1)
                dvc.v += 1
                v.wait_ge(dvc.h, dvc.v)
                v.scalar_tensor_tensor(R_MU, R_MU, 1.0, R_A, ALU.mult, ALU.mult
                                       ).then_inc(evrow.h, 1)
                v.wait_ge(mmb.h, 2 * (qb + 1))
                v.wait_ge(xlds.h, 16 * (qb + 1))
                if qb >= 1:
                    v.wait_ge(xst.h, 16 * qb)
                v.tensor_copy(abc[:], pc[:])
                v.tensor_copy(bbc[:], pd[:]).then_inc(dvc.h, 1)
                dvc.v += 1
                v.wait_ge(dvc.h, dvc.v)
                for k in range(NK):
                    v.tensor_mul(t3[:], xs[:, k * 512:(k + 1) * 512], abc[:]
                                 ).then_inc(dvc.h, 1)
                    dvc.v += 1
                    v.wait_ge(dvc.h, dvc.v)
                    ins = v.tensor_sub(xnb[:, k * 512:(k + 1) * 512], t3[:], bbc[:])
                    if k == NK - 1:
                        ins.then_inc(evx.h, 1)
                        evx.v += 1
                    else:
                        ins.then_inc(dvc.h, 1)
                        dvc.v += 1
                        v.wait_ge(dvc.h, dvc.v)

        @blk.scalar
        def _(s):
            for qb in range(NTB):
                s.wait_ge(evrow.h, 3 * qb + 2)
                s.sqrt(R_V, R_V).then_inc(evsq.h, 1)

        @blk.tensor
        def _(t):
            t.wait_ge(evq.h, 1)
            for qb in range(NTB):
                if qb >= 1:
                    t.wait_ge(evrow.h, 3 * qb - 2)
                t.wait_ge(xldb.h, 16 * (qb + 1))
                for k in range(NK):
                    j = qb * NK + k
                    t.wait_ge(evq.h, j + 2)
                    t.matmul(pa[0:1, :], onesc[:], xb[:, k * 512:(k + 1) * 512],
                             start=(k == 0), stop=(k == NK - 1))
                    ins = t.matmul(pb[0:1, :], onesc[:], sqs[j % 2][:],
                                   start=(k == 0), stop=(k == NK - 1))
                    ins.then_inc(mmq.h, 1)
                    mmq.v += 1
                t.wait_ge(evrow.h, 3 * (qb + 1))
                if qb >= 1:
                    t.wait_ge(evx.h, qb)
                t.matmul(pc[:], ones1[:, :], R_A, start=True, stop=True).then_inc(mmb.h, 1)
                t.matmul(pd[:], ones1[:, :], R_MU, start=True, stop=True).then_inc(mmb.h, 1)
                mmb.v += 2

        @blk.sync
        def _(s):
            for qb in range(NTB):
                s.wait_ge(evx.h, qb + 1)
                s.dma_start(feat_ap(xn, T, nch=NK, col0=qb * 512, ncols=512), xnb[:]
                            ).then_inc(xst.h, 16)
                xst.v += 16
            s.wait_ge(xst.h, xst.v)

MBIAS = 30.0


def emit_attn0(nc, aps, NH=16, T=4096, tag="a0"):
    D = NH * 128
    NK = D // 128
    NTB = T // 512
    NW = NK + 2
    x, xn, wqT, wkT, wvT = aps["x"], aps["xn"], aps["wqT"], aps["wkT"], aps["wvT"]
    cosq, sinq, cosk, sink = aps["cosq"], aps["sinq"], aps["cosk"], aps["sink"]
    cmask, x1p = aps["cmask"], aps["x1p"]

    tiles = []
    ndiag = 0
    for qb in range(NTB):
        NKT = 4 * (qb + 1)
        for h in range(NH):
            for kt in range(NKT):
                if kt >= NKT - 4:
                    tiles.append((qb, h, kt, NKT, ndiag))
                    ndiag += 1
                else:
                    tiles.append((qb, h, kt, NKT, None))

    from contextlib import ExitStack
    es = ExitStack()
    wq = es.enter_context(nc.sbuf_tensor(f"{tag}wq", [128, NK * NW * 128], BF16))
    xnb = es.enter_context(nc.sbuf_tensor(f"{tag}xnb", [128, NK * 512], BF16))
    xs = es.enter_context(nc.sbuf_tensor(f"{tag}xs", [128, NK * 512], F32))
    qs = es.enter_context(nc.sbuf_tensor(f"{tag}qs", [128, NH * 512], BF16))
    kc = es.enter_context(nc.sbuf_tensor(f"{tag}kc", [128, T], BF16))
    vt = es.enter_context(nc.sbuf_tensor(f"{tag}vt", [128, T], BF16))
    vtmp = es.enter_context(nc.sbuf_tensor(f"{tag}vtmp", [128, 512], BF16))
    rope = es.enter_context(nc.sbuf_tensor(f"{tag}rope", [64, 4 * 512], F32))
    pb0 = es.enter_context(nc.sbuf_tensor(f"{tag}pb0", [128, 512], BF16))
    pb1 = es.enter_context(nc.sbuf_tensor(f"{tag}pb1", [128, 512], BF16))
    t1 = es.enter_context(nc.sbuf_tensor(f"{tag}t1", [64, 512], F32))
    t2 = es.enter_context(nc.sbuf_tensor(f"{tag}t2", [64, 512], F32))
    t3 = es.enter_context(nc.sbuf_tensor(f"{tag}t3", [128, 512], F32))
    rrow = es.enter_context(nc.sbuf_tensor(f"{tag}rrow", [1, 512], F32))
    rbc = es.enter_context(nc.sbuf_tensor(f"{tag}rbc", [128, 512], F32))
    ones1 = es.enter_context(nc.sbuf_tensor(f"{tag}ones1", [1, 128], F32))
    onesc = es.enter_context(nc.sbuf_tensor(f"{tag}onesc", [128, 1], BF16))
    mbc = es.enter_context(nc.sbuf_tensor(f"{tag}mbc", [128, 1], F32))
    cm = es.enter_context(nc.sbuf_tensor(f"{tag}cm", [128, 4 * 512], BF16))
    idn = es.enter_context(nc.sbuf_tensor(f"{tag}idn", [128, 128], BF16))
    ps0 = es.enter_context(nc.psum_tensor(f"{tag}s0", [128, 512], F32))
    ps1 = es.enter_context(nc.psum_tensor(f"{tag}s1", [128, 512], F32))
    ps2 = es.enter_context(nc.psum_tensor(f"{tag}s2", [128, 512], F32))
    ps3 = es.enter_context(nc.psum_tensor(f"{tag}s3", [128, 512], F32))
    psO = es.enter_context(nc.psum_tensor(f"{tag}sO", [128, 512], F32))
    psden = es.enter_context(nc.psum_tensor(f"{tag}sden", [128, 512], F32))
    pst = es.enter_context(nc.psum_tensor(f"{tag}st", [128, 128], BF16))
    with es, nc.Block() as blk:
        psum = [ps0, ps1, ps2, ps3]
        pbufs = [pb0, pb1]
        wld = Ctr(nc, f"{tag}wld")
        xnl = Ctr(nc, f"{tag}xnl")
        xsl = Ctr(nc, f"{tag}xsl")
        rl = Ctr(nc, f"{tag}rl")
        ini = Ctr(nc, f"{tag}ini")
        mmQ = Ctr(nc, f"{tag}mmQ")
        evQ = Ctr(nc, f"{tag}evQ")
        mmT = Ctr(nc, f"{tag}mmT")
        evT = Ctr(nc, f"{tag}evT")
        mmA = Ctr(nc, f"{tag}mmA")
        evE = Ctr(nc, f"{tag}evE")
        evM = Ctr(nc, f"{tag}evM")
        mmO = Ctr(nc, f"{tag}mmO")
        evR = Ctr(nc, f"{tag}evR")
        mmR = Ctr(nc, f"{tag}mmR")
        evN = Ctr(nc, f"{tag}evN")
        st = Ctr(nc, f"{tag}st")
        dvc = Ctr(nc, f"{tag}dvc")

        @blk.gpsimd
        def _(g):
            srcq = bass.AP(wqT.tensor, wqT.offset,
                           [[NH * 128, 128], [NH * 128 * 128, NK], [128, NH], [1, 128]])
            dstq = bass.AP(wq, 0, [[NK * NW * 128, 128], [NW * 128, NK], [128, NK], [1, 128]])
            g.dma_start(dstq, srcq).then_inc(wld.h, 16)
            srck = bass.AP(wkT.tensor, wkT.offset, [[128, 128], [128 * 128, NK], [1, 128]])
            dstk = bass.AP(wq, NK * 128, [[NK * NW * 128, 128], [NW * 128, NK], [1, 128]])
            g.dma_start(dstk, srck).then_inc(wld.h, 16)
            srcv = bass.AP(wvT.tensor, wvT.offset, [[128, 128], [128 * 128, NK], [1, 128]])
            dstv = bass.AP(wq, (NK + 1) * 128, [[NK * NW * 128, 128], [NW * 128, NK], [1, 128]])
            g.dma_start(dstv, srcv).then_inc(wld.h, 16)
            g.dma_start(cm[:], bass.AP(cmask.tensor, cmask.offset,
                                       [[4 * 512, 128], [1, 4 * 512]])).then_inc(wld.h, 16)
            g.dma_start(idn[:], bass.AP(aps["ident"].tensor, aps["ident"].offset,
                                        [[128, 128], [1, 128]])).then_inc(wld.h, 16)
            wld.v = 80
            for qb in range(NTB):
                if qb >= 1:
                    g.wait_ge(st.h, 16 * qb)
                    g.wait_ge(mmQ.h, NW * qb)
                    g.wait_ge(evQ.h, NW * qb)
                g.dma_start(xnb[:], feat_ap(xn, T, nch=NK, col0=qb * 512, ncols=512)
                            ).then_inc(xnl.h, 16)
                xnl.v += 16
                g.dma_start(xs[:], feat_ap(x, T, nch=NK, col0=qb * 512, ncols=512)
                            ).then_inc(xsl.h, 16)
                xsl.v += 16
                for i, tb in enumerate((cosq, sinq, cosk, sink)):
                    g.dma_start(rope[:, i * 512:(i + 1) * 512],
                                bass.AP(tb.tensor, tb.offset + qb * 512, [[T, 64], [1, 512]])
                                ).then_inc(rl.h, 16)
                rl.v += 64

        @blk.tensor
        def _(t):
            t.wait_ge(wld.h, wld.v)
            t.wait_ge(ini.h, 2)
            atc = 0
            for qb in range(NTB):
                NKT = 4 * (qb + 1)
                hbase = qb * NH
                t.wait_ge(xnl.h, 16 * (qb + 1))
                if qb >= 1:
                    t.wait_ge(evN.h, qb * NH)
                for j in range(NW):
                    n = qb * NW + j
                    if n >= 4:
                        t.wait_ge(evQ.h, n - 3)
                    bank = psum[n % 4]
                    ins = None
                    for k in range(NK):
                        ins = t.matmul(bank[:],
                                       wq[:, k * NW * 128 + j * 128: k * NW * 128 + j * 128 + 128],
                                       xnb[:, k * 512:(k + 1) * 512],
                                       start=(k == 0), stop=(k == NK - 1))
                    ins.then_inc(mmQ.h, 1)
                    mmQ.v += 1
                t.wait_ge(evQ.h, (qb + 1) * NW)
                for i in range(4):
                    if qb > 0 or i > 0:
                        t.wait_ge(evT.h, qb * 4 + i)
                    t.transpose(pst[:], vtmp[:, i * 128:(i + 1) * 128], idn[:]
                                ).then_inc(mmT.h, 1)
                    mmT.v += 1
                t.wait_ge(evT.h, (qb + 1) * 4)
                for h in range(NH):
                    hh = hbase + h
                    t.wait_ge(evR.h, hh)
                    t.wait_ge(evN.h, hh)
                    for kt in range(NKT):
                        a = atc + kt
                        if a >= 2:
                            t.wait_ge(evE.h, a - 1)
                        t.matmul(psum[a % 2][:], kc[:, kt * 128:(kt + 1) * 128],
                                 qs[:, h * 512:(h + 1) * 512], start=True, stop=True
                                 ).then_inc(mmA.h, 1)
                        mmA.v += 1
                        if kt >= 1:
                            b = a - 1
                            bkt = kt - 1
                            t.wait_ge(evE.h, b + 1)
                            if tiles[b][4] is not None:
                                t.wait_ge(evM.h, tiles[b][4] + 1)
                            t.matmul(psden[0:1, :], onesc[:], pbufs[b % 2][:],
                                     start=(bkt == 0), stop=False)
                            t.matmul(psO[:], vt[:, bkt * 128:(bkt + 1) * 128],
                                     pbufs[b % 2][:],
                                     start=(bkt == 0), stop=(bkt == NKT - 1)
                                     ).then_inc(mmO.h, 1)
                            mmO.v += 1
                    b = atc + NKT - 1
                    bkt = NKT - 1
                    t.wait_ge(evE.h, b + 1)
                    if tiles[b][4] is not None:
                        t.wait_ge(evM.h, tiles[b][4] + 1)
                    t.matmul(psden[0:1, :], onesc[:], pbufs[b % 2][:],
                             start=(bkt == 0), stop=True)
                    ins = t.matmul(psO[:], vt[:, bkt * 128:(bkt + 1) * 128],
                                   pbufs[b % 2][:], start=(bkt == 0), stop=True)
                    ins.then_inc(mmO.h, 1)
                    mmO.v += 1
                    atc += NKT
                    t.wait_ge(evR.h, hh + 1)
                    t.matmul(ps2[:], ones1[:, :], rrow[0:1, :], start=True, stop=True
                             ).then_inc(mmR.h, 1)
                    mmR.v += 1

        @blk.scalar
        def _(s):
            for a, (qb, h, kt, NKT, di) in enumerate(tiles):
                s.wait_ge(mmA.h, a + 1)
                if a >= 2:
                    s.wait_ge(mmO.h, a - 1)
                s.activation(pbufs[a % 2][:], psum[a % 2][:], AF.Exp,
                             bias=mbc[:, 0:1]).then_inc(evE.h, 1)
            evE.v = len(tiles)

        @blk.vector
        def _(v):
            v.memset(ones1[:], 1.0)
            v.memset(mbc[:], -MBIAS)
            v.memset(onesc[:], 1.0).then_inc(ini.h, 2)
            ini.v = 2
            atc = 0
            for qb in range(NTB):
                NKT = 4 * (qb + 1)
                hbase = qb * NH
                v.wait_ge(rl.h, 64 * (qb + 1))
                for j in range(NW):
                    n = qb * NW + j
                    v.wait_ge(mmQ.h, n + 1)
                    bank = psum[n % 4]
                    if j <= NK:
                        co = 0 if j < NK else 2 * 512
                        so = 512 if j < NK else 3 * 512
                        if j < NK:
                            dstE = qs[0:64, j * 512:(j + 1) * 512]
                            dstO = qs[64:128, j * 512:(j + 1) * 512]
                        else:
                            dstE = kc[0:64, qb * 512:(qb + 1) * 512]
                            dstO = kc[64:128, qb * 512:(qb + 1) * 512]
                        E = bank[0:64, :]
                        O = bank[64:128, :]
                        v.wait_ge(evQ.h, evQ.v)
                        v.tensor_mul(t1[:], E, rope[:, co:co + 512])
                        v.tensor_mul(t2[:], O, rope[:, so:so + 512]).then_inc(dvc.h, 1)
                        dvc.v += 1
                        v.wait_ge(dvc.h, dvc.v)
                        v.tensor_sub(dstE, t1[:], t2[:]).then_inc(dvc.h, 1)
                        dvc.v += 1
                        v.wait_ge(dvc.h, dvc.v)
                        v.tensor_mul(t1[:], O, rope[:, co:co + 512])
                        v.tensor_mul(t2[:], E, rope[:, so:so + 512]).then_inc(dvc.h, 1)
                        dvc.v += 1
                        v.wait_ge(dvc.h, dvc.v)
                        ins = v.tensor_add(dstO, t1[:], t2[:])
                    else:
                        ins = v.tensor_copy(vtmp[:], bank[:])
                    ins.then_inc(evQ.h, 1)
                    evQ.v += 1
                for i in range(4):
                    v.wait_ge(mmT.h, qb * 4 + i + 1)
                    v.tensor_copy(vt[:, (qb * 4 + i) * 128:(qb * 4 + i) * 128 + 128],
                                  pst[:]).then_inc(evT.h, 1)
                    evT.v += 1
                v.wait_ge(xsl.h, 16 * (qb + 1))
                for h in range(NH):
                    hh = hbase + h
                    for kt in range(NKT - 4, NKT):
                        a = atc + kt
                        di = tiles[a][4]
                        o = kt - (NKT - 4)
                        v.wait_ge(evE.h, a + 1)
                        v.tensor_mul(pbufs[a % 2][:], pbufs[a % 2][:],
                                     cm[:, o * 512:(o + 1) * 512]).then_inc(evM.h, 1)
                        evM.v += 1
                    atc += NKT
                    v.wait_ge(mmO.h, atc)
                    v.reciprocal(rrow[0:1, :], psden[0:1, :]).then_inc(evR.h, 1)
                    evR.v += 1
                    v.wait_ge(mmR.h, hh + 1)
                    v.wait_ge(evN.h, hh)
                    v.tensor_copy(rbc[:], ps2[:]).then_inc(dvc.h, 1)
                    dvc.v += 1
                    v.wait_ge(dvc.h, dvc.v)
                    v.tensor_mul(t3[:], psO[:], rbc[:]).then_inc(dvc.h, 1)
                    dvc.v += 1
                    v.wait_ge(dvc.h, dvc.v)
                    v.tensor_add(xs[:, h * 512:(h + 1) * 512],
                                 xs[:, h * 512:(h + 1) * 512], t3[:]).then_inc(evN.h, 1)
                    evN.v += 1

        @blk.sync
        def _(s):
            for qb in range(NTB):
                s.wait_ge(evN.h, (qb + 1) * NH)
                s.dma_start(feat_ap(x1p, T, nch=NK, col0=qb * 512, ncols=512), xs[:]
                            ).then_inc(st.h, 16)
                st.v += 16
            s.wait_ge(st.h, st.v)

class Sched:
    ENGINES = ("gpsimd", "tensor", "scalar", "vector", "sync")

    def __init__(self, nc, tag):
        self.nc = nc
        self.tag = tag
        self.ops = {e: [] for e in self.ENGINES}
        self._ctrs = []

    def ctr(self, name):
        c = Ctr(self.nc, f"{self.tag}{name}")
        self._ctrs.append(c)
        return c

    def add(self, eng, fn, waits=(), inc=None, inc_n=1):
        post = None
        if inc is not None:
            inc.v += inc_n
            post = inc.v
        self.ops[eng].append((tuple(waits), fn, inc, inc_n))
        return post

    def add_raw(self, eng, fn, waits=(), inc=None, inc_n=1):
        self.ops[eng].append((tuple(waits), fn, inc, inc_n))

    def emit(self, blk):
        for eng in self.ENGINES:
            if not self.ops[eng]:
                continue
            def body(e, _ops=self.ops[eng]):
                for waits, fn, inc, inc_n in _ops:
                    for c, val in waits:
                        if val > 0:
                            e.wait_ge(c.h, val)
                    ins = fn(e)
                    if inc is not None:
                        ins.then_inc(inc.h, inc_n)
            getattr(blk, eng)(body)


def emit_l1proj(nc, aps, NH=16, T=4096, tag="c1"):
    D = NH * 128
    NK = D // 128
    NTB = T // 512
    R = 4
    xn = aps["xn1"]
    qo, ko, vo, bsg = aps["q"], aps["k"], aps["v"], aps["bsg"]
    f1s, g1s = aps["f1s"], aps["g1s"]
    qw, kw, vw, bw, f1w, g1w = (aps[s] for s in ("qw", "kw", "vw", "bw", "f1w", "g1w"))
    cvw = aps["cvw"]

    chunks = []
    cv = 0
    for c in range(NH):
        chunks.append((qw, D, c, qo, c, "qn", cv)); cv += 1
    for c in range(NH * R):
        chunks.append((kw, NH * R * 128, c, ko, c, "kn", cv)); cv += 1
    for c in range(NH * R):
        chunks.append((vw, NH * R * 128, c, vo, c, "v", cv)); cv += 1
    chunks.append((bw, 64, 0, bsg, 0, "b", None))
    chunks.append((f1w, 128, 0, f1s, 0, "plain", None))
    chunks.append((g1w, 128, 0, g1s, 0, "plain", None))
    NCV = NH + 2 * NH * R

    from contextlib import ExitStack
    es = ExitStack()
    xns = es.enter_context(nc.sbuf_tensor(f"{tag}xns", [128, NK * T], BF16))
    wb = [es.enter_context(nc.sbuf_tensor(f"{tag}wb{j}", [128, D], BF16)) for j in range(2)]
    cbs = [es.enter_context(nc.sbuf_tensor(f"{tag}cb{j}", [128, T + 3], BF16))
           for j in range(2)]
    t4 = es.enter_context(nc.sbuf_tensor(f"{tag}t4", [128, T], F32))
    t5 = es.enter_context(nc.sbuf_tensor(f"{tag}t5", [128, T], BF16))
    ysil = es.enter_context(nc.sbuf_tensor(f"{tag}ysil", [128, T], BF16))
    ob = [es.enter_context(nc.sbuf_tensor(f"{tag}ob{j}", [128, T], BF16)) for j in range(2)]
    nln = es.enter_context(nc.sbuf_tensor(f"{tag}nln", [1, 512], F32))
    nrm = es.enter_context(nc.sbuf_tensor(f"{tag}nrm", [1, 2 * 512], BF16))
    cvs = es.enter_context(nc.sbuf_tensor(f"{tag}cvs", [128, NCV * 4], F32))
    onesc = es.enter_context(nc.sbuf_tensor(f"{tag}onesc", [128, 1], BF16))
    ones1 = es.enter_context(nc.sbuf_tensor(f"{tag}ones1", [1, 128], BF16))
    cbank = es.enter_context(nc.sbuf_tensor(f"{tag}cbank", [128, 4], F32))
    psum = [es.enter_context(nc.psum_tensor(f"{tag}p{j}", [128, 512], F32)) for j in range(4)]
    pn = es.enter_context(nc.psum_tensor(f"{tag}pn", [128, 512], F32))
    pg = [es.enter_context(nc.psum_tensor(f"{tag}pg{j}", [128, 512], F32)) for j in range(2)]
    with es, nc.Block() as blk:
        S = Sched(nc, tag)
        xld, cld = S.ctr("xld"), S.ctr("cld")
        wl = [S.ctr("wl0"), S.ctr("wl1")]
        mm, evc, gcv = S.ctr("mm"), S.ctr("evc"), S.ctr("gcv")
        dvc, acc = S.ctr("dvc"), S.ctr("acc")
        mmn, mnb = S.ctr("mmn"), S.ctr("mnb")
        ost = [S.ctr("ost0"), S.ctr("ost1")]

        S.add("gpsimd", lambda g: g.dma_start(xns[:], feat_ap(xn, T, nch=NK)),
              inc=xld, inc_n=16)
        S.add("gpsimd", lambda g: g.dma_start(
            cvs[:], bass.AP(cvw.tensor, cvw.offset, [[NCV * 4, 128], [1, NCV * 4]])),
            inc=cld, inc_n=16)
        S.add("vector", lambda v: v.memset(onesc[:], 1.0))
        S.add("vector", lambda v: v.memset(cbank[:, 0:1], 1e-6))
        S.add("vector", lambda v: v.memset(cbank[:, 1:2], -0.5 * float(np.log(128.0))))
        S.add("vector", lambda v: v.memset(cbank[:, 2:3], 0.0))
        S.add("vector", lambda v: v.memset(ones1[:], 1.0), inc=dvc)

        st_count = [0, 0]
        cb_free = {}
        t4_free = None
        ysil_free = None
        t5_free = None
        nrm_free = {}
        exp_acc = {}
        pg_free = [None, None]
        deferred_pe = []

        for i, (wap, wcols, wc, oap, oc, kind, cvc) in enumerate(chunks):
            M = 64 if kind == "b" else 128
            par = i % 2
            ncols = 64 if kind == "b" else 128
            wwaits = [(mm, (i - 1) * NTB)] if i >= 2 else []

            def wload(g, wap=wap, wcols=wcols, wc=wc, M=M, par=par):
                dst = bass.AP(wb[par], 0, [[D, 128], [M, NK], [1, M]])
                return g.dma_start(dst,
                                   bass.AP(wap.tensor, wap.offset + wc * 128,
                                           [[wcols, 128], [wcols * 128, NK], [1, M]]))
            S.add("gpsimd", wload, waits=wwaits, inc=wl[par], inc_n=16)

            for tb in range(NTB):
                n = i * NTB + tb
                waits = [(xld, 16), (dvc, 1)] if n == 0 else []
                waits.append((wl[par], 16 * (i // 2 + 1)))
                if n >= 4:
                    waits.append((evc, n - 3))

                def pmm(t, tb=tb, n=n, M=M, par=par):
                    ins = None
                    for kk in range(NK):
                        ins = t.matmul(psum[n % 4][0:M, :],
                                       wb[par][:, kk * M:kk * M + M],
                                       xns[:, kk * T + tb * 512: kk * T + tb * 512 + 512],
                                       start=(kk == 0), stop=(kk == NK - 1))
                    return ins
                S.add("tensor", pmm, waits=waits, inc=mm)
                if tb == 0:
                    for w_, f_, c_ in deferred_pe:
                        S.add_raw("tensor", f_, waits=w_, inc=c_)
                    deferred_pe = []

            cb = cbs[i % 2]
            guards = [(mm, i * NTB + 1), (dvc, dvc.v)]
            if cb_free.get(i % 2) is not None:
                guards.append(cb_free[i % 2])
            S.add("vector", lambda v, M=M, cb=cb: v.memset(cb[0:M, 0:3], 0.0),
                  waits=guards, inc=dvc)
            for tb in range(NTB):
                n = i * NTB + tb
                S.add("vector",
                      lambda v, M=M, tb=tb, n=n, cb=cb: v.tensor_copy(
                          cb[0:M, 3 + tb * 512:3 + (tb + 1) * 512], psum[n % 4][0:M, :]),
                      waits=[(mm, n + 1), (dvc, dvc.v)], inc=evc)

            if kind in ("qn", "kn", "v"):
                vwaits = [(evc, (i + 1) * NTB), (cld, 16), (dvc, dvc.v)]
                if t4_free is not None:
                    vwaits.append((acc, t4_free))
                S.add("vector", lambda v, cvc=cvc, cb=cb: v.tensor_scalar(
                    t4[:], cb[:, 3:T + 3], cvs[:, 4 * cvc + 3:4 * cvc + 4], None, ALU.mult),
                    waits=vwaits, inc=dvc)
                for tap in (2, 1, 0):
                    S.add("vector", lambda v, cvc=cvc, tap=tap, cb=cb: v.scalar_tensor_tensor(
                        t4[:], cb[:, tap:T + tap], cvs[:, 4 * cvc + tap:4 * cvc + tap + 1],
                        t4[:], ALU.mult, ALU.add), waits=[(dvc, dvc.v)], inc=dvc)
                conv_done = dvc.v
                cb_free[i % 2] = (dvc, conv_done)

                if kind == "v":
                    swaits = [(dvc, conv_done)]
                    if st_count[par]:
                        swaits.append((ost[par], 16 * st_count[par]))
                    S.add("scalar", lambda s, par=par: s.activation(
                        ob[par][:], t4[:], AF.Silu),
                        waits=swaits + [(acc, acc.v)], inc=acc)
                    t4_free = acc.v
                    out_ready = (acc, acc.v)
                else:
                    swaits = [(dvc, conv_done)]
                    if ysil_free is not None:
                        swaits.append((dvc, ysil_free))
                    S.add("scalar", lambda s: s.activation(ysil[:], t4[:], AF.Silu),
                          waits=swaits + [(acc, acc.v)], inc=acc)
                    t4_free = acc.v
                    sq_waits = []
                    if t5_free is not None:
                        sq_waits.append((mmn, t5_free))
                    S.add("scalar", lambda s: s.activation(t5[:], ysil[:], AF.Square),
                          waits=sq_waits + [(acc, acc.v)], inc=acc)
                    sq_done = acc.v
                    ln_prev = 0
                    for tb in range(NTB):
                        w_ = [(acc, sq_done)]
                        if ln_prev:
                            w_.append((acc, ln_prev))
                        def smm(t, tb=tb):
                            return t.matmul(pn[0:1, :], onesc[:],
                                            t5[:, tb * 512:(tb + 1) * 512],
                                            start=True, stop=True)
                        deferred_pe.append((w_, smm, mmn))
                        mmn.v += 1
                        mv = mmn.v
                        def lnop(s, tb=tb):
                            return s.activation(nln[0:1, :],
                                                pn[0:1, :], AF.Ln, bias=cbank[0:1, 0:1])
                        S.add("scalar", lnop, waits=[(mmn, mv), (acc, acc.v)], inc=acc)
                        bcol = 1 if kind == "qn" else 2
                        ew = [(acc, acc.v)]
                        if nrm_free.get(tb % 2):
                            ew.append((mnb, nrm_free[tb % 2]))
                        S.add("scalar", lambda s, tb=tb, bcol=bcol: s.activation(
                            nrm[0:1, (tb % 2) * 512:(tb % 2) * 512 + 512],
                            nln[0:1, :], AF.Exp, scale=-0.5,
                            bias=cbank[0:1, bcol:bcol + 1]), waits=ew, inc=acc)
                        exp_acc[tb] = acc.v
                        ln_prev = acc.v
                    t5_free = mmn.v
                    for tb in range(NTB):
                        w_ = [(acc, exp_acc[tb])]
                        if pg_free[tb % 2] is not None:
                            w_.append((dvc, pg_free[tb % 2]))
                        def bmm(t, tb=tb):
                            return t.matmul(pg[tb % 2][:, :], ones1[0:1, :],
                                            nrm[0:1, (tb % 2) * 512:(tb % 2) * 512 + 512],
                                            start=True, stop=True)
                        deferred_pe.append((w_, bmm, mnb))
                        mnb.v += 1
                        nrm_free[tb % 2] = mnb.v
                        bv = mnb.v
                        stw = [(mnb, bv), (dvc, dvc.v)]
                        if st_count[par]:
                            stw.append((ost[par], 16 * st_count[par]))
                        S.add("vector", lambda v, tb=tb, par=par: v.tensor_mul(
                            ob[par][:, tb * 512:(tb + 1) * 512],
                            ysil[:, tb * 512:(tb + 1) * 512], pg[tb % 2][:, :]),
                            waits=stw, inc=dvc)
                        pg_free[tb % 2] = dvc.v
                    ysil_free = dvc.v
                    out_ready = (dvc, dvc.v)
            elif kind == "b":
                swaits = [(evc, (i + 1) * NTB)]
                if st_count[par]:
                    swaits.append((ost[par], 16 * st_count[par]))
                S.add("scalar", lambda s, par=par, cb=cb: s.activation(
                    ob[par][0:64, :], cb[0:64, 0:T], AF.Sigmoid),
                    waits=swaits + [(acc, acc.v)], inc=acc)
                cb_free[i % 2] = (acc, acc.v)
                out_ready = (acc, acc.v)
            else:
                pwaits = [(evc, (i + 1) * NTB), (dvc, dvc.v)]
                if st_count[par]:
                    pwaits.append((ost[par], 16 * st_count[par]))
                S.add("vector", lambda v, par=par, cb=cb: v.tensor_copy(
                    ob[par][:, 0:T], cb[:, 0:T]), waits=pwaits, inc=dvc)
                cb_free[i % 2] = (dvc, dvc.v)
                out_ready = (dvc, dvc.v)

            rows = 64 if kind == "b" else 128

            def store(sy, oap=oap, oc=oc, par=par, rows=rows):
                dst = bass.AP(oap.tensor, oap.offset + oc * 128 * T, [[T, rows], [1, T]])
                return sy.dma_start(dst, ob[par][0:rows, 0:T])
            S.add("sync", store, waits=[out_ready], inc=ost[par], inc_n=16)
            st_count[par] += 1

        for w_, f_, c_ in deferred_pe:
            S.add_raw("tensor", f_, waits=w_, inc=c_)
        S.add("sync", lambda sy: sy.nop(),
              waits=[(ost[0], 16 * st_count[0]), (ost[1], 16 * st_count[1])])
        S.emit(blk)

def emit_scan(nc, aps, NH=16, T=4096, NHORN=5, tag="s2", max_stage=99, ncc_cap=None,
              sub_stage=99):
    H, R, D = NH, 4, 128
    NT = T * R
    C = 128
    CT = C // R
    NCC = NT // C
    SCG = 4
    TSC = CT * SCG
    qai, kai, vai, gdi, bsgi = (aps[s] for s in ("q", "k", "v", "gd", "bsg"))
    gsig, ldf, mlog, osc = aps["gsig"], aps["ldf"], aps["mlog"], aps["osc"]
    wmxs = aps["wmxs"]
    msku = aps["masku"]
    pall = aps["pall"]
    i4t = aps["i4t"]
    onw = aps["onw"]

    from contextlib import ExitStack
    es = ExitStack()
    sb = lambda n, sh, dt: es.enter_context(nc.sbuf_tensor(f"{tag}{n}", sh, dt))
    ps = lambda n, sh, dt: es.enter_context(nc.psum_tensor(f"{tag}{n}", sh, dt))
    kslab = sb("kslab", [128, 64 * TSC], BF16)
    vslab = sb("vslab", [128, 64 * TSC], BF16)
    qslab = sb("qslab", [128, 16 * TSC], BF16)
    gdslab = sb("gdslab", [128, 16 * (TSC // 2)], F32)
    Ssb = sb("S", [128, 2048], F32)
    Sbf = sb("Sbf", [128, 2048], BF16)
    Gf = sb("Gf", [128, 2048], F32)
    Ge = sb("Ge", [128, 16 * 129], F32)
    dA = sb("dA", [128, 2048], F32)
    kscn = sb("kscn", [128, 2048], BF16)
    vscn = sb("vscn", [128, 2048], BF16)
    EL16 = sb("EL16", [128, 2048], BF16)
    ELI = sb("ELI", [128, 2048], BF16)
    f_kl16 = sb("fkl16", [128, 2048], BF16)
    f_kl32 = sb("fkl32", [128, 2048], BF16)
    f_kl64 = sb("fkl64", [128, 2048], BF16)
    f_ks = sb("fks", [128, 2048], BF16)
    f_kli = sb("fkli", [128, 2048], BF16)
    f_kr16 = sb("fkr16", [128, 2048], BF16)
    f_kr32 = sb("fkr32", [128, 2048], BF16)
    f_kr64 = sb("fkr64", [128, 2048], BF16)
    f_kg = sb("fkg", [128, 2048], BF16)
    f_ql16 = sb("fql16", [128, 2048], BF16)
    f_ql32 = sb("fql32", [128, 2048], BF16)
    f_ql64 = sb("fql64", [128, 2048], BF16)
    f_qs = sb("fqs", [128, 2048], BF16)
    pc16 = sb("pc16", [128, 128], F32)
    pcr = sb("pcr", [128, 7 * 128], F32)
    TTm = sb("TT", [128, 2048], BF16)
    BTm = sb("BT", [128, 2048], BF16)
    Vtok = sb("Vtok", [128, 2048], BF16)
    KGt = sb("KGt", [128, 2048], BF16)
    Rsb = sb("R", [128, 2048], BF16)
    Xsb = sb("X", [128, 2048], BF16)
    Osb = sb("Osb", [128, 2048], BF16)
    o1 = sb("o1", [32, 2048], BF16)
    rr16 = sb("rr16", [32, 16], F32)
    ofeat = sb("ofeat", [128, 512], BF16)
    onesf = sb("onesf", [128, 1], F32)
    cbank = sb("cbank", [128, 4], F32)
    gsl = sb("gsl", [128, 512], BF16)
    bscn = sb("bscn", [16, 128], BF16)
    bsrt = sb("bsrt", [16, 128], BF16)
    msksu = sb("msksu", [128, 5 * 2048], mybir.dt.uint8)
    onehs = sb("onehs", [16, 2048], BF16)
    palls = sb("palls", [128, 32], BF16)
    i4ts = sb("i4ts", [4, 128], BF16)
    mixm = sb("mixm", [128, 16 * 32], BF16)
    wcol = sb("wcol", [128, 16], F32)
    mls = sb("mls", [16, 4], F32)
    wmx = sb("wmx", [16, 4], F32)
    wmxT = sb("wmxT", [4, 16], BF16)
    ldfs = sb("ldfs", [128, 16], F32)
    onwc = sb("onwc", [128, 1], F32)
    ident = sb("ident", [128, 128], BF16)
    ECt = sb("ECt", [128, 16], F32)
    trow = sb("trow", [16, 8], F32)
    psGA = ps("psGA", [128, 2048], F32)
    psT = ps("psT", [128, 2048], BF16)
    psM = ps("psM", [32, 1024], F32)

    with es, nc.Block() as blk:
        S = Sched(nc, tag)
        dvc = S.ctr("dvc")
        gpc = S.ctr("gpc")
        acc = S.ctr("acc")
        pec = S.ctr("pec")
        dld = S.ctr("dld")
        cl0 = S.ctr("cl0")
        bld = S.ctr("bld")
        gld = S.ctr("gld")
        wst = S.ctr("wst")
        wst2 = S.ctr("wst2")
        ost = S.ctr("ost")

        def dve(fn, waits=(), tail=None):
            w = list(waits) + [(dvc, dvc.v)]
            S.add("vector", fn, waits=w, inc=tail or dvc)

        def gpo(fn, waits=(), tail=None):
            w = list(waits) + [(gpc, gpc.v)]
            S.add("gpsimd", fn, waits=w, inc=tail or gpc)

        def act(fn, waits=(), tail=None):
            w = list(waits) + [(acc, acc.v)]
            S.add("scalar", fn, waits=w, inc=tail or acc)

        S.add("sync", lambda s: s.dma_start(msksu[:], bass.AP(
            msku.tensor, msku.offset, [[2048, 128], [2048 * 128, 5], [1, 2048]])),
            inc=cl0, inc_n=16)
        S.add("sync", lambda s: s.dma_start(onehs[:], bass.AP(
            aps["oneh16"].tensor, aps["oneh16"].offset, [[2048, 16], [1, 2048]])),
            inc=cl0, inc_n=16)
        S.add("sync", lambda s: s.dma_start(palls[:], bass.AP(
            pall.tensor, pall.offset, [[32, 128], [1, 32]])), inc=cl0, inc_n=16)
        S.add("sync", lambda s: s.dma_start(i4ts[:], bass.AP(
            i4t.tensor, i4t.offset, [[128, 4], [1, 128]])), inc=cl0, inc_n=16)
        S.add("sync", lambda s: s.dma_start(ldfs[:], bass.AP(
            ldf.tensor, ldf.offset, [[16, 128], [1, 16]])), inc=cl0, inc_n=16)
        S.add("sync", lambda s: s.dma_start(mls[:], bass.AP(
            mlog.tensor, mlog.offset, [[4, 16], [1, 4]])), inc=cl0, inc_n=16)
        S.add("sync", lambda s: s.dma_start(onwc[:], bass.AP(
            onw.tensor, onw.offset, [[1, 128], [1, 1]])), inc=cl0, inc_n=16)
        S.add("sync", lambda s: s.dma_start(ident[:], bass.AP(
            aps["ident"].tensor, aps["ident"].offset, [[128, 128], [1, 128]])),
            inc=cl0, inc_n=16)
        CL = cl0.v

        dve(lambda v: v.memset(Ssb[:], 0.0), waits=[])
        dve(lambda v: v.memset(Sbf[:], 0.0))
        dve(lambda v: v.memset(Ge[:], 0.0))
        dve(lambda v: v.tensor_copy(
            bass.AP(Gf, 1, [[2048, 128], [128, 16], [4, 32], [1, 3]]),
            bass.AP(ldfs, 0, [[16, 128], [1, 16], [0, 32], [0, 3]])),
            waits=[(cl0, CL)])
        dve(lambda v: v.tensor_reduce(trow[:, 0:1], mls[:], AX.X, ALU.max,
                                      negate=True), waits=[(cl0, CL)])
        act(lambda s: s.activation(wmx[:], mls[:], AF.Exp, bias=trow[:, 0:1]),
            waits=[(dvc, dvc.v)])
        dve(lambda v: v.tensor_reduce(trow[:, 1:2], wmx[:], AX.X, ALU.add),
            waits=[(acc, acc.v)])
        dve(lambda v: v.reciprocal(trow[:, 2:3], trow[:, 1:2]))
        dve(lambda v: v.tensor_scalar(wmx[:], wmx[:], trow[:, 2:3], None, ALU.mult))
        S.add("sync", lambda s: s.dma_start(
            bass.AP(wmxs.tensor, wmxs.offset, [[4, 16], [1, 4]]), wmx[:]),
            waits=[(dvc, dvc.v)], inc=wst, inc_n=16)
        S.add("gpsimd", lambda g: g.dma_start(
            wmxT[:], bass.AP(wmxs.tensor, wmxs.offset, [[1, 4], [4, 16]]),
            allow_slow_non_contiguous=True),
            waits=[(wst, 16)], inc=wst2, inc_n=16)
        S.add("tensor", lambda t: t.matmul(psGA[:, 0:16], i4ts[:], wmxT[:],
                                           start=True, stop=True),
              waits=[(wst2, 16), (cl0, CL)], inc=pec)
        dve(lambda v: v.tensor_copy(wcol[:], psGA[:, 0:16]), waits=[(pec, pec.v)])
        dve(lambda v: v.tensor_mul(
            mixm[:], bass.AP(palls, 0, [[32, 128], [0, 16], [1, 32]]),
            bass.AP(wcol, 0, [[16, 128], [1, 16], [0, 32]])))
        dve(lambda v: v.memset(onesf[:], 1.0))
        dve(lambda v: v.memset(cbank[:, 0:1], 1e-5))
        mix_ready = dvc.v

        prev_s_done = 0
        prev_gf = 0
        NCCr = min(NCC, ncc_cap) if ncc_cap else NCC
        for cc in range(NCCr):
            sg, tc = cc // SCG, cc % SCG
            if tc == 0:
                w = [(dvc, prev_s_done)] if cc > 0 else [(dvc, mix_ready)]
                for part in range(4):
                    S.add("sync", lambda s, sg=sg, part=part: s.dma_start(
                        bass.AP(kslab, part * 16 * TSC,
                                [[64 * TSC, 128], [TSC, 16], [1, TSC]]),
                        bass.AP(kai.tensor, kai.offset + sg * TSC + part * 16 * 128 * T,
                                [[T, 128], [128 * T, 16], [1, TSC]])),
                        waits=(w if part == 0 else ()), inc=dld, inc_n=16)
                    S.add("sync", lambda s, sg=sg, part=part: s.dma_start(
                        bass.AP(vslab, part * 16 * TSC,
                                [[64 * TSC, 128], [TSC, 16], [1, TSC]]),
                        bass.AP(vai.tensor, vai.offset + sg * TSC + part * 16 * 128 * T,
                                [[T, 128], [128 * T, 16], [1, TSC]])),
                        inc=dld, inc_n=16)
                for part in range(2):
                    S.add("sync", lambda s, sg=sg, part=part: s.dma_start(
                        bass.AP(qslab, part * 8 * TSC,
                                [[16 * TSC, 128], [TSC, 8], [1, TSC]]),
                        bass.AP(qai.tensor, qai.offset + sg * TSC + part * 8 * 128 * T,
                                [[T, 128], [128 * T, 8], [1, TSC]])),
                        inc=dld, inc_n=16)

            if cc % 2 == 0:
                wgd = [(dvc, prev_gf)] if cc > 0 else []
                for part in range(2):
                    S.add("sync", lambda s, cc=cc, part=part: s.dma_start(
                        bass.AP(gdslab, part * 8 * (TSC // 2),
                                [[16 * (TSC // 2), 128], [TSC // 2, 8], [1, TSC // 2]]),
                        bass.AP(gdi.tensor, gdi.offset + cc * CT + part * 8 * 128 * T,
                                [[T, 128], [128 * T, 8], [1, TSC // 2]])),
                        waits=(wgd if part == 0 else ()), inc=dld, inc_n=16)
            SL = dld.v
            S.add("sync", lambda g, cc=cc: g.dma_start(bsrt[:], bass.AP(
                bsgi.tensor, bsgi.offset + cc * CT,
                [[4 * T, 16], [T, 4], [1, CT]])),
                waits=[(dvc, dvc.v)], inc=bld, inc_n=16)
            BL = bld.v
            for part in range(4):
                S.add("sync", lambda g, cc=cc, part=part: g.dma_start(
                    bass.AP(gsl, part * 4 * CT, [[16 * CT, 128], [CT, 4], [1, CT]]),
                    bass.AP(gsig.tensor, gsig.offset + cc * CT + part * 4 * 128 * T,
                            [[T, 128], [128 * T, 4], [1, CT]])),
                    waits=([(dvc, dvc.v)] if part == 0 else ()), inc=gld, inc_n=16)
            GL = gld.v

            kview = bass.AP(kslab, tc * CT,
                            [[64 * TSC, 128], [4 * TSC, 16], [1, CT], [TSC, 4]])
            vview = bass.AP(vslab, tc * CT,
                            [[64 * TSC, 128], [4 * TSC, 16], [1, CT], [TSC, 4]])
            qview = bass.AP(qslab, tc * CT,
                            [[16 * TSC, 128], [TSC, 16], [1, CT], [0, 4]])
            gdview = bass.AP(gdslab, (cc % 2) * CT,
                             [[16 * (TSC // 2), 128], [TSC // 2, 16], [1, CT]])

            dve(lambda v, kview=kview: v.tensor_copy(kscn[:], kview),
                waits=[(dld, SL)])
            dve(lambda v, vview=vview: v.tensor_copy(vscn[:], vview),
                waits=[(dld, SL)])
            dve(lambda v, gdview=gdview: v.tensor_copy(
                bass.AP(Gf, 0, [[2048, 128], [128, 16], [4, 32]]), gdview),
                waits=[(dld, SL)])
            dve(lambda v: v.tensor_copy(
                bscn[:], bass.AP(bsrt, 0, [[128, 16], [1, 32], [32, 4]])),
                waits=[(bld, BL)])
            gf_done = dvc.v
            prev_gf = gf_done

            if max_stage < 2:
                prev_s_done = dvc.v
                continue
            ones_col = bass.AP(onesf, 0, [[1, 128], [0, 128]])
            for h in range(NH):
                fn = (lambda e, h=h: e.tensor_tensor_scan(
                    Ge[:, h * 129 + 1:h * 129 + 129],
                    Gf[:, h * 128:(h + 1) * 128], ones_col,
                    0.0, ALU.add, ALU.mult))
                dve(fn, waits=[(dvc, gf_done)])
            scans_d, scans_g = dvc.v, gpc.v

            GV = bass.AP(Ge, 1, [[2064, 128], [129, 16], [1, 128]])
            PV16 = bass.AP(Ge, 0, [[2064, 128], [129, 16], [16, 8], [0, 16]])
            gE16 = bass.AP(Ge, 16, [[2064, 128], [129, 16], [16, 8]])
            gP16 = bass.AP(Ge, 0, [[2064, 128], [129, 16], [16, 8]])
            gE32 = bass.AP(Ge, 32, [[2064, 128], [129, 16], [32, 4], [0, 2]])
            gP32 = bass.AP(Ge, 0, [[2064, 128], [129, 16], [32, 4], [0, 2]])
            gE64 = bass.AP(Ge, 64, [[2064, 128], [129, 16], [64, 2], [0, 4]])
            gP64 = bass.AP(Ge, 0, [[2064, 128], [129, 16], [64, 2], [0, 4]])
            gEC = bass.AP(Ge, 128, [[2064, 128], [129, 16], [0, 8]])

            if max_stage < 3:
                prev_s_done = dvc.v
                continue
            dve(lambda v: v.tensor_sub(dA[:], GV, PV16), waits=[(dvc, scans_d)])
            dve(lambda v: v.tensor_scalar_max(dA[:], dA[:], -80.0))
            d16_done = dvc.v
            act(lambda s: s.activation(EL16[:], dA[:], AF.Exp), waits=[(dvc, d16_done)])
            act(lambda s: s.activation(ELI[:], dA[:], AF.Exp, scale=-1.0))
            e16_done = acc.v
            prs = [(gE16, gP16), (gE32, gP16), (gE64, gP16),
                   (gP16, gP32), (gP16, gP64), (gEC, gP16), (gP16, None)]
            for ii, (a_, b_) in enumerate(prs):
                if b_ is None:
                    dve(lambda v, ii=ii, a_=a_: v.tensor_copy(
                        pcr[:, ii * 128:ii * 128 + 128], a_))
                else:
                    dve(lambda v, ii=ii, a_=a_, b_=b_: v.tensor_sub(
                        pcr[:, ii * 128:ii * 128 + 128], a_, b_))
            pr_done = dvc.v
            act(lambda s: s.activation(pcr[:], pcr[:], AF.Exp),
                waits=[(dvc, pr_done)])
            pr_exp = acc.v

            def RVi(ii):
                return bass.AP(pcr, ii * 128, [[7 * 128, 128], [8, 16], [1, 8], [0, 16]])

            if max_stage < 4:
                prev_s_done = dvc.v
                continue
            dve(lambda v: v.tensor_mul(f_kl16[:], kscn[:], EL16[:]),
                waits=[(acc, e16_done)])
            gpo(lambda g, qview=qview: g.tensor_mul(f_ql16[:], qview, EL16[:]),
                waits=[(acc, e16_done), (dld, SL)])
            for h in range(NH):
                S.add("tensor", lambda t, h=h: t.matmul(
                    psGA[:, h * 128:(h + 1) * 128], onehs[:, h * 128:(h + 1) * 128],
                    bscn[:], start=True, stop=True),
                    waits=([(dvc, gf_done), (cl0, CL), (pec, pec.v)] if h == 0 else []),
                    inc=(pec if h == NH - 1 else None))
            bb_mm = pec.v
            dve(lambda v: v.scalar_tensor_tensor(f_kli[:], psGA[:], 1.0, kscn[:],
                                                 ALU.mult, ALU.mult),
                waits=[(pec, bb_mm)])
            dve(lambda v: v.tensor_mul(f_kli[:], f_kli[:], ELI[:]))
            kli_done = dvc.v
            dve(lambda v: v.tensor_mul(f_kr16[:], f_kli[:], RVi(0)),
                waits=[(acc, pr_exp)])
            dve(lambda v: v.tensor_mul(f_kr32[:], f_kli[:], RVi(1)))
            dve(lambda v: v.tensor_mul(f_kr64[:], f_kli[:], RVi(2)))
            dve(lambda v: v.tensor_mul(f_kg[:], f_kli[:], RVi(5)))
            gpo(lambda g: g.tensor_mul(f_kl32[:], f_kl16[:], RVi(3)),
                waits=[(acc, pr_exp), (dvc, dvc.v)])
            gpo(lambda g: g.tensor_mul(f_kl64[:], f_kl16[:], RVi(4)))
            gpo(lambda g: g.tensor_mul(f_ql32[:], f_ql16[:], RVi(3)))
            gpo(lambda g: g.tensor_mul(f_ql64[:], f_ql16[:], RVi(4)))
            gpo(lambda g: g.tensor_mul(f_ks[:], f_kl16[:], RVi(6)))
            gpo(lambda g: g.tensor_mul(f_qs[:], f_ql16[:], RVi(6)))
            fields_d, fields_g = dvc.v, gpc.v

            if max_stage < 5:
                prev_s_done = dvc.v
                continue
            pieces = [(f_kli, f_kl16, 0, TTm), (f_kr16, f_kl16, 2, TTm),
                      (f_kr32, f_kl32, 3, TTm), (f_kr64, f_kl64, 4, TTm),
                      (f_kli, f_ql16, 1, BTm), (f_kr16, f_ql16, 2, BTm),
                      (f_kr32, f_ql32, 3, BTm), (f_kr64, f_ql64, 4, BTm)]
            pv_half = {0: None, 1: None}
            for pi, (lf, rf, mi, dst) in enumerate(pieces):
                mview = msksu[:, mi * 2048:(mi + 1) * 2048]
                if pi == 0 or pi == 4:
                    dve(lambda v, dst=dst: v.memset(dst[:], 0.0))
                    md = dvc.v
                else:
                    md = None
                for half in range(2):
                    for hh2 in range(8):
                        h = half * 8 + hh2
                        w = []
                        if h == 0:
                            w = [(dvc, fields_d), (gpc, fields_g)]
                            if cc > 0 and pi == 0:
                                w.append((pec, pec.v))
                        if hh2 == 0 and pv_half[half] is not None:
                            w.append((dvc, pv_half[half]))
                        S.add("tensor", lambda t, h=h, lf=lf, rf=rf: t.matmul(
                            psGA[:, h * 128:(h + 1) * 128],
                            lf[:, h * 128:(h + 1) * 128], rf[:, h * 128:(h + 1) * 128],
                            start=True, stop=True), waits=w,
                            inc=(pec if hh2 == 7 else None))
                    pmm = pec.v
                    mw = [(pec, pmm)]
                    if md is not None:
                        mw.append((dvc, md))
                    dve(lambda v, dst=dst, mview=mview, half=half: v.copy_predicated(
                        dst[:, half * 1024:(half + 1) * 1024],
                        mview[:, half * 1024:(half + 1) * 1024],
                        psGA[:, half * 1024:(half + 1) * 1024]), waits=mw)
                    pv_half[half] = dvc.v
            tb_done = dvc.v

            if max_stage < 6:
                prev_s_done = dvc.v
                continue
            for h in range(NH):
                w = [(pec, pec.v), (dvc, tb_done)] if h == 0 else []
                S.add("tensor", lambda t, h=h: t.transpose(
                    psT[:, h * 128:(h + 1) * 128],
                    vscn[:, h * 128:(h + 1) * 128],
                    ident[:]), waits=w, inc=(pec if h == NH - 1 else None))
            vt_mm = pec.v
            dve(lambda v: v.tensor_copy(Vtok[:], psT[:]), waits=[(pec, vt_mm)])
            vt_done = dvc.v
            for h in range(NH):
                w = [(dvc, vt_done)] if h == 0 else []
                S.add("tensor", lambda t, h=h: t.transpose(
                    psT[:, h * 128:(h + 1) * 128], f_kg[:, h * 128:(h + 1) * 128],
                    ident[:]), waits=w, inc=(pec if h == NH - 1 else None))
            kg_mm = pec.v
            dve(lambda v: v.tensor_copy(KGt[:], psT[:]), waits=[(pec, kg_mm)])
            kg_done = dvc.v

            if max_stage < 7:
                prev_s_done = dvc.v
                continue
            for h in range(NH):
                w = [(dvc, kg_done)] if h == 0 else []
                S.add("tensor", lambda t, h=h: t.matmul(
                    psGA[:, h * 128:(h + 1) * 128],
                    f_ks[:, h * 128:(h + 1) * 128], Sbf[:, h * 128:(h + 1) * 128],
                    start=True, stop=True),
                    waits=w, inc=(pec if h == NH - 1 else None))
            c_mm = pec.v
            dve(lambda v: v.scalar_tensor_tensor(
                Rsb[:], psGA[:], -1.0, Vtok[:], ALU.mult, ALU.add), waits=[(pec, c_mm)])
            dve(lambda v: v.tensor_copy(Xsb[:], Rsb[:]))
            r_done = dvc.v

            if max_stage < 8:
                prev_s_done = dvc.v
                continue
            xh = {0: r_done, 1: r_done}
            for it in range(NHORN):
                for half in range(2):
                    for hh2 in range(8):
                        h = half * 8 + hh2
                        w = [(dvc, xh[half])] if hh2 == 0 else []
                        S.add("tensor", lambda t, h=h: t.matmul(
                            psGA[:, h * 128:(h + 1) * 128],
                            TTm[:, h * 128:(h + 1) * 128],
                            Xsb[:, h * 128:(h + 1) * 128], start=True, stop=True),
                            waits=w, inc=(pec if hh2 == 7 else None))
                    hm = pec.v
                    dve(lambda v, half=half: v.scalar_tensor_tensor(
                        Xsb[:, half * 1024:(half + 1) * 1024],
                        psGA[:, half * 1024:(half + 1) * 1024], -1.0,
                        Rsb[:, half * 1024:(half + 1) * 1024], ALU.mult, ALU.add),
                        waits=[(pec, hm)])
                    xh[half] = dvc.v
            xprev = dvc.v

            if max_stage < 9:
                prev_s_done = dvc.v
                continue
            for h in range(NH):
                w = [(dvc, xprev)] if h == 0 else []
                S.add("tensor", lambda t, h=h: (
                    t.matmul(psGA[:, h * 128:(h + 1) * 128],
                             f_qs[:, h * 128:(h + 1) * 128],
                             Sbf[:, h * 128:(h + 1) * 128],
                             start=True, stop=False),
                    t.matmul(psGA[:, h * 128:(h + 1) * 128],
                             BTm[:, h * 128:(h + 1) * 128],
                             Xsb[:, h * 128:(h + 1) * 128],
                             start=False, stop=True))[-1],
                    waits=w, inc=(pec if h == NH - 1 else None))
            o_mm = pec.v
            dve(lambda v: v.tensor_copy(Osb[:], psGA[:]),
                waits=[(pec, o_mm)])
            o_done = dvc.v

            if max_stage < 10:
                prev_s_done = dvc.v
                continue
            for h in range(NH):
                w = [(dvc, o_done)] if h == 0 else []
                S.add("tensor", lambda t, h=h: t.matmul(
                    psGA[:, h * 128:(h + 1) * 128],
                    KGt[:, h * 128:(h + 1) * 128],
                    Xsb[:, h * 128:(h + 1) * 128], start=True, stop=True),
                    waits=w, inc=(pec if h == NH - 1 else None))
            su_mm = pec.v
            ECexp = bass.AP(Ge, 128, [[2064, 128], [129, 16], [0, 128]])
            gpo(lambda g: g.tensor_copy(pc16[:, 0:16],
                                        bass.AP(Ge, 128, [[2064, 128], [129, 16]])),
                waits=[(dvc, o_done)])
            S.add("scalar", lambda s: s.activation(ECt[:], pc16[:, 0:16], AF.Exp),
                  waits=[(gpc, gpc.v), (acc, acc.v)], inc=acc)
            ec_done = acc.v
            gpo(lambda g: g.tensor_mul(
                Ssb[:], Ssb[:], bass.AP(ECt, 0, [[16, 128], [1, 16], [0, 128]])),
                waits=[(acc, ec_done)])
            sdec = gpc.v
            dve(lambda v: v.tensor_add(Ssb[:], Ssb[:], psGA[:]),
                waits=[(gpc, sdec), (pec, su_mm)])
            dve(lambda v: v.tensor_copy(Sbf[:], Ssb[:]))
            s_done = dvc.v

            if max_stage < 11:
                prev_s_done = dvc.v
                continue
            for h in range(NH):
                w = [(dvc, o_done)] if h == 0 else []
                if h == 8 and sub_stage >= 1:
                    w = [(acc, sq_a), (dvc, o1a)]
                S.add("tensor", lambda t, h=h: t.matmul(
                    psM[:, (h % 8) * 128:(h % 8) * 128 + 128],
                    mixm[:, h * 32:(h + 1) * 32],
                    Osb[:, h * 128:(h + 1) * 128], start=True, stop=True),
                    waits=w, inc=(pec if h % 8 == 7 else None))
                if h == 7 and sub_stage >= 1:
                    m1 = pec.v
                    dve(lambda v: v.tensor_copy(o1[:, 0:1024], psM[:]),
                        waits=[(pec, m1)])
                    o1a = dvc.v
                    act(lambda s: s.activation(dA[0:32, 0:1024], o1[:, 0:1024],
                                               AF.Square), waits=[(dvc, o1a)])
                    sq_a = acc.v
                if h == NH - 1 and sub_stage < 1:
                    prev_s_done = dvc.v
            if sub_stage < 1:
                continue
            m2 = pec.v
            dve(lambda v: v.tensor_copy(o1[:, 1024:2048], psM[:]),
                waits=[(pec, m2)])
            o1_done = dvc.v
            act(lambda s: s.activation(dA[0:32, 1024:2048], o1[:, 1024:2048],
                                       AF.Square), waits=[(dvc, o1_done)])
            sq_b = acc.v
            if sub_stage < 2:
                prev_s_done = dvc.v
                continue
            dve(lambda v: v.tensor_reduce(
                rr16[:, 0:16], bass.AP(dA, 0, [[2048, 32], [128, 16], [1, 128]]),
                AX.X, ALU.add), waits=[(acc, sq_b)])
            act(lambda s: s.activation(rr16[:, 0:16], rr16[:, 0:16], AF.Sqrt,
                                       scale=1.0 / 128, bias=cbank[0:32, 0:1]),
                waits=[(dvc, dvc.v)])
            dve(lambda v: v.reciprocal(rr16[:, 0:16], rr16[:, 0:16]),
                waits=[(acc, acc.v)])
            dve(lambda v: v.tensor_mul(
                o1[:], o1[:], bass.AP(rr16, 0, [[16, 32], [1, 16], [0, 128]])))
            on_done = dvc.v
            if sub_stage < 3:
                prev_s_done = dvc.v
                continue
            for h in range(NH):
                w = [(dvc, on_done)] if h == 0 else []
                S.add("tensor", lambda t, h=h: t.transpose(
                    psT[:, h * 32:(h + 1) * 32], o1[0:32, h * 128:(h + 1) * 128],
                    ident[0:32, 0:32]), waits=w, inc=(pec if h == NH - 1 else None))
            ot_mm = pec.v
            dve(lambda v: v.scalar_tensor_tensor(
                ofeat[:], psT[:, 0:512], onwc[:, 0:1], gsl[:], ALU.mult, ALU.mult),
                waits=[(pec, ot_mm), (gld, GL), (ost, ost.v)])
            of_done = dvc.v
            if sub_stage < 4:
                prev_s_done = dvc.v
                continue
            S.add("sync", lambda s, cc=cc: s.dma_start(
                bass.AP(osc.tensor, osc.offset + cc * CT,
                        [[T, 128], [128 * T, 16], [1, CT]]), ofeat[:]),
                waits=[(dvc, of_done)], inc=ost, inc_n=16)
            prev_s_done = s_done

        S.add("sync", lambda s: s.nop(), waits=[(ost, ost.v)])
        S.emit(blk)

def emit_l1bg(nc, aps, NH=16, T=4096, tag="cb"):
    D = NH * 128
    NTB = T // 512
    f1s, g1s, f2wT, g2wT = (aps[s] for s in ("f1s", "g1s", "f2wT", "g2wT"))
    dtb, alog, g2b, gd, gsig, ldf = (aps[s] for s in
                                     ("dtb", "alog", "g2b", "gd", "gsig", "ldf"))
    from contextlib import ExitStack
    es = ExitStack()
    sb = lambda n, sh, dt: es.enter_context(nc.sbuf_tensor(f"{tag}{n}", sh, dt))
    ps = lambda n, sh, dt: es.enter_context(nc.psum_tensor(f"{tag}{n}", sh, dt))
    f1o = sb("f1o", [128, T], BF16)
    g1o = sb("g1o", [128, T], BF16)
    f2s = sb("f2s", [128, D], BF16)
    g2s = sb("g2s", [128, D], BF16)
    dtbs = sb("dtbs", [128, NH], F32)
    g2bs = sb("g2bs", [128, NH], F32)
    abc = sb("abc", [128, NH], F32)
    arow = sb("arow", [1, NH], F32)
    cb = sb("cbk", [128, 2], F32)
    hb = sb("hb", [128, 512], F32)
    eb = sb("eb", [128, 512], F32)
    rb = sb("rb", [128, 512], F32)
    gdb = sb("gdb", [128, T], F32)
    gsb = sb("gsb", [128, T], BF16)
    lf1 = sb("lf1", [128, NH], F32)
    lf2 = sb("lf2", [128, NH], F32)
    ones1 = sb("ones1", [1, 128], F32)
    psum = [ps(f"p{j}", [128, 512], F32) for j in range(4)]
    with es, nc.Block() as blk:
        S = Sched(nc, tag)
        ld = S.ctr("ld")
        mm = S.ctr("mm")
        dvc = S.ctr("dvc")
        acc = S.ctr("acc")
        pec = S.ctr("pec")
        ost = S.ctr("ost")
        ostg = S.ctr("ostg")
        osts = S.ctr("osts")

        S.add("gpsimd", lambda g: g.dma_start(f1o[:], bass.AP(
            f1s.tensor, f1s.offset, [[T, 128], [1, T]])), inc=ld, inc_n=16)
        S.add("gpsimd", lambda g: g.dma_start(g1o[:], bass.AP(
            g1s.tensor, g1s.offset, [[T, 128], [1, T]])), inc=ld, inc_n=16)
        S.add("gpsimd", lambda g: g.dma_start(f2s[:], bass.AP(
            f2wT.tensor, f2wT.offset, [[D, 128], [1, D]])), inc=ld, inc_n=16)
        S.add("gpsimd", lambda g: g.dma_start(g2s[:], bass.AP(
            g2wT.tensor, g2wT.offset, [[D, 128], [1, D]])), inc=ld, inc_n=16)
        S.add("gpsimd", lambda g: g.dma_start(dtbs[:], bass.AP(
            dtb.tensor, dtb.offset, [[NH, 128], [1, NH]])), inc=ld, inc_n=16)
        S.add("gpsimd", lambda g: g.dma_start(g2bs[:], bass.AP(
            g2b.tensor, g2b.offset, [[NH, 128], [1, NH]])), inc=ld, inc_n=16)
        S.add("gpsimd", lambda g: g.dma_start(arow[:], bass.AP(
            alog.tensor, alog.offset, [[NH, 1], [1, NH]])), inc=ld, inc_n=16)
        LD = ld.v
        S.add("vector", lambda v: v.memset(ones1[:], 1.0), inc=dvc)
        S.add("vector", lambda v: v.memset(cb[:, 0:1], 1.0), waits=[(dvc, 1)], inc=dvc)
        S.add("vector", lambda v: v.memset(cb[:, 1:2], -10000.0),
              waits=[(dvc, 2)], inc=dvc)
        S.add("scalar", lambda s: s.activation(arow[:], arow[:], AF.Exp),
              waits=[(ld, LD)], inc=acc)
        S.add("tensor", lambda t: t.matmul(psum[0][:, 0:NH], ones1[:], arow[:],
                                           start=True, stop=True),
              waits=[(acc, 1), (dvc, 3)], inc=pec)
        S.add("vector", lambda v: v.tensor_copy(abc[:], psum[0][:, 0:NH]),
              waits=[(pec, 1), (dvc, 3)], inc=dvc)
        ab_done = dvc.v

        relu_acc = {}
        sig_acc = {}

        def dve(fn, waits=()):
            S.add("vector", fn, waits=list(waits) + [(dvc, dvc.v)], inc=dvc)

        def act(fn, waits=()):
            S.add("scalar", fn, waits=list(waits) + [(acc, acc.v)], inc=acc)

        act(lambda s: s.activation(lf1[:], dtbs[:], AF.Abs, bias=cb[:, 1:2]),
            waits=[(ld, LD), (dvc, ab_done)])
        act(lambda s: s.activation(lf1[:], lf1[:], AF.Exp, scale=-1.0))
        act(lambda s: s.activation(lf1[:], lf1[:], AF.Ln, bias=cb[:, 0:1]))
        act(lambda s: s.activation(lf2[:], dtbs[:], AF.Relu, bias=cb[:, 1:2]))
        lfa = acc.v
        dve(lambda v: v.tensor_add(lf1[:], lf1[:], lf2[:]), waits=[(acc, lfa)])
        dve(lambda v: v.scalar_tensor_tensor(lf1[:], lf1[:], -1.0, abc[:],
                                             ALU.mult, ALU.mult))
        lf_done = dvc.v
        S.add("sync", lambda s: s.dma_start(bass.AP(
            ldf.tensor, ldf.offset, [[NH, 128], [1, NH]]), lf1[:]),
            waits=[(dvc, lf_done)], inc=ost, inc_n=16)

        for c in range(NH):
            for tb in range(NTB):
                n = c * NTB + tb
                w = [(ld, LD), (dvc, ab_done)] if n == 0 else []
                if n >= 2:
                    w.append((acc, relu_acc[n - 2]))
                S.add("tensor", lambda t, c=c, tb=tb, n=n: t.matmul(
                    psum[n % 2][:], f2s[:, c * 128:(c + 1) * 128],
                    f1o[:, tb * 512:(tb + 1) * 512], start=True, stop=True),
                    waits=w, inc=pec)
                pm = pec.v
                act(lambda s, c=c, tb=tb, n=n: s.activation(
                    hb[:], psum[n % 2][:], AF.Abs, bias=dtbs[:, c:c + 1]),
                    waits=[(pec, pm)] + ([(dvc, dvc.v)] if True else []))
                act(lambda s: s.activation(eb[:], hb[:], AF.Exp, scale=-1.0))
                act(lambda s: s.activation(eb[:], eb[:], AF.Ln, bias=cb[:, 0:1]))
                act(lambda s, c=c, n=n: s.activation(
                    rb[:], psum[n % 2][:], AF.Relu, bias=dtbs[:, c:c + 1]))
                relu_acc[n] = acc.v
                aa = acc.v
                dve(lambda v: v.tensor_add(rb[:], rb[:], eb[:]), waits=[(acc, aa)])
                dve(lambda v, c=c, tb=tb: v.scalar_tensor_tensor(
                    gdb[:, tb * 512:(tb + 1) * 512], rb[:], -1.0,
                    bass.AP(abc, c, [[NH, 128], [0, 512]]), ALU.mult, ALU.mult))
            gdc = dvc.v
            S.add("sync", lambda s, c=c: s.dma_start(
                bass.AP(gd.tensor, gd.offset + c * 128 * T, [[T, 128], [1, T]]),
                gdb[:]), waits=[(dvc, gdc)], inc=ostg, inc_n=16)
            gd_st = ostg.v
            for tb in range(NTB):
                n = c * NTB + tb
                S.add("tensor", lambda t, c=c, tb=tb, n=n: t.matmul(
                    psum[2 + n % 2][:], g2s[:, c * 128:(c + 1) * 128],
                    g1o[:, tb * 512:(tb + 1) * 512], start=True, stop=True),
                    waits=([(acc, sig_acc[n - 2])] if n >= 2 else []), inc=pec)
                pm = pec.v
                act(lambda s, c=c, tb=tb, n=n: s.activation(
                    gsb[:, tb * 512:(tb + 1) * 512], psum[2 + n % 2][:],
                    AF.Sigmoid, bias=g2bs[:, c:c + 1]),
                    waits=[(pec, pm)] + ([(osts, gs_st)] if c > 0 and tb == 0 else []))
                sig_acc[n] = acc.v
            gsc = acc.v
            S.add("sync", lambda s, c=c: s.dma_start(
                bass.AP(gsig.tensor, gsig.offset + c * 128 * T, [[T, 128], [1, T]]),
                gsb[:]), waits=[(acc, gsc)], inc=osts, inc_n=16)
            gs_st = osts.v
            dve(lambda v: v.engine_nop(), waits=[(ostg, gd_st)])
        S.add("sync", lambda s: s.nop(),
              waits=[(ost, ost.v), (ostg, ostg.v), (osts, osts.v)])
        S.emit(blk)

def host_prep(ins, T=4096, NH=16):
    D = NH * 128
    p = {}
    ct = lambda a: np.ascontiguousarray(a)
    perm = np.concatenate([np.arange(0, 128, 2), np.arange(1, 128, 2)])
    Wq, Wk, Wv = ins["L0_attn_Wq"], ins["L0_attn_Wk"], ins["L0_attn_Wv"]
    p["wqT"] = ct(np.concatenate([Wq[:, h * 128 + perm] for h in range(NH)], axis=1))
    p["wkT"] = ct(Wk[:, perm])
    p["wvT"] = ct(Wv)
    inv = 1.0 / (10000.0 ** (np.arange(0, 128, 2, dtype=np.float32) / 128))
    f = np.outer(np.arange(T, dtype=np.float32), inv)
    p["cosk"] = ct(np.cos(f).T.astype(np.float32))
    p["sink"] = ct(np.sin(f).T.astype(np.float32))
    sc = np.float32(1.0 / np.sqrt(128.0))
    p["cosq"] = ct(p["cosk"] * sc)
    p["sinq"] = ct(p["sink"] * sc)
    cm = np.zeros((128, 4 * 512), dtype=np.float32)
    for o in range(4):
        for kk in range(128):
            cm[kk, o * 512 + o * 128 + kk:(o + 1) * 512] = 1.0
    p["cmask"] = cm.astype(BFNP)
    p["ident"] = np.eye(128, dtype=np.float32).astype(BFNP)
    for L in (0, 1):
        p[f"gwT{L}"] = ct(ins[f"L{L}_ffn_gate_w"].T)
        p[f"uwT{L}"] = ct(ins[f"L{L}_ffn_up_w"].T)
        p[f"dwT{L}"] = ct(ins[f"L{L}_ffn_down_w"].T)
        p[f"gs{L}"] = ct(ins[f"L{L}_ffn_gate_s"].reshape(-1, 128).T)
        p[f"us{L}"] = ct(ins[f"L{L}_ffn_up_s"].reshape(-1, 128).T)
        p[f"ds{L}"] = ct(ins[f"L{L}_ffn_down_s"].reshape(-1, 128).T)
    p["qw"] = ct(ins["L1_attn_q_proj_w"].T)
    p["kw"] = ct(ins["L1_attn_k_proj_w"].T)
    p["vw"] = ct(ins["L1_attn_v_proj_w"].T)
    p["bw"] = ct(ins["L1_attn_b_proj_w"].T)
    p["f1w"] = ct(ins["L1_attn_f1_w"].T)
    p["g1w"] = ct(ins["L1_attn_g1_w"].T)
    cv = np.concatenate([ins["L1_attn_q_conv_w"], ins["L1_attn_k_conv_w"],
                         ins["L1_attn_v_conv_w"]], axis=0)
    ncv = cv.shape[0] // 128
    p["cvw"] = ct(cv.reshape(ncv, 128, 4).transpose(1, 0, 2).reshape(128, ncv * 4))
    p["f2wT"] = ct(ins["L1_attn_f2_w"].T)
    p["g2wT"] = ct(ins["L1_attn_g2_w"].T)
    p["dtb"] = ct(ins["L1_attn_dt_bias"].reshape(NH, 128).T)
    p["g2b"] = ct(ins["L1_attn_g2_b"].reshape(NH, 128).T)
    p["alog"] = ct(ins["L1_attn_A_log"].reshape(1, NH))
    p["mlog"] = ct(ins["L1_attn_micro_logits"])
    p["onw"] = ct(ins["L1_attn_o_norm_w"].reshape(128, 1))
    p["opT"] = ct(ins["L1_attn_o_proj_w"].T)
    p["dsone"] = np.ones((128, NH), dtype=np.float32)
    mk = np.zeros((5, 128, 128), np.float32)
    mi = np.arange(128)[:, None]
    mj = np.arange(128)[None, :]
    mk[0] = (mi < mj) & (mi // 16 == mj // 16)
    mk[1] = (mi <= mj) & (mi // 16 == mj // 16)
    mk[2] = (mi // 16 + 1 == mj // 16) & (mi // 32 == mj // 32)
    mk[3] = (mi // 32 + 1 == mj // 32) & (mi // 64 == mj // 64)
    mk[4] = (mi // 64 == 0) & (mj // 64 == 1)
    p["masku"] = np.tile(mk, (1, 1, NH)).reshape(5 * 128, NH * 128).astype(np.uint8)
    oneh = np.zeros((NH, NH * 128), np.float32)
    for h in range(NH):
        oneh[h, h * 128:(h + 1) * 128] = 1.0
    p["oneh16"] = oneh.astype(BFNP)
    pall = np.zeros((128, 32), np.float32)
    pall[np.arange(128), np.arange(128) // 4] = 1.0
    p["pall"] = pall.astype(BFNP)
    i4 = np.zeros((4, 128), np.float32)
    i4[np.arange(128) % 4, np.arange(128)] = 1.0
    p["i4t"] = i4.astype(BFNP)
    return p


def _d(nc, name, arr_shape, dt, kind):
    return nc.dram_tensor(name, list(arr_shape), dt, kind=kind).ap()


def _recycled(f, nc, *args, **kw):
    mark = len(CTR_LOG)
    f(nc, *args, **kw)
    nc.clear_and_free_semaphores([c.h for c in CTR_LOG[mark:]])


def build_A(T, NH=16):
    D = NH * 128
    nc = bass.Bass("TRN2", target_bir_lowering=False)
    aps = {
        "x": _d(nc, "x", (D, T), F32, "ExternalInput"),
        "xn": _d(nc, "xn0", (D, T), BF16, "ExternalOutput"),
        "wqT": _d(nc, "wqT", (D, NH * 128), F32, "ExternalInput"),
        "wkT": _d(nc, "wkT", (D, 128), F32, "ExternalInput"),
        "wvT": _d(nc, "wvT", (D, 128), F32, "ExternalInput"),
        "cosq": _d(nc, "cosq", (64, T), F32, "ExternalInput"),
        "sinq": _d(nc, "sinq", (64, T), F32, "ExternalInput"),
        "cosk": _d(nc, "cosk", (64, T), F32, "ExternalInput"),
        "sink": _d(nc, "sink", (64, T), F32, "ExternalInput"),
        "cmask": _d(nc, "cmask", (128, 2048), BF16, "ExternalInput"),
        "ident": _d(nc, "ident", (128, 128), BF16, "ExternalInput"),
        "x1p": _d(nc, "x1p", (D, T), F32, "ExternalOutput"),
    }
    emit_ln(nc, {"x": aps["x"], "xn": aps["xn"]}, D=D, T=T, tag="ln0")
    _recycled(emit_attn0, nc, aps, NH=NH, T=T, tag="a0")
    return nc


def build_ffn(T, L, D=2048, I=5632):
    nc = bass.Bass("TRN2", target_bir_lowering=False)
    aps = {
        "xn": _d(nc, "xn", (D, T), BF16, "ExternalInput"),
        "resid": _d(nc, "resid", (D, T), F32, "ExternalInput"),
        "gwT": _d(nc, "gwT", (D, I), F32, "ExternalInput"),
        "uwT": _d(nc, "uwT", (D, I), F32, "ExternalInput"),
        "dwT": _d(nc, "dwT", (I, D), F32, "ExternalInput"),
        "gs": _d(nc, "gs", (128, I // 128), F32, "ExternalInput"),
        "us": _d(nc, "us", (128, I // 128), F32, "ExternalInput"),
        "ds": _d(nc, "ds", (128, D // 128), F32, "ExternalInput"),
        "hscr": _d(nc, "hscr", (I, T), BF16, "Internal"),
        "out": _d(nc, "out", (D, T), F32, "ExternalOutput"),
    }
    _recycled(emit_ffn, nc, aps, D=D, I=I, T=T, NQ=4, tag="f0")
    return nc


def build_C1(T, NH=16):
    D = NH * 128
    nc = bass.Bass("TRN2", target_bir_lowering=False)
    aps = {
        "x": _d(nc, "x1", (D, T), F32, "ExternalInput"),
        "xn": _d(nc, "xn1", (D, T), BF16, "ExternalOutput"),
        "qw": _d(nc, "qw", (D, D), F32, "ExternalInput"),
        "kw": _d(nc, "kw", (D, 4 * D), F32, "ExternalInput"),
        "vw": _d(nc, "vw", (D, 4 * D), F32, "ExternalInput"),
        "bw": _d(nc, "bw", (D, 64), F32, "ExternalInput"),
        "f1w": _d(nc, "f1w", (D, 128), F32, "ExternalInput"),
        "g1w": _d(nc, "g1w", (D, 128), F32, "ExternalInput"),
        "cvw": _d(nc, "cvw", (128, (NH + 8 * NH) * 4), F32, "ExternalInput"),
        "f2wT": _d(nc, "f2wT", (128, D), F32, "ExternalInput"),
        "g2wT": _d(nc, "g2wT", (128, D), F32, "ExternalInput"),
        "dtb": _d(nc, "dtb", (128, NH), F32, "ExternalInput"),
        "g2b": _d(nc, "g2b", (128, NH), F32, "ExternalInput"),
        "alog": _d(nc, "alog", (1, NH), F32, "ExternalInput"),
        "q": _d(nc, "q", (D, T), BF16, "ExternalOutput"),
        "k": _d(nc, "k", (4 * D, T), BF16, "ExternalOutput"),
        "v": _d(nc, "v", (4 * D, T), BF16, "ExternalOutput"),
        "bsg": _d(nc, "bsg", (64, T), BF16, "ExternalOutput"),
        "f1s": _d(nc, "f1s", (128, T), BF16, "Internal"),
        "g1s": _d(nc, "g1s", (128, T), BF16, "Internal"),
        "gd": _d(nc, "gd", (D, T), F32, "ExternalOutput"),
        "gsig": _d(nc, "gsig", (D, T), BF16, "ExternalOutput"),
        "ldf": _d(nc, "ldf", (128, NH), F32, "ExternalOutput"),
    }
    emit_ln(nc, {"x": aps["x"], "xn": aps["xn"]}, D=D, T=T, tag="ln1")
    aps["xn1"] = aps["xn"]
    _recycled(emit_l1proj, nc, aps, NH=NH, T=T, tag="c1")
    _recycled(emit_l1bg, nc, aps, NH=NH, T=T, tag="cb")
    return nc


def build_C2(T, NH=16):
    D = NH * 128
    nc = bass.Bass("TRN2", target_bir_lowering=False)
    aps = {
        "q": _d(nc, "q", (D, T), BF16, "ExternalInput"),
        "k": _d(nc, "k", (4 * D, T), BF16, "ExternalInput"),
        "v": _d(nc, "v", (4 * D, T), BF16, "ExternalInput"),
        "gd": _d(nc, "gd", (D, T), F32, "ExternalInput"),
        "bsg": _d(nc, "bsg", (64, T), BF16, "ExternalInput"),
        "gsig": _d(nc, "gsig", (D, T), BF16, "ExternalInput"),
        "ldf": _d(nc, "ldf", (128, NH), F32, "ExternalInput"),
        "mlog": _d(nc, "mlog", (NH, 4), F32, "ExternalInput"),
        "onw": _d(nc, "onw", (128, 1), F32, "ExternalInput"),
        "masku": _d(nc, "masku", (5 * 128, NH * 128), mybir.dt.uint8, "ExternalInput"),
        "oneh16": _d(nc, "oneh16", (NH, NH * 128), BF16, "ExternalInput"),
        "pall": _d(nc, "pall", (128, 32), BF16, "ExternalInput"),
        "i4t": _d(nc, "i4t", (4, 128), BF16, "ExternalInput"),
        "ident": _d(nc, "ident", (128, 128), BF16, "ExternalInput"),
        "wmxs": _d(nc, "wmxs", (NH, 4), F32, "Internal"),
        "osc": _d(nc, "osc", (D, T), BF16, "ExternalOutput"),
    }
    _recycled(emit_scan, nc, aps, NH=NH, T=T, tag="s2")
    return nc


def build_C3(T, D=2048):
    nc = bass.Bass("TRN2", target_bir_lowering=False)
    aps = {
        "resid": _d(nc, "resid", (D, T), F32, "ExternalInput"),
        "dwT": _d(nc, "dwT", (D, D), F32, "ExternalInput"),
        "ds": _d(nc, "ds", (128, D // 128), F32, "ExternalInput"),
        "hscr": _d(nc, "hscr", (D, T), BF16, "ExternalInput"),
        "out": _d(nc, "out", (D, T), F32, "ExternalOutput"),
    }
    _recycled(emit_ffn, nc, aps, D=D, I=D, T=T, NQ=2, tag="c3", down_only=True)
    return nc


def run_pipeline(inputs, runner, T=4096, NH=16, ncores=NCORES):
    D = NH * 128
    prep = host_prep(inputs, T=T, NH=NH)
    B = inputs["L0_attn_x"].shape[0]
    xs = [np.ascontiguousarray(inputs["L0_attn_x"][b].T.astype(np.float32))
          for b in range(B)]

    ncA = build_A(T, NH)
    mapsA = [dict(x=xs[b], wqT=prep["wqT"], wkT=prep["wkT"], wvT=prep["wvT"],
                  cosq=prep["cosq"], sinq=prep["sinq"], cosk=prep["cosk"],
                  sink=prep["sink"], cmask=prep["cmask"], ident=prep["ident"])
             for b in range(B)]
    resA = runner(ncA, mapsA, ["xn0", "x1p"])

    ncB = build_ffn(T, 0)
    mapsB = [dict(xn=resA[b]["xn0"], resid=resA[b]["x1p"], gwT=prep["gwT0"],
                  uwT=prep["uwT0"], dwT=prep["dwT0"], gs=prep["gs0"],
                  us=prep["us0"], ds=prep["ds0"]) for b in range(B)]
    resB = runner(ncB, mapsB, ["out"])

    ncC1 = build_C1(T, NH)
    mapsC1 = [dict(x1=resB[b]["out"], qw=prep["qw"], kw=prep["kw"], vw=prep["vw"],
                   bw=prep["bw"], f1w=prep["f1w"], g1w=prep["g1w"], cvw=prep["cvw"],
                   f2wT=prep["f2wT"], g2wT=prep["g2wT"], dtb=prep["dtb"],
                   g2b=prep["g2b"], alog=prep["alog"]) for b in range(B)]
    resC1 = runner(ncC1, mapsC1, ["xn1", "q", "k", "v", "bsg", "gd", "gsig", "ldf"])

    ncC2 = build_C2(T, NH)
    mapsC2 = [dict(q=resC1[b]["q"], k=resC1[b]["k"], v=resC1[b]["v"],
                   gd=resC1[b]["gd"], bsg=resC1[b]["bsg"], gsig=resC1[b]["gsig"],
                   ldf=resC1[b]["ldf"], mlog=prep["mlog"], onw=prep["onw"],
                   masku=prep["masku"], oneh16=prep["oneh16"],
                   pall=prep["pall"], i4t=prep["i4t"],
                   ident=prep["ident"]) for b in range(B)]
    resC2 = runner(ncC2, mapsC2, ["osc"])

    ncC3 = build_C3(T)
    mapsC3 = [dict(resid=resB[b]["out"], dwT=prep["opT"], ds=prep["dsone"],
                   hscr=resC2[b]["osc"]) for b in range(B)]
    resC3 = runner(ncC3, mapsC3, ["out"])

    ncB1 = build_ffn(T, 1)
    mapsB1 = [dict(xn=resC1[b]["xn1"], resid=resC3[b]["out"], gwT=prep["gwT1"],
                   uwT=prep["uwT1"], dwT=prep["dwT1"], gs=prep["gs1"],
                   us=prep["us1"], ds=prep["ds1"]) for b in range(B)]
    resB1 = runner(ncB1, mapsB1, ["out"])

    out = np.stack([resB1[b]["out"].T for b in range(B)], axis=0)
    return np.ascontiguousarray(out.astype(np.float32))


def kernel_multilaunch(**inputs):
    def runner(nc, in_maps, outnames):
        res = run_bass_kernel_spmd(nc, in_maps, core_ids=list(range(len(in_maps))))
        return res.results
    return run_pipeline(inputs, runner, T=4096, NH=16)


def kernel(**inputs):
    return kernel_fused(**inputs)

def build_full(T=4096, NH=16, I=5632):
    D = NH * 128
    nc = bass.Bass("TRN2", target_bir_lowering=False)
    EI, EO, IN = "ExternalInput", "ExternalOutput", "Internal"
    a = {}
    a["x"] = _d(nc, "x", (D, T), F32, EI)
    for n, sh in (("wqT", (D, D)), ("wkT", (D, 128)), ("wvT", (D, 128)),
                  ("cosq", (64, T)), ("sinq", (64, T)), ("cosk", (64, T)),
                  ("sink", (64, T)),
                  ("gwT0", (D, I)), ("uwT0", (D, I)), ("dwT0", (I, D)),
                  ("gs0", (128, I // 128)), ("us0", (128, I // 128)), ("ds0", (128, D // 128)),
                  ("gwT1", (D, I)), ("uwT1", (D, I)), ("dwT1", (I, D)),
                  ("gs1", (128, I // 128)), ("us1", (128, I // 128)), ("ds1", (128, D // 128)),
                  ("qw", (D, D)), ("kw", (D, 4 * D)), ("vw", (D, 4 * D)),
                  ("bw", (D, 64)), ("f1w", (D, 128)), ("g1w", (D, 128)),
                  ("cvw", (128, (NH + 8 * NH) * 4)), ("f2wT", (128, D)),
                  ("g2wT", (128, D)), ("dtb", (128, NH)), ("g2b", (128, NH)),
                  ("alog", (1, NH)), ("mlog", (NH, 4)), ("onw", (128, 1)),
                  ("opT", (D, D)), ("dsone", (128, NH))):
        a[n] = _d(nc, n, sh, F32, EI)
    for n, sh, dt in (("cmask", (128, 2048), BF16), ("ident", (128, 128), BF16),
                      ("masku", (5 * 128, NH * 128), mybir.dt.uint8),
                      ("oneh16", (NH, NH * 128), BF16),
                      ("pall", (128, 32), BF16), ("i4t", (4, 128), BF16)):
        a[n] = _d(nc, n, sh, dt, EI)
    a["xn0"] = _d(nc, "xn0", (D, T), BF16, IN)
    a["x1p"] = _d(nc, "x1p", (D, T), F32, IN)
    a["x1"] = _d(nc, "x1", (D, T), F32, IN)
    a["h0"] = _d(nc, "h0", (I, T), BF16, IN)
    a["xn1"] = _d(nc, "xn1", (D, T), BF16, IN)
    a["q"] = _d(nc, "q", (D, T), BF16, IN)
    a["k"] = _d(nc, "k", (4 * D, T), BF16, IN)
    a["v"] = _d(nc, "v", (4 * D, T), BF16, IN)
    a["bsg"] = _d(nc, "bsg", (64, T), BF16, IN)
    a["f1s"] = _d(nc, "f1s", (128, T), BF16, IN)
    a["g1s"] = _d(nc, "g1s", (128, T), BF16, IN)
    a["gd"] = _d(nc, "gd", (D, T), F32, IN)
    a["gsig"] = _d(nc, "gsig", (D, T), BF16, IN)
    a["ldf"] = _d(nc, "ldf", (128, NH), F32, IN)
    a["wmxs"] = _d(nc, "wmxs", (NH, 4), F32, IN)
    a["osc"] = _d(nc, "osc", (D, T), BF16, IN)
    a["x2p"] = _d(nc, "x2p", (D, T), F32, IN)
    a["h1"] = _d(nc, "h1", (I, T), BF16, IN)
    a["out"] = _d(nc, "out", (D, T), F32, EO)

    _recycled(emit_ln, nc, {"x": a["x"], "xn": a["xn0"]}, D=D, T=T, tag="ln0")
    _recycled(emit_attn0, nc, {"x": a["x"], "xn": a["xn0"], "wqT": a["wqT"], "wkT": a["wkT"],
                    "wvT": a["wvT"], "cosq": a["cosq"], "sinq": a["sinq"],
                    "cosk": a["cosk"], "sink": a["sink"], "cmask": a["cmask"],
                    "ident": a["ident"], "x1p": a["x1p"]}, NH=NH, T=T, tag="a0")
    _recycled(emit_ffn, nc, {"xn": a["xn0"], "resid": a["x1p"], "gwT": a["gwT0"],
                  "uwT": a["uwT0"], "dwT": a["dwT0"], "gs": a["gs0"],
                  "us": a["us0"], "ds": a["ds0"], "hscr": a["h0"],
                  "out": a["x1"]}, D=D, I=I, T=T, NQ=4, tag="f0")
    _recycled(emit_ln, nc, {"x": a["x1"], "xn": a["xn1"]}, D=D, T=T, tag="ln1")
    _recycled(emit_l1proj, nc, {"xn1": a["xn1"], "q": a["q"], "k": a["k"], "v": a["v"],
                     "bsg": a["bsg"], "f1s": a["f1s"], "g1s": a["g1s"],
                     "qw": a["qw"], "kw": a["kw"], "vw": a["vw"], "bw": a["bw"],
                     "f1w": a["f1w"], "g1w": a["g1w"], "cvw": a["cvw"]},
                NH=NH, T=T, tag="c1")
    _recycled(emit_l1bg, nc, {"f1s": a["f1s"], "g1s": a["g1s"], "f2wT": a["f2wT"],
                   "g2wT": a["g2wT"], "dtb": a["dtb"], "alog": a["alog"],
                   "g2b": a["g2b"], "gd": a["gd"], "gsig": a["gsig"],
                   "ldf": a["ldf"]}, NH=NH, T=T, tag="cb")
    _recycled(emit_scan, nc, {"q": a["q"], "k": a["k"], "v": a["v"], "gd": a["gd"],
                   "bsg": a["bsg"], "gsig": a["gsig"], "ldf": a["ldf"],
                   "mlog": a["mlog"], "onw": a["onw"], "masku": a["masku"],
                   "oneh16": a["oneh16"], "pall": a["pall"], "i4t": a["i4t"],
                   "ident": a["ident"], "wmxs": a["wmxs"], "osc": a["osc"]},
              NH=NH, T=T, tag="s2")
    _recycled(emit_ffn, nc, {"resid": a["x1"], "dwT": a["opT"], "ds": a["dsone"],
                  "hscr": a["osc"], "out": a["x2p"]}, D=D, I=D, T=T, NQ=2,
             tag="c3", down_only=True)
    _recycled(emit_ffn, nc, {"xn": a["xn1"], "resid": a["x2p"], "gwT": a["gwT1"],
                  "uwT": a["uwT1"], "dwT": a["dwT1"], "gs": a["gs1"],
                  "us": a["us1"], "ds": a["ds1"], "hscr": a["h1"],
                  "out": a["out"]}, D=D, I=I, T=T, NQ=4, tag="f1")
    return nc


def kernel_fused(**inputs):
    T, NH = 4096, 16
    prep = host_prep(inputs, T=T, NH=NH)
    B = inputs["L0_attn_x"].shape[0]
    xs = [np.ascontiguousarray(inputs["L0_attn_x"][b].T.astype(np.float32))
          for b in range(B)]
    nc = build_full(T=T, NH=NH)
    names = ("wqT wkT wvT cosq sinq cosk sink cmask ident gwT0 uwT0 dwT0 gs0 us0 "
             "ds0 gwT1 uwT1 dwT1 gs1 us1 ds1 qw kw vw bw f1w g1w cvw f2wT g2wT "
             "dtb g2b alog mlog onw opT dsone masku oneh16 pall i4t").split()
    base = {n: prep[n] for n in names}
    in_maps = [dict(base, x=xs[b]) for b in range(B)]
    res = run_bass_kernel_spmd(nc, in_maps, core_ids=list(range(B)))
    out = np.stack([res.results[b]["out"].T for b in range(B)], axis=0)
    return np.ascontiguousarray(out.astype(np.float32))


# version 11
# speedup vs baseline: 1.1043x; 1.0187x over previous
import numpy as np
import ml_dtypes
import concourse.bass as bass
import concourse.mybir as mybir
from concourse.bass_utils import run_bass_kernel_spmd

F32 = mybir.dt.float32
BF16 = mybir.dt.bfloat16
AF = mybir.ActivationFunctionType
ALU = mybir.AluOpType
AX = mybir.AxisListType
BFNP = ml_dtypes.bfloat16

NCORES = 8


CTR_LOG = []


class Ctr:

    def __init__(self, nc, name):
        self.h = nc.alloc_semaphore(name)
        self.v = 0
        CTR_LOG.append(self)


def feat_ap(dram_ap, T, chunk0=0, nch=1, col0=0, ncols=None):
    if ncols is None:
        ncols = T
    return bass.AP(dram_ap.tensor, dram_ap.offset + chunk0 * 128 * T + col0,
                   [[T, 128], [T * 128, nch], [1, ncols]])


def emit_ffn(nc, aps, D=2048, I=5632, T=4096, NQ=4, tag="f0", down_only=False):
    NK = D // 128
    NI = I // 128
    NTB = T // 512
    ND = D // 128
    NDq = ND // NQ

    resid, dwT = aps["resid"], aps["dwT"]
    ds, hscr, out = aps["ds"], aps["hscr"], aps["out"]
    if down_only:
        _emit_ffn_b2(nc, resid, dwT, ds, hscr, out, D, I, T, NQ, tag)
        return
    xn, gwT, uwT = aps["xn"], aps["gwT"], aps["uwT"]
    gs, us = aps["gs"], aps["us"]

    with (
        nc.sbuf_tensor(f"{tag}xns", [128, NK * T], BF16) as xns,
        nc.sbuf_tensor(f"{tag}gw0", [128, D], BF16) as gw0,
        nc.sbuf_tensor(f"{tag}gw1", [128, D], BF16) as gw1,
        nc.sbuf_tensor(f"{tag}uw0", [128, D], BF16) as uw0,
        nc.sbuf_tensor(f"{tag}uw1", [128, D], BF16) as uw1,
        nc.sbuf_tensor(f"{tag}hb0", [128, T], BF16) as hb0,
        nc.sbuf_tensor(f"{tag}hb1", [128, T], BF16) as hb1,
        nc.sbuf_tensor(f"{tag}gss", [128, NI], F32) as gss,
        nc.sbuf_tensor(f"{tag}uss", [128, NI], F32) as uss,
        nc.sbuf_tensor(f"{tag}sil0", [128, 512], F32) as sil0,
        nc.sbuf_tensor(f"{tag}sil1", [128, 512], F32) as sil1,
        nc.psum_tensor(f"{tag}p0", [128, 512], F32) as p0,
        nc.psum_tensor(f"{tag}p1", [128, 512], F32) as p1,
        nc.psum_tensor(f"{tag}p2", [128, 512], F32) as p2,
        nc.psum_tensor(f"{tag}p3", [128, 512], F32) as p3,
        nc.Block() as blk,
    ):
        wbufs = [(gw0, uw0), (gw1, uw1)]
        hbufs = [hb0, hb1]
        sils = [sil0, sil1]
        psum = [p0, p1, p2, p3]
        ld = [Ctr(nc, f"{tag}ld0"), Ctr(nc, f"{tag}ld1")]
        xld = Ctr(nc, f"{tag}xld")
        mm = Ctr(nc, f"{tag}mm")
        evA = Ctr(nc, f"{tag}evA")
        evD = Ctr(nc, f"{tag}evD")
        hst = [Ctr(nc, f"{tag}hst0"), Ctr(nc, f"{tag}hst1")]

        @blk.gpsimd
        def _(g):
            g.dma_start(xns[:], feat_ap(xn, T, nch=NK)).then_inc(xld.h, 16)
            g.dma_start(gss[:], bass.AP(gs.tensor, gs.offset, [[NI, 128], [1, NI]])
                        ).then_inc(xld.h, 16)
            g.dma_start(uss[:], bass.AP(us.tensor, us.offset, [[NI, 128], [1, NI]])
                        ).then_inc(xld.h, 16)
            xld.v = 48
            for ic in range(NI):
                if ic >= 2:
                    g.wait_ge(mm.h, (ic - 1) * NTB * 2)
                gw, uw = wbufs[ic % 2]
                g.dma_start(gw[:], feat_ap(gwT, I, col0=ic * 128, ncols=128, nch=NK)
                            ).then_inc(ld[ic % 2].h, 16)
                g.dma_start(uw[:], feat_ap(uwT, I, col0=ic * 128, ncols=128, nch=NK)
                            ).then_inc(ld[ic % 2].h, 16)
                ld[ic % 2].v += 32

        @blk.tensor
        def _(t):
            t.wait_ge(xld.h, xld.v)
            for ic in range(NI):
                gw, uw = wbufs[ic % 2]
                t.wait_ge(ld[ic % 2].h, 32 * (ic // 2 + 1))
                for tb in range(NTB):
                    j = ic * NTB + tb
                    for half, w in ((0, gw), (1, uw)):
                        n = 2 * j + half
                        bank = psum[n % 4]
                        if n >= 4:
                            if half == 0:
                                t.wait_ge(evA.h, (n - 4) // 2 + 1)
                            else:
                                t.wait_ge(evD.h, (n - 5) // 2 + 1)
                        ins = None
                        for k in range(NK):
                            ins = t.matmul(bank[:],
                                           w[:, k * 128:(k + 1) * 128],
                                           xns[:, k * T + tb * 512: k * T + tb * 512 + 512],
                                           start=(k == 0), stop=(k == NK - 1))
                        ins.then_inc(mm.h, 1)
            mm.v = 2 * NI * NTB

        @blk.scalar
        def _(s):
            s.wait_ge(xld.h, 48)
            for ic in range(NI):
                for tb in range(NTB):
                    j = ic * NTB + tb
                    if j >= 2:
                        s.wait_ge(evD.h, j - 1)
                    s.wait_ge(mm.h, 2 * j + 1)
                    s.activation(sils[j % 2][:], psum[(2 * j) % 4][:], AF.Silu,
                                 scale=gss[:, ic:ic + 1]).then_inc(evA.h, 1)
            evA.v = NI * NTB

        @blk.vector
        def _(v):
            for ic in range(NI):
                hbuf = hbufs[ic % 2]
                if ic >= 2:
                    v.wait_ge(hst[ic % 2].h, 16 * (ic // 2))
                for tb in range(NTB):
                    j = ic * NTB + tb
                    v.wait_ge(mm.h, 2 * j + 2)
                    v.wait_ge(evA.h, j + 1)
                    v.scalar_tensor_tensor(hbuf[:, tb * 512:(tb + 1) * 512],
                                           psum[(2 * j + 1) % 4][:], uss[:, ic:ic + 1],
                                           sils[j % 2][:], ALU.mult, ALU.mult
                                           ).then_inc(evD.h, 1)
            evD.v = NI * NTB

        @blk.sync
        def _(s):
            for ic in range(NI):
                s.wait_ge(evD.h, NTB * (ic + 1))
                s.dma_start(hscr[ic * 128:(ic + 1) * 128, :], hbufs[ic % 2][:]
                            ).then_inc(hst[ic % 2].h, 16)
                hst[ic % 2].v += 16
            s.wait_ge(hst[0].h, hst[0].v)
            s.wait_ge(hst[1].h, hst[1].v)

    _emit_ffn_b2(nc, resid, dwT, ds, hscr, out, D, I, T, NQ, tag)


def _emit_ffn_b2(nc, resid, dwT, ds, hscr, out, D, I, T, NQ, tag):
    NK = D // 128
    NI = I // 128
    NTB = T // 512
    ND = D // 128
    NDq = ND // NQ
    with (
        nc.sbuf_tensor(f"{tag}dw", [128, NI * NDq * 128], BF16) as dw,
        nc.sbuf_tensor(f"{tag}hs0", [128, NI * 512], BF16) as hs0,
        nc.sbuf_tensor(f"{tag}hs1", [128, NI * 512], BF16) as hs1,
        nc.sbuf_tensor(f"{tag}rs0", [128, NDq * 512], F32) as rs0,
        nc.sbuf_tensor(f"{tag}rs1", [128, NDq * 512], F32) as rs1,
        nc.sbuf_tensor(f"{tag}os0", [128, NDq * 512], F32) as os0,
        nc.sbuf_tensor(f"{tag}os1", [128, NDq * 512], F32) as os1,
        nc.sbuf_tensor(f"{tag}dss", [128, ND], F32) as dss,
        nc.psum_tensor(f"{tag}q0", [128, 512], F32) as q0,
        nc.psum_tensor(f"{tag}q1", [128, 512], F32) as q1,
        nc.psum_tensor(f"{tag}q2", [128, 512], F32) as q2,
        nc.psum_tensor(f"{tag}q3", [128, 512], F32) as q3,
        nc.Block() as blk,
    ):
        psum = [q0, q1, q2, q3]
        hbufs = [hs0, hs1]
        rbufs = [rs0, rs1]
        obufs = [os0, os1]
        wld = Ctr(nc, f"{tag}wld")
        hld = [Ctr(nc, f"{tag}hld0"), Ctr(nc, f"{tag}hld1")]
        rld = [Ctr(nc, f"{tag}rld0"), Ctr(nc, f"{tag}rld1")]
        mm2 = Ctr(nc, f"{tag}mm2")
        ev2 = Ctr(nc, f"{tag}ev2")
        ost = [Ctr(nc, f"{tag}ost0"), Ctr(nc, f"{tag}ost1")]

        @blk.gpsimd
        def _(g):
            g.dma_start(dss[:], bass.AP(ds.tensor, ds.offset, [[ND, 128], [1, ND]])
                        ).then_inc(wld.h, 16)
            wld.v += 16
            nb = 0
            for q in range(NQ):
                src = bass.AP(dwT.tensor, dwT.offset + q * NDq * 128,
                              [[D, 128], [D * 128, NI], [128, NDq], [1, 128]])
                dst = bass.AP(dw, 0, [[NI * NDq * 128, 128], [NDq * 128, NI],
                                      [128, NDq], [1, 128]])
                if q >= 1:
                    g.wait_ge(mm2.h, q * NTB * NDq)
                g.dma_start(dst, src).then_inc(wld.h, 16)
                wld.v += 16
                for tb in range(NTB):
                    if nb >= 2:
                        g.wait_ge(mm2.h, (nb - 1) * NDq)
                        g.wait_ge(ev2.h, (nb - 1) * NDq)
                    g.dma_start(hbufs[nb % 2][:],
                                feat_ap(hscr, T, col0=tb * 512, ncols=512, nch=NI)
                                ).then_inc(hld[nb % 2].h, 16)
                    hld[nb % 2].v += 16
                    g.dma_start(rbufs[nb % 2][:],
                                feat_ap(resid, T, chunk0=q * NDq, nch=NDq,
                                        col0=tb * 512, ncols=512)
                                ).then_inc(rld[nb % 2].h, 16)
                    rld[nb % 2].v += 16
                    nb += 1

        @blk.tensor
        def _(t):
            n = 0
            nb = 0
            for q in range(NQ):
                t.wait_ge(wld.h, 16 * (q + 2))
                for tb in range(NTB):
                    t.wait_ge(hld[nb % 2].h, 16 * (nb // 2 + 1))
                    for dc in range(NDq):
                        bank = psum[n % 4]
                        if n >= 4:
                            t.wait_ge(ev2.h, n - 3)
                        ins = None
                        for icc in range(NI):
                            ins = t.matmul(bank[:],
                                           dw[:, icc * NDq * 128 + dc * 128:
                                              icc * NDq * 128 + dc * 128 + 128],
                                           hbufs[nb % 2][:, icc * 512:(icc + 1) * 512],
                                           start=(icc == 0), stop=(icc == NI - 1))
                        ins.then_inc(mm2.h, 1)
                        n += 1
                    nb += 1
            mm2.v = n

        @blk.vector
        def _(v):
            n = 0
            nb = 0
            for q in range(NQ):
                for tb in range(NTB):
                    v.wait_ge(rld[nb % 2].h, 16 * (nb // 2 + 1))
                    if nb >= 2:
                        v.wait_ge(ost[nb % 2].h, 16 * (nb // 2))
                    for dc in range(NDq):
                        v.wait_ge(mm2.h, n + 1)
                        v.scalar_tensor_tensor(obufs[nb % 2][:, dc * 512:(dc + 1) * 512],
                                               psum[n % 4][:],
                                               dss[:, q * NDq + dc:q * NDq + dc + 1],
                                               rbufs[nb % 2][:, dc * 512:(dc + 1) * 512],
                                               ALU.mult, ALU.add).then_inc(ev2.h, 1)
                        n += 1
                    nb += 1
            ev2.v = n

        @blk.sync
        def _(s):
            nb = 0
            for q in range(NQ):
                for tb in range(NTB):
                    s.wait_ge(ev2.h, (nb + 1) * NDq)
                    dst = feat_ap(out, T, chunk0=q * NDq, nch=NDq, col0=tb * 512, ncols=512)
                    s.dma_start(dst, obufs[nb % 2][:]).then_inc(ost[nb % 2].h, 16)
                    ost[nb % 2].v += 16
                    nb += 1
            s.wait_ge(ost[0].h, ost[0].v)
            s.wait_ge(ost[1].h, ost[1].v)


def kernel(**inputs):
    raise NotImplementedError("assembled later")




def emit_ln(nc, aps, D=2048, T=4096, tag="ln0"):
    NK = D // 128
    NTB = T // 512
    x, xn = aps["x"], aps["xn"]
    with (
        nc.sbuf_tensor(f"{tag}xb", [128, NK * 512], BF16) as xb,
        nc.sbuf_tensor(f"{tag}xs", [128, NK * 512], F32) as xs,
        nc.sbuf_tensor(f"{tag}xnb", [128, NK * 512], BF16) as xnb,
        nc.sbuf_tensor(f"{tag}sq0", [128, 512], BF16) as sq0,
        nc.sbuf_tensor(f"{tag}sq1", [128, 512], BF16) as sq1,
        nc.sbuf_tensor(f"{tag}onesc", [128, 1], BF16) as onesc,
        nc.sbuf_tensor(f"{tag}ones1", [1, 128], F32) as ones1,
        nc.sbuf_tensor(f"{tag}rows", [1, 4 * 512], F32) as rows,
        nc.sbuf_tensor(f"{tag}abc", [128, 512], F32) as abc,
        nc.sbuf_tensor(f"{tag}bbc", [128, 512], F32) as bbc,
        nc.sbuf_tensor(f"{tag}t3", [128, 512], F32) as t3,
        nc.psum_tensor(f"{tag}pa", [128, 512], F32) as pa,
        nc.psum_tensor(f"{tag}pb", [128, 512], F32) as pb,
        nc.psum_tensor(f"{tag}pc", [128, 512], F32) as pc,
        nc.psum_tensor(f"{tag}pd", [128, 512], F32) as pd,
        nc.Block() as blk,
    ):
        sqs = [sq0, sq1]
        R_MU = rows[0:1, 0:512]
        R_MU2 = rows[0:1, 512:1024]
        R_V = rows[0:1, 1024:1536]
        R_A = rows[0:1, 1536:2048]
        xldb = Ctr(nc, f"{tag}xldb")
        xlds = Ctr(nc, f"{tag}xlds")
        evq = Ctr(nc, f"{tag}evq")
        mmq = Ctr(nc, f"{tag}mmq")
        evrow = Ctr(nc, f"{tag}evrow")
        evsq = Ctr(nc, f"{tag}evsq")
        mmb = Ctr(nc, f"{tag}mmb")
        evx = Ctr(nc, f"{tag}evx")
        xst = Ctr(nc, f"{tag}xst")
        dvc = Ctr(nc, f"{tag}dvc")

        @blk.gpsimd
        def _(g):
            for qb in range(NTB):
                if qb >= 1:
                    g.wait_ge(mmq.h, NK * qb)
                    g.wait_ge(evq.h, NK * qb)
                    g.wait_ge(evx.h, qb)
                g.dma_start(xb[:], feat_ap(x, T, nch=NK, col0=qb * 512, ncols=512)
                            ).then_inc(xldb.h, 16)
                g.dma_start(xs[:], feat_ap(x, T, nch=NK, col0=qb * 512, ncols=512)
                            ).then_inc(xlds.h, 16)
                xldb.v += 16
                xlds.v += 16

        @blk.vector
        def _(v):
            v.memset(onesc[:], 1.0)
            v.memset(ones1[:], 1.0).then_inc(evq.h, 1)
            for qb in range(NTB):
                v.wait_ge(xldb.h, 16 * (qb + 1))
                for k in range(NK):
                    j = qb * NK + k
                    if j >= 2:
                        v.wait_ge(mmq.h, j - 1)
                    v.tensor_mul(sqs[j % 2][:], xb[:, k * 512:(k + 1) * 512],
                                 xb[:, k * 512:(k + 1) * 512]).then_inc(evq.h, 1)
                v.wait_ge(mmq.h, NK * (qb + 1))
                v.tensor_scalar_mul(R_MU, pa[0:1, :], 1.0 / D)
                v.tensor_scalar_mul(R_MU2, pb[0:1, :], 1.0 / D).then_inc(evrow.h, 1)
                v.wait_ge(evrow.h, 3 * qb + 1)
                v.scalar_tensor_tensor(R_V, R_MU, 1.0, R_MU, ALU.mult, ALU.mult
                                       ).then_inc(dvc.h, 1)
                dvc.v += 1
                v.wait_ge(dvc.h, dvc.v)
                v.scalar_tensor_tensor(R_V, R_MU2, 1e-5, R_V, ALU.add, ALU.subtract
                                       ).then_inc(evrow.h, 1)
                v.wait_ge(evsq.h, qb + 1)
                v.reciprocal(R_A, R_V).then_inc(dvc.h, 1)
                dvc.v += 1
                v.wait_ge(dvc.h, dvc.v)
                v.scalar_tensor_tensor(R_MU, R_MU, 1.0, R_A, ALU.mult, ALU.mult
                                       ).then_inc(evrow.h, 1)
                v.wait_ge(mmb.h, 2 * (qb + 1))
                v.wait_ge(xlds.h, 16 * (qb + 1))
                if qb >= 1:
                    v.wait_ge(xst.h, 16 * qb)
                v.tensor_copy(abc[:], pc[:])
                v.tensor_copy(bbc[:], pd[:]).then_inc(dvc.h, 1)
                dvc.v += 1
                v.wait_ge(dvc.h, dvc.v)
                for k in range(NK):
                    v.tensor_mul(t3[:], xs[:, k * 512:(k + 1) * 512], abc[:]
                                 ).then_inc(dvc.h, 1)
                    dvc.v += 1
                    v.wait_ge(dvc.h, dvc.v)
                    ins = v.tensor_sub(xnb[:, k * 512:(k + 1) * 512], t3[:], bbc[:])
                    if k == NK - 1:
                        ins.then_inc(evx.h, 1)
                        evx.v += 1
                    else:
                        ins.then_inc(dvc.h, 1)
                        dvc.v += 1
                        v.wait_ge(dvc.h, dvc.v)

        @blk.scalar
        def _(s):
            for qb in range(NTB):
                s.wait_ge(evrow.h, 3 * qb + 2)
                s.sqrt(R_V, R_V).then_inc(evsq.h, 1)

        @blk.tensor
        def _(t):
            t.wait_ge(evq.h, 1)
            for qb in range(NTB):
                if qb >= 1:
                    t.wait_ge(evrow.h, 3 * qb - 2)
                t.wait_ge(xldb.h, 16 * (qb + 1))
                for k in range(NK):
                    j = qb * NK + k
                    t.wait_ge(evq.h, j + 2)
                    t.matmul(pa[0:1, :], onesc[:], xb[:, k * 512:(k + 1) * 512],
                             start=(k == 0), stop=(k == NK - 1))
                    ins = t.matmul(pb[0:1, :], onesc[:], sqs[j % 2][:],
                                   start=(k == 0), stop=(k == NK - 1))
                    ins.then_inc(mmq.h, 1)
                    mmq.v += 1
                t.wait_ge(evrow.h, 3 * (qb + 1))
                if qb >= 1:
                    t.wait_ge(evx.h, qb)
                t.matmul(pc[:], ones1[:, :], R_A, start=True, stop=True).then_inc(mmb.h, 1)
                t.matmul(pd[:], ones1[:, :], R_MU, start=True, stop=True).then_inc(mmb.h, 1)
                mmb.v += 2

        @blk.sync
        def _(s):
            for qb in range(NTB):
                s.wait_ge(evx.h, qb + 1)
                s.dma_start(feat_ap(xn, T, nch=NK, col0=qb * 512, ncols=512), xnb[:]
                            ).then_inc(xst.h, 16)
                xst.v += 16
            s.wait_ge(xst.h, xst.v)

MBIAS = 30.0


def emit_attn0(nc, aps, NH=16, T=4096, tag="a0"):
    D = NH * 128
    NK = D // 128
    NTB = T // 512
    NW = NK + 2
    x, xn, wqT, wkT, wvT = aps["x"], aps["xn"], aps["wqT"], aps["wkT"], aps["wvT"]
    cosq, sinq, cosk, sink = aps["cosq"], aps["sinq"], aps["cosk"], aps["sink"]
    cmask, x1p = aps["cmask"], aps["x1p"]

    tiles = []
    atc_end = []
    ndiag = 0
    for qb in range(NTB):
        NKT = 4 * (qb + 1)
        for h in range(NH):
            for kt in range(NKT):
                if kt >= NKT - 4:
                    tiles.append((qb, h, kt, NKT, ndiag))
                    ndiag += 1
                else:
                    tiles.append((qb, h, kt, NKT, None))
            atc_end.append(len(tiles))

    from contextlib import ExitStack
    es = ExitStack()
    wq = es.enter_context(nc.sbuf_tensor(f"{tag}wq", [128, NK * NW * 128], BF16))
    xnb = es.enter_context(nc.sbuf_tensor(f"{tag}xnb", [128, NK * 512], BF16))
    xs = es.enter_context(nc.sbuf_tensor(f"{tag}xs", [128, NK * 512], F32))
    qs = es.enter_context(nc.sbuf_tensor(f"{tag}qs", [128, NH * 512], BF16))
    kc = es.enter_context(nc.sbuf_tensor(f"{tag}kc", [128, T], BF16))
    vt = es.enter_context(nc.sbuf_tensor(f"{tag}vt", [128, T], BF16))
    vtmp = es.enter_context(nc.sbuf_tensor(f"{tag}vtmp", [128, 512], BF16))
    rope = es.enter_context(nc.sbuf_tensor(f"{tag}rope", [64, 4 * 512], F32))
    pb0 = es.enter_context(nc.sbuf_tensor(f"{tag}pb0", [128, 512], BF16))
    pb1 = es.enter_context(nc.sbuf_tensor(f"{tag}pb1", [128, 512], BF16))
    t1 = es.enter_context(nc.sbuf_tensor(f"{tag}t1", [64, 512], F32))
    t2 = es.enter_context(nc.sbuf_tensor(f"{tag}t2", [64, 512], F32))
    t3 = es.enter_context(nc.sbuf_tensor(f"{tag}t3", [128, 512], F32))
    rrow = es.enter_context(nc.sbuf_tensor(f"{tag}rrow", [1, 512], BF16))
    rln = es.enter_context(nc.sbuf_tensor(f"{tag}rln", [1, 1024], F32))
    rbc = es.enter_context(nc.sbuf_tensor(f"{tag}rbc", [128, 512], F32))
    ones1 = es.enter_context(nc.sbuf_tensor(f"{tag}ones1", [1, 128], BF16))
    onesc = es.enter_context(nc.sbuf_tensor(f"{tag}onesc", [128, 1], BF16))
    mbc = es.enter_context(nc.sbuf_tensor(f"{tag}mbc", [128, 1], F32))
    cm = es.enter_context(nc.sbuf_tensor(f"{tag}cm", [128, 4 * 512], BF16))
    idn = es.enter_context(nc.sbuf_tensor(f"{tag}idn", [128, 128], BF16))
    ps0 = es.enter_context(nc.psum_tensor(f"{tag}s0", [128, 512], F32))
    ps1 = es.enter_context(nc.psum_tensor(f"{tag}s1", [128, 512], F32))
    ps2 = es.enter_context(nc.psum_tensor(f"{tag}s2", [128, 512], F32))
    ps3 = es.enter_context(nc.psum_tensor(f"{tag}s3", [128, 512], F32))
    psO = es.enter_context(nc.psum_tensor(f"{tag}sO", [128, 512], F32))
    psden = es.enter_context(nc.psum_tensor(f"{tag}sden", [128, 512], F32))
    pst = es.enter_context(nc.psum_tensor(f"{tag}st", [128, 128], BF16))
    with es, nc.Block() as blk:
        psum = [ps0, ps1, ps2, ps3]
        pbufs = [pb0, pb1]
        wld = Ctr(nc, f"{tag}wld")
        xnl = Ctr(nc, f"{tag}xnl")
        xsl = Ctr(nc, f"{tag}xsl")
        rl = Ctr(nc, f"{tag}rl")
        ini = Ctr(nc, f"{tag}ini")
        mmQ = Ctr(nc, f"{tag}mmQ")
        evQ = Ctr(nc, f"{tag}evQ")
        mmT = Ctr(nc, f"{tag}mmT")
        evT = Ctr(nc, f"{tag}evT")
        mmA = Ctr(nc, f"{tag}mmA")
        evE = Ctr(nc, f"{tag}evE")
        evM = Ctr(nc, f"{tag}evM")
        mmO = Ctr(nc, f"{tag}mmO")
        evR = Ctr(nc, f"{tag}evR")
        acR = Ctr(nc, f"{tag}acR")
        mmR = Ctr(nc, f"{tag}mmR")
        evN = Ctr(nc, f"{tag}evN")
        st = Ctr(nc, f"{tag}st")
        dvc = Ctr(nc, f"{tag}dvc")

        @blk.gpsimd
        def _(g):
            srcq = bass.AP(wqT.tensor, wqT.offset,
                           [[NH * 128, 128], [NH * 128 * 128, NK], [128, NH], [1, 128]])
            dstq = bass.AP(wq, 0, [[NK * NW * 128, 128], [NW * 128, NK], [128, NK], [1, 128]])
            g.dma_start(dstq, srcq).then_inc(wld.h, 16)
            srck = bass.AP(wkT.tensor, wkT.offset, [[128, 128], [128 * 128, NK], [1, 128]])
            dstk = bass.AP(wq, NK * 128, [[NK * NW * 128, 128], [NW * 128, NK], [1, 128]])
            g.dma_start(dstk, srck).then_inc(wld.h, 16)
            srcv = bass.AP(wvT.tensor, wvT.offset, [[128, 128], [128 * 128, NK], [1, 128]])
            dstv = bass.AP(wq, (NK + 1) * 128, [[NK * NW * 128, 128], [NW * 128, NK], [1, 128]])
            g.dma_start(dstv, srcv).then_inc(wld.h, 16)
            g.dma_start(cm[:], bass.AP(cmask.tensor, cmask.offset,
                                       [[4 * 512, 128], [1, 4 * 512]])).then_inc(wld.h, 16)
            g.dma_start(idn[:], bass.AP(aps["ident"].tensor, aps["ident"].offset,
                                        [[128, 128], [1, 128]])).then_inc(wld.h, 16)
            wld.v = 80
            for qb in range(NTB):
                if qb >= 1:
                    g.wait_ge(st.h, 16 * qb)
                    g.wait_ge(mmQ.h, NW * qb)
                    g.wait_ge(evQ.h, NW * qb)
                g.dma_start(xnb[:], feat_ap(xn, T, nch=NK, col0=qb * 512, ncols=512)
                            ).then_inc(xnl.h, 16)
                xnl.v += 16
                g.dma_start(xs[:], feat_ap(x, T, nch=NK, col0=qb * 512, ncols=512)
                            ).then_inc(xsl.h, 16)
                xsl.v += 16
                for i, tb in enumerate((cosq, sinq, cosk, sink)):
                    g.dma_start(rope[:, i * 512:(i + 1) * 512],
                                bass.AP(tb.tensor, tb.offset + qb * 512, [[T, 64], [1, 512]])
                                ).then_inc(rl.h, 16)
                rl.v += 64

        @blk.tensor
        def _(t):
            t.wait_ge(wld.h, wld.v)
            t.wait_ge(ini.h, 2)
            atc = 0
            for qb in range(NTB):
                NKT = 4 * (qb + 1)
                hbase = qb * NH
                t.wait_ge(xnl.h, 16 * (qb + 1))
                if qb >= 1:
                    t.wait_ge(evN.h, qb * NH)
                for j in range(NW):
                    n = qb * NW + j
                    if n >= 4:
                        t.wait_ge(evQ.h, n - 3)
                    bank = psum[n % 4]
                    ins = None
                    for k in range(NK):
                        ins = t.matmul(bank[:],
                                       wq[:, k * NW * 128 + j * 128: k * NW * 128 + j * 128 + 128],
                                       xnb[:, k * 512:(k + 1) * 512],
                                       start=(k == 0), stop=(k == NK - 1))
                    ins.then_inc(mmQ.h, 1)
                    mmQ.v += 1
                t.wait_ge(evQ.h, (qb + 1) * NW)
                for i in range(4):
                    if qb > 0 or i > 0:
                        t.wait_ge(evT.h, qb * 4 + i)
                    t.transpose(pst[:], vtmp[:, i * 128:(i + 1) * 128], idn[:]
                                ).then_inc(mmT.h, 1)
                    mmT.v += 1
                t.wait_ge(evT.h, (qb + 1) * 4)
                for h in range(NH):
                    hh = hbase + h
                    t.wait_ge(evR.h, hh)
                    t.wait_ge(evN.h, hh)
                    for kt in range(NKT):
                        a = atc + kt
                        if a >= 2:
                            t.wait_ge(evE.h, a - 1)
                        t.matmul(psum[a % 2][:], kc[:, kt * 128:(kt + 1) * 128],
                                 qs[:, h * 512:(h + 1) * 512], start=True, stop=True
                                 ).then_inc(mmA.h, 1)
                        mmA.v += 1
                        if kt >= 1:
                            b = a - 1
                            bkt = kt - 1
                            t.wait_ge(evE.h, b + 1)
                            if tiles[b][4] is not None:
                                t.wait_ge(evM.h, tiles[b][4] + 1)
                            t.matmul(psden[0:1, :], onesc[:], pbufs[b % 2][:],
                                     start=(bkt == 0), stop=False)
                            t.matmul(psO[:], vt[:, bkt * 128:(bkt + 1) * 128],
                                     pbufs[b % 2][:],
                                     start=(bkt == 0), stop=(bkt == NKT - 1)
                                     ).then_inc(mmO.h, 1)
                            mmO.v += 1
                    b = atc + NKT - 1
                    bkt = NKT - 1
                    t.wait_ge(evE.h, b + 1)
                    if tiles[b][4] is not None:
                        t.wait_ge(evM.h, tiles[b][4] + 1)
                    t.matmul(psden[0:1, :], onesc[:], pbufs[b % 2][:],
                             start=(bkt == 0), stop=True)
                    ins = t.matmul(psO[:], vt[:, bkt * 128:(bkt + 1) * 128],
                                   pbufs[b % 2][:], start=(bkt == 0), stop=True)
                    ins.then_inc(mmO.h, 1)
                    mmO.v += 1
                    atc += NKT
                    t.wait_ge(evR.h, hh + 1)
                    t.matmul(ps2[:], ones1[:, :], rrow[0:1, :], start=True, stop=True
                             ).then_inc(mmR.h, 1)
                    mmR.v += 1

        @blk.scalar
        def _(s):
            hh = 0
            for a, (qb, h, kt, NKT, di) in enumerate(tiles):
                s.wait_ge(mmA.h, a + 1)
                if a >= 2:
                    s.wait_ge(mmO.h, a - 1)
                s.activation(pbufs[a % 2][:], psum[a % 2][:], AF.Exp,
                             bias=mbc[:, 0:1]).then_inc(evE.h, 1)
                if kt == NKT - 1:
                    pr = (hh % 2) * 512
                    s.wait_ge(mmO.h, atc_end[hh])
                    if hh >= 2:
                        s.wait_ge(acR.h, hh - 1)
                    s.activation(rln[0:1, pr:pr + 512], psden[0:1, :], AF.Ln
                                 ).then_inc(acR.h, 1)
                    s.wait_ge(acR.h, hh + 1)
                    s.activation(rrow[0:1, :], rln[0:1, pr:pr + 512], AF.Exp,
                                 scale=-1.0).then_inc(evR.h, 1)
                    hh += 1
            evE.v = len(tiles)
            evR.v = hh

        @blk.vector
        def _(v):
            v.memset(ones1[:], 1.0)
            v.memset(mbc[:], -MBIAS)
            v.memset(onesc[:], 1.0).then_inc(ini.h, 2)
            ini.v = 2
            atc = 0
            for qb in range(NTB):
                NKT = 4 * (qb + 1)
                hbase = qb * NH
                v.wait_ge(rl.h, 64 * (qb + 1))
                for j in range(NW):
                    n = qb * NW + j
                    v.wait_ge(mmQ.h, n + 1)
                    bank = psum[n % 4]
                    if j <= NK:
                        co = 0 if j < NK else 2 * 512
                        so = 512 if j < NK else 3 * 512
                        if j < NK:
                            dstE = qs[0:64, j * 512:(j + 1) * 512]
                            dstO = qs[64:128, j * 512:(j + 1) * 512]
                        else:
                            dstE = kc[0:64, qb * 512:(qb + 1) * 512]
                            dstO = kc[64:128, qb * 512:(qb + 1) * 512]
                        E = bank[0:64, :]
                        O = bank[64:128, :]
                        v.wait_ge(evQ.h, evQ.v)
                        v.tensor_mul(t1[:], E, rope[:, co:co + 512])
                        v.tensor_mul(t2[:], O, rope[:, so:so + 512]).then_inc(dvc.h, 1)
                        dvc.v += 1
                        v.wait_ge(dvc.h, dvc.v)
                        v.tensor_sub(dstE, t1[:], t2[:]).then_inc(dvc.h, 1)
                        dvc.v += 1
                        v.wait_ge(dvc.h, dvc.v)
                        v.tensor_mul(t1[:], O, rope[:, co:co + 512])
                        v.tensor_mul(t2[:], E, rope[:, so:so + 512]).then_inc(dvc.h, 1)
                        dvc.v += 1
                        v.wait_ge(dvc.h, dvc.v)
                        ins = v.tensor_add(dstO, t1[:], t2[:])
                    else:
                        ins = v.tensor_copy(vtmp[:], bank[:])
                    ins.then_inc(evQ.h, 1)
                    evQ.v += 1
                for i in range(4):
                    v.wait_ge(mmT.h, qb * 4 + i + 1)
                    v.tensor_copy(vt[:, (qb * 4 + i) * 128:(qb * 4 + i) * 128 + 128],
                                  pst[:]).then_inc(evT.h, 1)
                    evT.v += 1
                v.wait_ge(xsl.h, 16 * (qb + 1))
                for h in range(NH):
                    hh = hbase + h
                    for kt in range(NKT - 4, NKT):
                        a = atc + kt
                        di = tiles[a][4]
                        o = kt - (NKT - 4)
                        v.wait_ge(evE.h, a + 1)
                        v.tensor_mul(pbufs[a % 2][:], pbufs[a % 2][:],
                                     cm[:, o * 512:(o + 1) * 512]).then_inc(evM.h, 1)
                        evM.v += 1
                    atc += NKT
                    v.wait_ge(mmR.h, hh + 1)
                    v.wait_ge(evN.h, hh)
                    v.tensor_copy(rbc[:], ps2[:]).then_inc(dvc.h, 1)
                    dvc.v += 1
                    v.wait_ge(dvc.h, dvc.v)
                    v.tensor_mul(t3[:], psO[:], rbc[:]).then_inc(dvc.h, 1)
                    dvc.v += 1
                    v.wait_ge(dvc.h, dvc.v)
                    v.tensor_add(xs[:, h * 512:(h + 1) * 512],
                                 xs[:, h * 512:(h + 1) * 512], t3[:]).then_inc(evN.h, 1)
                    evN.v += 1

        @blk.sync
        def _(s):
            for qb in range(NTB):
                s.wait_ge(evN.h, (qb + 1) * NH)
                s.dma_start(feat_ap(x1p, T, nch=NK, col0=qb * 512, ncols=512), xs[:]
                            ).then_inc(st.h, 16)
                st.v += 16
            s.wait_ge(st.h, st.v)

class Sched:
    ENGINES = ("gpsimd", "tensor", "scalar", "vector", "sync")

    def __init__(self, nc, tag):
        self.nc = nc
        self.tag = tag
        self.ops = {e: [] for e in self.ENGINES}
        self._ctrs = []

    def ctr(self, name):
        c = Ctr(self.nc, f"{self.tag}{name}")
        self._ctrs.append(c)
        return c

    def add(self, eng, fn, waits=(), inc=None, inc_n=1):
        post = None
        if inc is not None:
            inc.v += inc_n
            post = inc.v
        self.ops[eng].append((tuple(waits), fn, inc, inc_n))
        return post

    def add_raw(self, eng, fn, waits=(), inc=None, inc_n=1):
        self.ops[eng].append((tuple(waits), fn, inc, inc_n))

    def emit(self, blk):
        for eng in self.ENGINES:
            if not self.ops[eng]:
                continue
            def body(e, _ops=self.ops[eng]):
                for waits, fn, inc, inc_n in _ops:
                    for c, val in waits:
                        if val > 0:
                            e.wait_ge(c.h, val)
                    ins = fn(e)
                    if inc is not None:
                        ins.then_inc(inc.h, inc_n)
            getattr(blk, eng)(body)


def emit_l1proj(nc, aps, NH=16, T=4096, tag="c1"):
    D = NH * 128
    NK = D // 128
    NTB = T // 512
    R = 4
    xn = aps["xn1"]
    qo, ko, vo, bsg = aps["q"], aps["k"], aps["v"], aps["bsg"]
    f1s, g1s = aps["f1s"], aps["g1s"]
    qw, kw, vw, bw, f1w, g1w = (aps[s] for s in ("qw", "kw", "vw", "bw", "f1w", "g1w"))
    cvw = aps["cvw"]

    chunks = []
    cv = 0
    for c in range(NH):
        chunks.append((qw, D, c, qo, c, "qn", cv)); cv += 1
    for c in range(NH * R):
        chunks.append((kw, NH * R * 128, c, ko, c, "kn", cv)); cv += 1
    for c in range(NH * R):
        chunks.append((vw, NH * R * 128, c, vo, c, "v", cv)); cv += 1
    chunks.append((bw, 64, 0, bsg, 0, "b", None))
    chunks.append((f1w, 128, 0, f1s, 0, "plain", None))
    chunks.append((g1w, 128, 0, g1s, 0, "plain", None))
    NCV = NH + 2 * NH * R

    from contextlib import ExitStack
    es = ExitStack()
    xns = es.enter_context(nc.sbuf_tensor(f"{tag}xns", [128, NK * T], BF16))
    wb = [es.enter_context(nc.sbuf_tensor(f"{tag}wb{j}", [128, D], BF16)) for j in range(2)]
    cbs = [es.enter_context(nc.sbuf_tensor(f"{tag}cb{j}", [128, T + 3], BF16))
           for j in range(2)]
    t4 = es.enter_context(nc.sbuf_tensor(f"{tag}t4", [128, T], F32))
    t5 = es.enter_context(nc.sbuf_tensor(f"{tag}t5", [128, T], BF16))
    ysil = es.enter_context(nc.sbuf_tensor(f"{tag}ysil", [128, T], BF16))
    ob = [es.enter_context(nc.sbuf_tensor(f"{tag}ob{j}", [128, T], BF16)) for j in range(2)]
    nln = es.enter_context(nc.sbuf_tensor(f"{tag}nln", [1, 512], F32))
    nrm = es.enter_context(nc.sbuf_tensor(f"{tag}nrm", [1, 2 * 512], BF16))
    cvs = es.enter_context(nc.sbuf_tensor(f"{tag}cvs", [128, NCV * 4], F32))
    onesc = es.enter_context(nc.sbuf_tensor(f"{tag}onesc", [128, 1], BF16))
    ones1 = es.enter_context(nc.sbuf_tensor(f"{tag}ones1", [1, 128], BF16))
    cbank = es.enter_context(nc.sbuf_tensor(f"{tag}cbank", [128, 4], F32))
    psum = [es.enter_context(nc.psum_tensor(f"{tag}p{j}", [128, 512], F32)) for j in range(4)]
    pn = es.enter_context(nc.psum_tensor(f"{tag}pn", [128, 512], F32))
    pg = [es.enter_context(nc.psum_tensor(f"{tag}pg{j}", [128, 512], F32)) for j in range(2)]
    with es, nc.Block() as blk:
        S = Sched(nc, tag)
        xld, cld = S.ctr("xld"), S.ctr("cld")
        wl = [S.ctr("wl0"), S.ctr("wl1")]
        mm, evc, gcv = S.ctr("mm"), S.ctr("evc"), S.ctr("gcv")
        dvc, acc = S.ctr("dvc"), S.ctr("acc")
        mmn, mnb = S.ctr("mmn"), S.ctr("mnb")
        ost = [S.ctr("ost0"), S.ctr("ost1")]

        S.add("gpsimd", lambda g: g.dma_start(xns[:], feat_ap(xn, T, nch=NK)),
              inc=xld, inc_n=16)
        S.add("gpsimd", lambda g: g.dma_start(
            cvs[:], bass.AP(cvw.tensor, cvw.offset, [[NCV * 4, 128], [1, NCV * 4]])),
            inc=cld, inc_n=16)
        S.add("vector", lambda v: v.memset(onesc[:], 1.0))
        S.add("vector", lambda v: v.memset(cbank[:, 0:1], 1e-6))
        S.add("vector", lambda v: v.memset(cbank[:, 1:2], -0.5 * float(np.log(128.0))))
        S.add("vector", lambda v: v.memset(cbank[:, 2:3], 0.0))
        S.add("vector", lambda v: v.memset(ones1[:], 1.0), inc=dvc)

        st_count = [0, 0]
        cb_free = {}
        t4_free = None
        ysil_free = None
        t5_free = None
        nrm_free = {}
        exp_acc = {}
        pg_free = [None, None]
        deferred_pe = []

        for i, (wap, wcols, wc, oap, oc, kind, cvc) in enumerate(chunks):
            M = 64 if kind == "b" else 128
            par = i % 2
            ncols = 64 if kind == "b" else 128
            wwaits = [(mm, (i - 1) * NTB)] if i >= 2 else []

            def wload(g, wap=wap, wcols=wcols, wc=wc, M=M, par=par):
                dst = bass.AP(wb[par], 0, [[D, 128], [M, NK], [1, M]])
                return g.dma_start(dst,
                                   bass.AP(wap.tensor, wap.offset + wc * 128,
                                           [[wcols, 128], [wcols * 128, NK], [1, M]]))
            S.add("gpsimd", wload, waits=wwaits, inc=wl[par], inc_n=16)

            for tb in range(NTB):
                n = i * NTB + tb
                waits = [(xld, 16), (dvc, 1)] if n == 0 else []
                waits.append((wl[par], 16 * (i // 2 + 1)))
                if n >= 4:
                    waits.append((evc, n - 3))

                def pmm(t, tb=tb, n=n, M=M, par=par):
                    ins = None
                    for kk in range(NK):
                        ins = t.matmul(psum[n % 4][0:M, :],
                                       wb[par][:, kk * M:kk * M + M],
                                       xns[:, kk * T + tb * 512: kk * T + tb * 512 + 512],
                                       start=(kk == 0), stop=(kk == NK - 1))
                    return ins
                S.add("tensor", pmm, waits=waits, inc=mm)
                if tb == 0:
                    for w_, f_, c_ in deferred_pe:
                        S.add_raw("tensor", f_, waits=w_, inc=c_)
                    deferred_pe = []

            cb = cbs[i % 2]
            guards = [(mm, i * NTB + 1), (dvc, dvc.v)]
            if cb_free.get(i % 2) is not None:
                guards.append(cb_free[i % 2])
            S.add("vector", lambda v, M=M, cb=cb: v.memset(cb[0:M, 0:3], 0.0),
                  waits=guards, inc=dvc)
            for tb in range(NTB):
                n = i * NTB + tb
                S.add("vector",
                      lambda v, M=M, tb=tb, n=n, cb=cb: v.tensor_copy(
                          cb[0:M, 3 + tb * 512:3 + (tb + 1) * 512], psum[n % 4][0:M, :]),
                      waits=[(mm, n + 1), (dvc, dvc.v)], inc=evc)

            if kind in ("qn", "kn", "v"):
                vwaits = [(evc, (i + 1) * NTB), (cld, 16), (dvc, dvc.v)]
                if t4_free is not None:
                    vwaits.append((acc, t4_free))
                S.add("vector", lambda v, cvc=cvc, cb=cb: v.tensor_scalar(
                    t4[:], cb[:, 3:T + 3], cvs[:, 4 * cvc + 3:4 * cvc + 4], None, ALU.mult),
                    waits=vwaits, inc=dvc)
                for tap in (2, 1, 0):
                    S.add("vector", lambda v, cvc=cvc, tap=tap, cb=cb: v.scalar_tensor_tensor(
                        t4[:], cb[:, tap:T + tap], cvs[:, 4 * cvc + tap:4 * cvc + tap + 1],
                        t4[:], ALU.mult, ALU.add), waits=[(dvc, dvc.v)], inc=dvc)
                conv_done = dvc.v
                cb_free[i % 2] = (dvc, conv_done)

                if kind == "v":
                    swaits = [(dvc, conv_done)]
                    if st_count[par]:
                        swaits.append((ost[par], 16 * st_count[par]))
                    S.add("scalar", lambda s, par=par: s.activation(
                        ob[par][:], t4[:], AF.Silu),
                        waits=swaits + [(acc, acc.v)], inc=acc)
                    t4_free = acc.v
                    out_ready = (acc, acc.v)
                else:
                    swaits = [(dvc, conv_done)]
                    if ysil_free is not None:
                        swaits.append((dvc, ysil_free))
                    S.add("scalar", lambda s: s.activation(ysil[:], t4[:], AF.Silu),
                          waits=swaits + [(acc, acc.v)], inc=acc)
                    t4_free = acc.v
                    sq_waits = []
                    if t5_free is not None:
                        sq_waits.append((mmn, t5_free))
                    S.add("scalar", lambda s: s.activation(t5[:], ysil[:], AF.Square),
                          waits=sq_waits + [(acc, acc.v)], inc=acc)
                    sq_done = acc.v
                    ln_prev = 0
                    for tb in range(NTB):
                        w_ = [(acc, sq_done)]
                        if ln_prev:
                            w_.append((acc, ln_prev))
                        def smm(t, tb=tb):
                            return t.matmul(pn[0:1, :], onesc[:],
                                            t5[:, tb * 512:(tb + 1) * 512],
                                            start=True, stop=True)
                        deferred_pe.append((w_, smm, mmn))
                        mmn.v += 1
                        mv = mmn.v
                        def lnop(s, tb=tb):
                            return s.activation(nln[0:1, :],
                                                pn[0:1, :], AF.Ln, bias=cbank[0:1, 0:1])
                        S.add("scalar", lnop, waits=[(mmn, mv), (acc, acc.v)], inc=acc)
                        bcol = 1 if kind == "qn" else 2
                        ew = [(acc, acc.v)]
                        if nrm_free.get(tb % 2):
                            ew.append((mnb, nrm_free[tb % 2]))
                        S.add("scalar", lambda s, tb=tb, bcol=bcol: s.activation(
                            nrm[0:1, (tb % 2) * 512:(tb % 2) * 512 + 512],
                            nln[0:1, :], AF.Exp, scale=-0.5,
                            bias=cbank[0:1, bcol:bcol + 1]), waits=ew, inc=acc)
                        exp_acc[tb] = acc.v
                        ln_prev = acc.v
                    t5_free = mmn.v
                    for tb in range(NTB):
                        w_ = [(acc, exp_acc[tb])]
                        if pg_free[tb % 2] is not None:
                            w_.append((dvc, pg_free[tb % 2]))
                        def bmm(t, tb=tb):
                            return t.matmul(pg[tb % 2][:, :], ones1[0:1, :],
                                            nrm[0:1, (tb % 2) * 512:(tb % 2) * 512 + 512],
                                            start=True, stop=True)
                        deferred_pe.append((w_, bmm, mnb))
                        mnb.v += 1
                        nrm_free[tb % 2] = mnb.v
                        bv = mnb.v
                        stw = [(mnb, bv), (dvc, dvc.v)]
                        if st_count[par]:
                            stw.append((ost[par], 16 * st_count[par]))
                        S.add("vector", lambda v, tb=tb, par=par: v.tensor_mul(
                            ob[par][:, tb * 512:(tb + 1) * 512],
                            ysil[:, tb * 512:(tb + 1) * 512], pg[tb % 2][:, :]),
                            waits=stw, inc=dvc)
                        pg_free[tb % 2] = dvc.v
                    ysil_free = dvc.v
                    out_ready = (dvc, dvc.v)
            elif kind == "b":
                swaits = [(evc, (i + 1) * NTB)]
                if st_count[par]:
                    swaits.append((ost[par], 16 * st_count[par]))
                S.add("scalar", lambda s, par=par, cb=cb: s.activation(
                    ob[par][0:64, :], cb[0:64, 0:T], AF.Sigmoid),
                    waits=swaits + [(acc, acc.v)], inc=acc)
                cb_free[i % 2] = (acc, acc.v)
                out_ready = (acc, acc.v)
            else:
                pwaits = [(evc, (i + 1) * NTB), (dvc, dvc.v)]
                if st_count[par]:
                    pwaits.append((ost[par], 16 * st_count[par]))
                S.add("vector", lambda v, par=par, cb=cb: v.tensor_copy(
                    ob[par][:, 0:T], cb[:, 0:T]), waits=pwaits, inc=dvc)
                cb_free[i % 2] = (dvc, dvc.v)
                out_ready = (dvc, dvc.v)

            rows = 64 if kind == "b" else 128

            def store(sy, oap=oap, oc=oc, par=par, rows=rows):
                dst = bass.AP(oap.tensor, oap.offset + oc * 128 * T, [[T, rows], [1, T]])
                return sy.dma_start(dst, ob[par][0:rows, 0:T])
            S.add("sync", store, waits=[out_ready], inc=ost[par], inc_n=16)
            st_count[par] += 1

        for w_, f_, c_ in deferred_pe:
            S.add_raw("tensor", f_, waits=w_, inc=c_)
        S.add("sync", lambda sy: sy.nop(),
              waits=[(ost[0], 16 * st_count[0]), (ost[1], 16 * st_count[1])])
        S.emit(blk)

def emit_scan(nc, aps, NH=16, T=4096, NHORN=5, tag="s2", max_stage=99, ncc_cap=None,
              sub_stage=99):
    H, R, D = NH, 4, 128
    NT = T * R
    C = 128
    CT = C // R
    NCC = NT // C
    SCG = 4
    TSC = CT * SCG
    qai, kai, vai, gdi, bsgi = (aps[s] for s in ("q", "k", "v", "gd", "bsg"))
    gsig, ldf, mlog, osc = aps["gsig"], aps["ldf"], aps["mlog"], aps["osc"]
    wmxs = aps["wmxs"]
    msku = aps["masku"]
    pall = aps["pall"]
    i4t = aps["i4t"]
    onw = aps["onw"]

    from contextlib import ExitStack
    es = ExitStack()
    sb = lambda n, sh, dt: es.enter_context(nc.sbuf_tensor(f"{tag}{n}", sh, dt))
    ps = lambda n, sh, dt: es.enter_context(nc.psum_tensor(f"{tag}{n}", sh, dt))
    kslab = sb("kslab", [128, 64 * TSC], BF16)
    vslab = sb("vslab", [128, 64 * TSC], BF16)
    qslab = sb("qslab", [128, 16 * TSC], BF16)
    gdslab = sb("gdslab", [128, 16 * (TSC // 2)], F32)
    Ssb = sb("S", [128, 2048], F32)
    Sbf = sb("Sbf", [128, 2048], BF16)
    Gf = sb("Gf", [128, 2048], F32)
    Ge = sb("Ge", [128, 16 * 129], F32)
    dA = sb("dA", [128, 2048], F32)
    kscn = sb("kscn", [128, 2048], BF16)
    vscn = sb("vscn", [128, 2048], BF16)
    EL16 = sb("EL16", [128, 2048], BF16)
    ELI = sb("ELI", [128, 2048], BF16)
    f_kl16 = sb("fkl16", [128, 2048], BF16)
    f_kl32 = sb("fkl32", [128, 2048], BF16)
    f_kl64 = sb("fkl64", [128, 2048], BF16)
    f_ks = sb("fks", [128, 2048], BF16)
    f_kli = sb("fkli", [128, 2048], BF16)
    f_kr16 = sb("fkr16", [128, 2048], BF16)
    f_kr32 = sb("fkr32", [128, 2048], BF16)
    f_kr64 = sb("fkr64", [128, 2048], BF16)
    f_kg = sb("fkg", [128, 2048], BF16)
    f_ql16 = sb("fql16", [128, 2048], BF16)
    f_ql32 = sb("fql32", [128, 2048], BF16)
    f_ql64 = sb("fql64", [128, 2048], BF16)
    f_qs = sb("fqs", [128, 2048], BF16)
    pc16 = sb("pc16", [128, 128], F32)
    pcr = sb("pcr", [128, 7 * 128], F32)
    TTm = sb("TT", [128, 2048], BF16)
    BTm = sb("BT", [128, 2048], BF16)
    Vtok = sb("Vtok", [128, 2048], BF16)
    KGt = sb("KGt", [128, 2048], BF16)
    Rsb = sb("R", [128, 2048], BF16)
    Xsb = sb("X", [128, 2048], BF16)
    Osb = sb("Osb", [128, 2048], BF16)
    o1 = sb("o1", [32, 2048], BF16)
    rr16 = sb("rr16", [32, 16], F32)
    ofeat = sb("ofeat", [128, 512], BF16)
    onesf = sb("onesf", [128, 1], F32)
    cbank = sb("cbank", [128, 4], F32)
    gsl = sb("gsl", [128, 512], BF16)
    bscn = sb("bscn", [16, 128], BF16)
    bsrt = sb("bsrt", [16, 128], BF16)
    msksu = sb("msksu", [128, 5 * 2048], mybir.dt.uint8)
    onehs = sb("onehs", [16, 2048], BF16)
    palls = sb("palls", [128, 32], BF16)
    i4ts = sb("i4ts", [4, 128], BF16)
    mixm = sb("mixm", [128, 16 * 32], BF16)
    wcol = sb("wcol", [128, 16], F32)
    mls = sb("mls", [16, 4], F32)
    wmx = sb("wmx", [16, 4], F32)
    wmxT = sb("wmxT", [4, 16], BF16)
    ldfs = sb("ldfs", [128, 16], F32)
    onwc = sb("onwc", [128, 1], F32)
    ident = sb("ident", [128, 128], BF16)
    ECt = sb("ECt", [128, 16], F32)
    trow = sb("trow", [16, 8], F32)
    psGA = ps("psGA", [128, 2048], F32)
    psT = ps("psT", [128, 2048], BF16)
    psM = ps("psM", [32, 1024], F32)

    with es, nc.Block() as blk:
        S = Sched(nc, tag)
        dvc = S.ctr("dvc")
        gpc = S.ctr("gpc")
        acc = S.ctr("acc")
        pec = S.ctr("pec")
        dld = S.ctr("dld")
        cl0 = S.ctr("cl0")
        bld = S.ctr("bld")
        gld = S.ctr("gld")
        wst = S.ctr("wst")
        wst2 = S.ctr("wst2")
        ost = S.ctr("ost")

        def dve(fn, waits=(), tail=None):
            w = list(waits) + [(dvc, dvc.v)]
            S.add("vector", fn, waits=w, inc=tail or dvc)

        def gpo(fn, waits=(), tail=None):
            w = list(waits) + [(gpc, gpc.v)]
            S.add("gpsimd", fn, waits=w, inc=tail or gpc)

        def act(fn, waits=(), tail=None):
            w = list(waits) + [(acc, acc.v)]
            S.add("scalar", fn, waits=w, inc=tail or acc)

        S.add("sync", lambda s: s.dma_start(msksu[:], bass.AP(
            msku.tensor, msku.offset, [[2048, 128], [2048 * 128, 5], [1, 2048]])),
            inc=cl0, inc_n=16)
        S.add("sync", lambda s: s.dma_start(onehs[:], bass.AP(
            aps["oneh16"].tensor, aps["oneh16"].offset, [[2048, 16], [1, 2048]])),
            inc=cl0, inc_n=16)
        S.add("sync", lambda s: s.dma_start(palls[:], bass.AP(
            pall.tensor, pall.offset, [[32, 128], [1, 32]])), inc=cl0, inc_n=16)
        S.add("sync", lambda s: s.dma_start(i4ts[:], bass.AP(
            i4t.tensor, i4t.offset, [[128, 4], [1, 128]])), inc=cl0, inc_n=16)
        S.add("sync", lambda s: s.dma_start(ldfs[:], bass.AP(
            ldf.tensor, ldf.offset, [[16, 128], [1, 16]])), inc=cl0, inc_n=16)
        S.add("sync", lambda s: s.dma_start(mls[:], bass.AP(
            mlog.tensor, mlog.offset, [[4, 16], [1, 4]])), inc=cl0, inc_n=16)
        S.add("sync", lambda s: s.dma_start(onwc[:], bass.AP(
            onw.tensor, onw.offset, [[1, 128], [1, 1]])), inc=cl0, inc_n=16)
        S.add("sync", lambda s: s.dma_start(ident[:], bass.AP(
            aps["ident"].tensor, aps["ident"].offset, [[128, 128], [1, 128]])),
            inc=cl0, inc_n=16)
        CL = cl0.v

        dve(lambda v: v.memset(Ssb[:], 0.0), waits=[])
        dve(lambda v: v.memset(Sbf[:], 0.0))
        dve(lambda v: v.memset(Ge[:], 0.0))
        dve(lambda v: v.tensor_copy(
            bass.AP(Gf, 1, [[2048, 128], [128, 16], [4, 32], [1, 3]]),
            bass.AP(ldfs, 0, [[16, 128], [1, 16], [0, 32], [0, 3]])),
            waits=[(cl0, CL)])
        dve(lambda v: v.tensor_reduce(trow[:, 0:1], mls[:], AX.X, ALU.max,
                                      negate=True), waits=[(cl0, CL)])
        act(lambda s: s.activation(wmx[:], mls[:], AF.Exp, bias=trow[:, 0:1]),
            waits=[(dvc, dvc.v)])
        dve(lambda v: v.tensor_reduce(trow[:, 1:2], wmx[:], AX.X, ALU.add),
            waits=[(acc, acc.v)])
        dve(lambda v: v.reciprocal(trow[:, 2:3], trow[:, 1:2]))
        dve(lambda v: v.tensor_scalar(wmx[:], wmx[:], trow[:, 2:3], None, ALU.mult))
        S.add("sync", lambda s: s.dma_start(
            bass.AP(wmxs.tensor, wmxs.offset, [[4, 16], [1, 4]]), wmx[:]),
            waits=[(dvc, dvc.v)], inc=wst, inc_n=16)
        S.add("gpsimd", lambda g: g.dma_start(
            wmxT[:], bass.AP(wmxs.tensor, wmxs.offset, [[1, 4], [4, 16]]),
            allow_slow_non_contiguous=True),
            waits=[(wst, 16)], inc=wst2, inc_n=16)
        S.add("tensor", lambda t: t.matmul(psGA[:, 0:16], i4ts[:], wmxT[:],
                                           start=True, stop=True),
              waits=[(wst2, 16), (cl0, CL)], inc=pec)
        dve(lambda v: v.tensor_copy(wcol[:], psGA[:, 0:16]), waits=[(pec, pec.v)])
        dve(lambda v: v.tensor_mul(
            mixm[:], bass.AP(palls, 0, [[32, 128], [0, 16], [1, 32]]),
            bass.AP(wcol, 0, [[16, 128], [1, 16], [0, 32]])))
        dve(lambda v: v.memset(onesf[:], 1.0))
        dve(lambda v: v.memset(cbank[:, 0:1], 1e-5))
        mix_ready = dvc.v

        prev_s_done = 0
        prev_gf = 0
        NCCr = min(NCC, ncc_cap) if ncc_cap else NCC
        for cc in range(NCCr):
            sg, tc = cc // SCG, cc % SCG
            if tc == 0:
                w = [(dvc, prev_s_done)] if cc > 0 else [(dvc, mix_ready)]
                for part in range(4):
                    S.add("sync", lambda s, sg=sg, part=part: s.dma_start(
                        bass.AP(kslab, part * 16 * TSC,
                                [[64 * TSC, 128], [TSC, 16], [1, TSC]]),
                        bass.AP(kai.tensor, kai.offset + sg * TSC + part * 16 * 128 * T,
                                [[T, 128], [128 * T, 16], [1, TSC]])),
                        waits=(w if part == 0 else ()), inc=dld, inc_n=16)
                    S.add("sync", lambda s, sg=sg, part=part: s.dma_start(
                        bass.AP(vslab, part * 16 * TSC,
                                [[64 * TSC, 128], [TSC, 16], [1, TSC]]),
                        bass.AP(vai.tensor, vai.offset + sg * TSC + part * 16 * 128 * T,
                                [[T, 128], [128 * T, 16], [1, TSC]])),
                        inc=dld, inc_n=16)
                for part in range(2):
                    S.add("sync", lambda s, sg=sg, part=part: s.dma_start(
                        bass.AP(qslab, part * 8 * TSC,
                                [[16 * TSC, 128], [TSC, 8], [1, TSC]]),
                        bass.AP(qai.tensor, qai.offset + sg * TSC + part * 8 * 128 * T,
                                [[T, 128], [128 * T, 8], [1, TSC]])),
                        inc=dld, inc_n=16)

            if cc % 2 == 0:
                wgd = [(dvc, prev_gf)] if cc > 0 else []
                for part in range(2):
                    S.add("sync", lambda s, cc=cc, part=part: s.dma_start(
                        bass.AP(gdslab, part * 8 * (TSC // 2),
                                [[16 * (TSC // 2), 128], [TSC // 2, 8], [1, TSC // 2]]),
                        bass.AP(gdi.tensor, gdi.offset + cc * CT + part * 8 * 128 * T,
                                [[T, 128], [128 * T, 8], [1, TSC // 2]])),
                        waits=(wgd if part == 0 else ()), inc=dld, inc_n=16)
            SL = dld.v
            S.add("sync", lambda g, cc=cc: g.dma_start(bsrt[:], bass.AP(
                bsgi.tensor, bsgi.offset + cc * CT,
                [[4 * T, 16], [T, 4], [1, CT]])),
                waits=[(dvc, dvc.v)], inc=bld, inc_n=16)
            BL = bld.v
            for part in range(4):
                S.add("sync", lambda g, cc=cc, part=part: g.dma_start(
                    bass.AP(gsl, part * 4 * CT, [[16 * CT, 128], [CT, 4], [1, CT]]),
                    bass.AP(gsig.tensor, gsig.offset + cc * CT + part * 4 * 128 * T,
                            [[T, 128], [128 * T, 4], [1, CT]])),
                    waits=([(dvc, dvc.v)] if part == 0 else ()), inc=gld, inc_n=16)
            GL = gld.v

            kview = bass.AP(kslab, tc * CT,
                            [[64 * TSC, 128], [4 * TSC, 16], [1, CT], [TSC, 4]])
            vview = bass.AP(vslab, tc * CT,
                            [[64 * TSC, 128], [4 * TSC, 16], [1, CT], [TSC, 4]])
            qview = bass.AP(qslab, tc * CT,
                            [[16 * TSC, 128], [TSC, 16], [1, CT], [0, 4]])
            gdview = bass.AP(gdslab, (cc % 2) * CT,
                             [[16 * (TSC // 2), 128], [TSC // 2, 16], [1, CT]])

            dve(lambda v, kview=kview: v.tensor_copy(kscn[:], kview),
                waits=[(dld, SL)])
            dve(lambda v, vview=vview: v.tensor_copy(vscn[:], vview),
                waits=[(dld, SL)])
            dve(lambda v, gdview=gdview: v.tensor_copy(
                bass.AP(Gf, 0, [[2048, 128], [128, 16], [4, 32]]), gdview),
                waits=[(dld, SL)])
            dve(lambda v: v.tensor_copy(
                bscn[:], bass.AP(bsrt, 0, [[128, 16], [1, 32], [32, 4]])),
                waits=[(bld, BL)])
            gf_done = dvc.v
            prev_gf = gf_done

            if max_stage < 2:
                prev_s_done = dvc.v
                continue
            ones_col = bass.AP(onesf, 0, [[1, 128], [0, 128]])
            for h in range(NH):
                fn = (lambda e, h=h: e.tensor_tensor_scan(
                    Ge[:, h * 129 + 1:h * 129 + 129],
                    Gf[:, h * 128:(h + 1) * 128], ones_col,
                    0.0, ALU.add, ALU.mult))
                dve(fn, waits=[(dvc, gf_done)])
            scans_d, scans_g = dvc.v, gpc.v

            GV = bass.AP(Ge, 1, [[2064, 128], [129, 16], [1, 128]])
            PV16 = bass.AP(Ge, 0, [[2064, 128], [129, 16], [16, 8], [0, 16]])
            gE16 = bass.AP(Ge, 16, [[2064, 128], [129, 16], [16, 8]])
            gP16 = bass.AP(Ge, 0, [[2064, 128], [129, 16], [16, 8]])
            gE32 = bass.AP(Ge, 32, [[2064, 128], [129, 16], [32, 4], [0, 2]])
            gP32 = bass.AP(Ge, 0, [[2064, 128], [129, 16], [32, 4], [0, 2]])
            gE64 = bass.AP(Ge, 64, [[2064, 128], [129, 16], [64, 2], [0, 4]])
            gP64 = bass.AP(Ge, 0, [[2064, 128], [129, 16], [64, 2], [0, 4]])
            gEC = bass.AP(Ge, 128, [[2064, 128], [129, 16], [0, 8]])

            if max_stage < 3:
                prev_s_done = dvc.v
                continue
            dve(lambda v: v.tensor_sub(dA[:], GV, PV16), waits=[(dvc, scans_d)])
            dve(lambda v: v.tensor_scalar_max(dA[:], dA[:], -80.0))
            d16_done = dvc.v
            act(lambda s: s.activation(EL16[:], dA[:], AF.Exp), waits=[(dvc, d16_done)])
            act(lambda s: s.activation(ELI[:], dA[:], AF.Exp, scale=-1.0))
            e16_done = acc.v
            prs = [(gE16, gP16), (gE32, gP16), (gE64, gP16),
                   (gP16, gP32), (gP16, gP64), (gEC, gP16), (gP16, None)]
            for ii, (a_, b_) in enumerate(prs):
                if b_ is None:
                    dve(lambda v, ii=ii, a_=a_: v.tensor_copy(
                        pcr[:, ii * 128:ii * 128 + 128], a_))
                else:
                    dve(lambda v, ii=ii, a_=a_, b_=b_: v.tensor_sub(
                        pcr[:, ii * 128:ii * 128 + 128], a_, b_))
            pr_done = dvc.v
            act(lambda s: s.activation(pcr[:], pcr[:], AF.Exp),
                waits=[(dvc, pr_done)])
            pr_exp = acc.v

            def RVi(ii):
                return bass.AP(pcr, ii * 128, [[7 * 128, 128], [8, 16], [1, 8], [0, 16]])

            if max_stage < 4:
                prev_s_done = dvc.v
                continue
            dve(lambda v: v.tensor_mul(f_kl16[:], kscn[:], EL16[:]),
                waits=[(acc, e16_done)])
            gpo(lambda g, qview=qview: g.tensor_mul(f_ql16[:], qview, EL16[:]),
                waits=[(acc, e16_done), (dld, SL)])
            for h in range(NH):
                S.add("tensor", lambda t, h=h: t.matmul(
                    psGA[:, h * 128:(h + 1) * 128], onehs[:, h * 128:(h + 1) * 128],
                    bscn[:], start=True, stop=True),
                    waits=([(dvc, gf_done), (cl0, CL), (pec, pec.v)] if h == 0 else []),
                    inc=(pec if h == NH - 1 else None))
            bb_mm = pec.v
            dve(lambda v: v.scalar_tensor_tensor(f_kli[:], psGA[:], 1.0, kscn[:],
                                                 ALU.mult, ALU.mult),
                waits=[(pec, bb_mm)])
            dve(lambda v: v.tensor_mul(f_kli[:], f_kli[:], ELI[:]))
            kli_done = dvc.v
            dve(lambda v: v.tensor_mul(f_kr16[:], f_kli[:], RVi(0)),
                waits=[(acc, pr_exp)])
            dve(lambda v: v.tensor_mul(f_kr32[:], f_kli[:], RVi(1)))
            dve(lambda v: v.tensor_mul(f_kr64[:], f_kli[:], RVi(2)))
            dve(lambda v: v.tensor_mul(f_kg[:], f_kli[:], RVi(5)))
            gpo(lambda g: g.tensor_mul(f_kl32[:], f_kl16[:], RVi(3)),
                waits=[(acc, pr_exp), (dvc, dvc.v)])
            gpo(lambda g: g.tensor_mul(f_kl64[:], f_kl16[:], RVi(4)))
            gpo(lambda g: g.tensor_mul(f_ql32[:], f_ql16[:], RVi(3)))
            gpo(lambda g: g.tensor_mul(f_ql64[:], f_ql16[:], RVi(4)))
            gpo(lambda g: g.tensor_mul(f_ks[:], f_kl16[:], RVi(6)))
            gpo(lambda g: g.tensor_mul(f_qs[:], f_ql16[:], RVi(6)))
            fields_d, fields_g = dvc.v, gpc.v

            if max_stage < 5:
                prev_s_done = dvc.v
                continue
            pieces = [(f_kli, f_kl16, 0, TTm), (f_kr16, f_kl16, 2, TTm),
                      (f_kr32, f_kl32, 3, TTm), (f_kr64, f_kl64, 4, TTm),
                      (f_kli, f_ql16, 1, BTm), (f_kr16, f_ql16, 2, BTm),
                      (f_kr32, f_ql32, 3, BTm), (f_kr64, f_ql64, 4, BTm)]
            pv_half = {0: None, 1: None}
            for pi, (lf, rf, mi, dst) in enumerate(pieces):
                mview = msksu[:, mi * 2048:(mi + 1) * 2048]
                if pi == 0 or pi == 4:
                    dve(lambda v, dst=dst: v.memset(dst[:], 0.0))
                    md = dvc.v
                else:
                    md = None
                for half in range(2):
                    for hh2 in range(8):
                        h = half * 8 + hh2
                        w = []
                        if h == 0:
                            w = [(dvc, fields_d), (gpc, fields_g)]
                            if cc > 0 and pi == 0:
                                w.append((pec, pec.v))
                        if hh2 == 0 and pv_half[half] is not None:
                            w.append((dvc, pv_half[half]))
                        S.add("tensor", lambda t, h=h, lf=lf, rf=rf: t.matmul(
                            psGA[:, h * 128:(h + 1) * 128],
                            lf[:, h * 128:(h + 1) * 128], rf[:, h * 128:(h + 1) * 128],
                            start=True, stop=True), waits=w,
                            inc=(pec if hh2 == 7 else None))
                    pmm = pec.v
                    mw = [(pec, pmm)]
                    if md is not None:
                        mw.append((dvc, md))
                    dve(lambda v, dst=dst, mview=mview, half=half: v.copy_predicated(
                        dst[:, half * 1024:(half + 1) * 1024],
                        mview[:, half * 1024:(half + 1) * 1024],
                        psGA[:, half * 1024:(half + 1) * 1024]), waits=mw)
                    pv_half[half] = dvc.v
            tb_done = dvc.v

            if max_stage < 6:
                prev_s_done = dvc.v
                continue
            for h in range(NH):
                w = [(pec, pec.v), (dvc, tb_done)] if h == 0 else []
                S.add("tensor", lambda t, h=h: t.transpose(
                    psT[:, h * 128:(h + 1) * 128],
                    vscn[:, h * 128:(h + 1) * 128],
                    ident[:]), waits=w, inc=(pec if h == NH - 1 else None))
            vt_mm = pec.v
            dve(lambda v: v.tensor_copy(Vtok[:], psT[:]), waits=[(pec, vt_mm)])
            vt_done = dvc.v
            for h in range(NH):
                w = [(dvc, vt_done)] if h == 0 else []
                S.add("tensor", lambda t, h=h: t.transpose(
                    psT[:, h * 128:(h + 1) * 128], f_kg[:, h * 128:(h + 1) * 128],
                    ident[:]), waits=w, inc=(pec if h == NH - 1 else None))
            kg_mm = pec.v
            dve(lambda v: v.tensor_copy(KGt[:], psT[:]), waits=[(pec, kg_mm)])
            kg_done = dvc.v

            if max_stage < 7:
                prev_s_done = dvc.v
                continue
            for h in range(NH):
                w = [(dvc, kg_done)] if h == 0 else []
                S.add("tensor", lambda t, h=h: t.matmul(
                    psGA[:, h * 128:(h + 1) * 128],
                    f_ks[:, h * 128:(h + 1) * 128], Sbf[:, h * 128:(h + 1) * 128],
                    start=True, stop=True),
                    waits=w, inc=(pec if h == NH - 1 else None))
            c_mm = pec.v
            dve(lambda v: v.scalar_tensor_tensor(
                Rsb[:], psGA[:], -1.0, Vtok[:], ALU.mult, ALU.add), waits=[(pec, c_mm)])
            dve(lambda v: v.tensor_copy(Xsb[:], Rsb[:]))
            r_done = dvc.v

            if max_stage < 8:
                prev_s_done = dvc.v
                continue
            xh = {0: r_done, 1: r_done}
            for it in range(NHORN):
                for half in range(2):
                    for hh2 in range(8):
                        h = half * 8 + hh2
                        w = [(dvc, xh[half])] if hh2 == 0 else []
                        S.add("tensor", lambda t, h=h: t.matmul(
                            psGA[:, h * 128:(h + 1) * 128],
                            TTm[:, h * 128:(h + 1) * 128],
                            Xsb[:, h * 128:(h + 1) * 128], start=True, stop=True),
                            waits=w, inc=(pec if hh2 == 7 else None))
                    hm = pec.v
                    dve(lambda v, half=half: v.scalar_tensor_tensor(
                        Xsb[:, half * 1024:(half + 1) * 1024],
                        psGA[:, half * 1024:(half + 1) * 1024], -1.0,
                        Rsb[:, half * 1024:(half + 1) * 1024], ALU.mult, ALU.add),
                        waits=[(pec, hm)])
                    xh[half] = dvc.v
            xprev = dvc.v

            if max_stage < 9:
                prev_s_done = dvc.v
                continue
            for h in range(NH):
                w = [(dvc, xprev)] if h == 0 else []
                S.add("tensor", lambda t, h=h: (
                    t.matmul(psGA[:, h * 128:(h + 1) * 128],
                             f_qs[:, h * 128:(h + 1) * 128],
                             Sbf[:, h * 128:(h + 1) * 128],
                             start=True, stop=False),
                    t.matmul(psGA[:, h * 128:(h + 1) * 128],
                             BTm[:, h * 128:(h + 1) * 128],
                             Xsb[:, h * 128:(h + 1) * 128],
                             start=False, stop=True))[-1],
                    waits=w, inc=(pec if h == NH - 1 else None))
            o_mm = pec.v
            dve(lambda v: v.tensor_copy(Osb[:], psGA[:]),
                waits=[(pec, o_mm)])
            o_done = dvc.v

            if max_stage < 10:
                prev_s_done = dvc.v
                continue
            for h in range(NH):
                w = [(dvc, o_done)] if h == 0 else []
                S.add("tensor", lambda t, h=h: t.matmul(
                    psGA[:, h * 128:(h + 1) * 128],
                    KGt[:, h * 128:(h + 1) * 128],
                    Xsb[:, h * 128:(h + 1) * 128], start=True, stop=True),
                    waits=w, inc=(pec if h == NH - 1 else None))
            su_mm = pec.v
            ECexp = bass.AP(Ge, 128, [[2064, 128], [129, 16], [0, 128]])
            gpo(lambda g: g.tensor_copy(pc16[:, 0:16],
                                        bass.AP(Ge, 128, [[2064, 128], [129, 16]])),
                waits=[(dvc, o_done)])
            S.add("scalar", lambda s: s.activation(ECt[:], pc16[:, 0:16], AF.Exp),
                  waits=[(gpc, gpc.v), (acc, acc.v)], inc=acc)
            ec_done = acc.v
            gpo(lambda g: g.tensor_mul(
                Ssb[:], Ssb[:], bass.AP(ECt, 0, [[16, 128], [1, 16], [0, 128]])),
                waits=[(acc, ec_done)])
            sdec = gpc.v
            dve(lambda v: v.tensor_add(Ssb[:], Ssb[:], psGA[:]),
                waits=[(gpc, sdec), (pec, su_mm)])
            dve(lambda v: v.tensor_copy(Sbf[:], Ssb[:]))
            s_done = dvc.v

            if max_stage < 11:
                prev_s_done = dvc.v
                continue
            for h in range(NH):
                w = [(dvc, o_done)] if h == 0 else []
                if h == 8 and sub_stage >= 1:
                    w = [(acc, sq_a), (dvc, o1a)]
                S.add("tensor", lambda t, h=h: t.matmul(
                    psM[:, (h % 8) * 128:(h % 8) * 128 + 128],
                    mixm[:, h * 32:(h + 1) * 32],
                    Osb[:, h * 128:(h + 1) * 128], start=True, stop=True),
                    waits=w, inc=(pec if h % 8 == 7 else None))
                if h == 7 and sub_stage >= 1:
                    m1 = pec.v
                    dve(lambda v: v.tensor_copy(o1[:, 0:1024], psM[:]),
                        waits=[(pec, m1)])
                    o1a = dvc.v
                    act(lambda s: s.activation(dA[0:32, 0:1024], o1[:, 0:1024],
                                               AF.Square), waits=[(dvc, o1a)])
                    sq_a = acc.v
                if h == NH - 1 and sub_stage < 1:
                    prev_s_done = dvc.v
            if sub_stage < 1:
                continue
            m2 = pec.v
            dve(lambda v: v.tensor_copy(o1[:, 1024:2048], psM[:]),
                waits=[(pec, m2)])
            o1_done = dvc.v
            act(lambda s: s.activation(dA[0:32, 1024:2048], o1[:, 1024:2048],
                                       AF.Square), waits=[(dvc, o1_done)])
            sq_b = acc.v
            if sub_stage < 2:
                prev_s_done = dvc.v
                continue
            dve(lambda v: v.tensor_reduce(
                rr16[:, 0:16], bass.AP(dA, 0, [[2048, 32], [128, 16], [1, 128]]),
                AX.X, ALU.add), waits=[(acc, sq_b)])
            act(lambda s: s.activation(rr16[:, 0:16], rr16[:, 0:16], AF.Sqrt,
                                       scale=1.0 / 128, bias=cbank[0:32, 0:1]),
                waits=[(dvc, dvc.v)])
            dve(lambda v: v.reciprocal(rr16[:, 0:16], rr16[:, 0:16]),
                waits=[(acc, acc.v)])
            dve(lambda v: v.tensor_mul(
                o1[:], o1[:], bass.AP(rr16, 0, [[16, 32], [1, 16], [0, 128]])))
            on_done = dvc.v
            if sub_stage < 3:
                prev_s_done = dvc.v
                continue
            for h in range(NH):
                w = [(dvc, on_done)] if h == 0 else []
                S.add("tensor", lambda t, h=h: t.transpose(
                    psT[:, h * 32:(h + 1) * 32], o1[0:32, h * 128:(h + 1) * 128],
                    ident[0:32, 0:32]), waits=w, inc=(pec if h == NH - 1 else None))
            ot_mm = pec.v
            dve(lambda v: v.scalar_tensor_tensor(
                ofeat[:], psT[:, 0:512], onwc[:, 0:1], gsl[:], ALU.mult, ALU.mult),
                waits=[(pec, ot_mm), (gld, GL), (ost, ost.v)])
            of_done = dvc.v
            if sub_stage < 4:
                prev_s_done = dvc.v
                continue
            S.add("sync", lambda s, cc=cc: s.dma_start(
                bass.AP(osc.tensor, osc.offset + cc * CT,
                        [[T, 128], [128 * T, 16], [1, CT]]), ofeat[:]),
                waits=[(dvc, of_done)], inc=ost, inc_n=16)
            prev_s_done = s_done

        S.add("sync", lambda s: s.nop(), waits=[(ost, ost.v)])
        S.emit(blk)

def emit_l1bg(nc, aps, NH=16, T=4096, tag="cb"):
    D = NH * 128
    NTB = T // 512
    f1s, g1s, f2wT, g2wT = (aps[s] for s in ("f1s", "g1s", "f2wT", "g2wT"))
    dtb, alog, g2b, gd, gsig, ldf = (aps[s] for s in
                                     ("dtb", "alog", "g2b", "gd", "gsig", "ldf"))
    from contextlib import ExitStack
    es = ExitStack()
    sb = lambda n, sh, dt: es.enter_context(nc.sbuf_tensor(f"{tag}{n}", sh, dt))
    ps = lambda n, sh, dt: es.enter_context(nc.psum_tensor(f"{tag}{n}", sh, dt))
    f1o = sb("f1o", [128, T], BF16)
    g1o = sb("g1o", [128, T], BF16)
    f2s = sb("f2s", [128, D], BF16)
    g2s = sb("g2s", [128, D], BF16)
    dtbs = sb("dtbs", [128, NH], F32)
    g2bs = sb("g2bs", [128, NH], F32)
    abc = sb("abc", [128, NH], F32)
    arow = sb("arow", [1, NH], F32)
    cb = sb("cbk", [128, 2], F32)
    hb = sb("hb", [128, 512], F32)
    eb = sb("eb", [128, 512], F32)
    rb = sb("rb", [128, 512], F32)
    gdb = sb("gdb", [128, T], F32)
    gsb = sb("gsb", [128, T], BF16)
    lf1 = sb("lf1", [128, NH], F32)
    lf2 = sb("lf2", [128, NH], F32)
    ones1 = sb("ones1", [1, 128], F32)
    psum = [ps(f"p{j}", [128, 512], F32) for j in range(4)]
    with es, nc.Block() as blk:
        S = Sched(nc, tag)
        ld = S.ctr("ld")
        mm = S.ctr("mm")
        dvc = S.ctr("dvc")
        acc = S.ctr("acc")
        pec = S.ctr("pec")
        ost = S.ctr("ost")
        ostg = S.ctr("ostg")
        osts = S.ctr("osts")

        S.add("gpsimd", lambda g: g.dma_start(f1o[:], bass.AP(
            f1s.tensor, f1s.offset, [[T, 128], [1, T]])), inc=ld, inc_n=16)
        S.add("gpsimd", lambda g: g.dma_start(g1o[:], bass.AP(
            g1s.tensor, g1s.offset, [[T, 128], [1, T]])), inc=ld, inc_n=16)
        S.add("gpsimd", lambda g: g.dma_start(f2s[:], bass.AP(
            f2wT.tensor, f2wT.offset, [[D, 128], [1, D]])), inc=ld, inc_n=16)
        S.add("gpsimd", lambda g: g.dma_start(g2s[:], bass.AP(
            g2wT.tensor, g2wT.offset, [[D, 128], [1, D]])), inc=ld, inc_n=16)
        S.add("gpsimd", lambda g: g.dma_start(dtbs[:], bass.AP(
            dtb.tensor, dtb.offset, [[NH, 128], [1, NH]])), inc=ld, inc_n=16)
        S.add("gpsimd", lambda g: g.dma_start(g2bs[:], bass.AP(
            g2b.tensor, g2b.offset, [[NH, 128], [1, NH]])), inc=ld, inc_n=16)
        S.add("gpsimd", lambda g: g.dma_start(arow[:], bass.AP(
            alog.tensor, alog.offset, [[NH, 1], [1, NH]])), inc=ld, inc_n=16)
        LD = ld.v
        S.add("vector", lambda v: v.memset(ones1[:], 1.0), inc=dvc)
        S.add("vector", lambda v: v.memset(cb[:, 0:1], 1.0), waits=[(dvc, 1)], inc=dvc)
        S.add("vector", lambda v: v.memset(cb[:, 1:2], -10000.0),
              waits=[(dvc, 2)], inc=dvc)
        S.add("scalar", lambda s: s.activation(arow[:], arow[:], AF.Exp),
              waits=[(ld, LD)], inc=acc)
        S.add("tensor", lambda t: t.matmul(psum[0][:, 0:NH], ones1[:], arow[:],
                                           start=True, stop=True),
              waits=[(acc, 1), (dvc, 3)], inc=pec)
        S.add("vector", lambda v: v.tensor_copy(abc[:], psum[0][:, 0:NH]),
              waits=[(pec, 1), (dvc, 3)], inc=dvc)
        ab_done = dvc.v

        relu_acc = {}
        sig_acc = {}

        def dve(fn, waits=()):
            S.add("vector", fn, waits=list(waits) + [(dvc, dvc.v)], inc=dvc)

        def act(fn, waits=()):
            S.add("scalar", fn, waits=list(waits) + [(acc, acc.v)], inc=acc)

        act(lambda s: s.activation(lf1[:], dtbs[:], AF.Abs, bias=cb[:, 1:2]),
            waits=[(ld, LD), (dvc, ab_done)])
        act(lambda s: s.activation(lf1[:], lf1[:], AF.Exp, scale=-1.0))
        act(lambda s: s.activation(lf1[:], lf1[:], AF.Ln, bias=cb[:, 0:1]))
        act(lambda s: s.activation(lf2[:], dtbs[:], AF.Relu, bias=cb[:, 1:2]))
        lfa = acc.v
        dve(lambda v: v.tensor_add(lf1[:], lf1[:], lf2[:]), waits=[(acc, lfa)])
        dve(lambda v: v.scalar_tensor_tensor(lf1[:], lf1[:], -1.0, abc[:],
                                             ALU.mult, ALU.mult))
        lf_done = dvc.v
        S.add("sync", lambda s: s.dma_start(bass.AP(
            ldf.tensor, ldf.offset, [[NH, 128], [1, NH]]), lf1[:]),
            waits=[(dvc, lf_done)], inc=ost, inc_n=16)

        for c in range(NH):
            for tb in range(NTB):
                n = c * NTB + tb
                w = [(ld, LD), (dvc, ab_done)] if n == 0 else []
                if n >= 2:
                    w.append((acc, relu_acc[n - 2]))
                S.add("tensor", lambda t, c=c, tb=tb, n=n: t.matmul(
                    psum[n % 2][:], f2s[:, c * 128:(c + 1) * 128],
                    f1o[:, tb * 512:(tb + 1) * 512], start=True, stop=True),
                    waits=w, inc=pec)
                pm = pec.v
                act(lambda s, c=c, tb=tb, n=n: s.activation(
                    hb[:], psum[n % 2][:], AF.Abs, bias=dtbs[:, c:c + 1]),
                    waits=[(pec, pm)] + ([(dvc, dvc.v)] if True else []))
                act(lambda s: s.activation(eb[:], hb[:], AF.Exp, scale=-1.0))
                act(lambda s: s.activation(eb[:], eb[:], AF.Ln, bias=cb[:, 0:1]))
                act(lambda s, c=c, n=n: s.activation(
                    rb[:], psum[n % 2][:], AF.Relu, bias=dtbs[:, c:c + 1]))
                relu_acc[n] = acc.v
                aa = acc.v
                dve(lambda v: v.tensor_add(rb[:], rb[:], eb[:]), waits=[(acc, aa)])
                dve(lambda v, c=c, tb=tb: v.scalar_tensor_tensor(
                    gdb[:, tb * 512:(tb + 1) * 512], rb[:], -1.0,
                    bass.AP(abc, c, [[NH, 128], [0, 512]]), ALU.mult, ALU.mult))
            gdc = dvc.v
            S.add("sync", lambda s, c=c: s.dma_start(
                bass.AP(gd.tensor, gd.offset + c * 128 * T, [[T, 128], [1, T]]),
                gdb[:]), waits=[(dvc, gdc)], inc=ostg, inc_n=16)
            gd_st = ostg.v
            for tb in range(NTB):
                n = c * NTB + tb
                S.add("tensor", lambda t, c=c, tb=tb, n=n: t.matmul(
                    psum[2 + n % 2][:], g2s[:, c * 128:(c + 1) * 128],
                    g1o[:, tb * 512:(tb + 1) * 512], start=True, stop=True),
                    waits=([(acc, sig_acc[n - 2])] if n >= 2 else []), inc=pec)
                pm = pec.v
                act(lambda s, c=c, tb=tb, n=n: s.activation(
                    gsb[:, tb * 512:(tb + 1) * 512], psum[2 + n % 2][:],
                    AF.Sigmoid, bias=g2bs[:, c:c + 1]),
                    waits=[(pec, pm)] + ([(osts, gs_st)] if c > 0 and tb == 0 else []))
                sig_acc[n] = acc.v
            gsc = acc.v
            S.add("sync", lambda s, c=c: s.dma_start(
                bass.AP(gsig.tensor, gsig.offset + c * 128 * T, [[T, 128], [1, T]]),
                gsb[:]), waits=[(acc, gsc)], inc=osts, inc_n=16)
            gs_st = osts.v
            dve(lambda v: v.engine_nop(), waits=[(ostg, gd_st)])
        S.add("sync", lambda s: s.nop(),
              waits=[(ost, ost.v), (ostg, ostg.v), (osts, osts.v)])
        S.emit(blk)

def host_prep(ins, T=4096, NH=16):
    D = NH * 128
    p = {}
    ct = lambda a: np.ascontiguousarray(a)
    perm = np.concatenate([np.arange(0, 128, 2), np.arange(1, 128, 2)])
    Wq, Wk, Wv = ins["L0_attn_Wq"], ins["L0_attn_Wk"], ins["L0_attn_Wv"]
    p["wqT"] = ct(np.concatenate([Wq[:, h * 128 + perm] for h in range(NH)], axis=1))
    p["wkT"] = ct(Wk[:, perm])
    p["wvT"] = ct(Wv)
    inv = 1.0 / (10000.0 ** (np.arange(0, 128, 2, dtype=np.float32) / 128))
    f = np.outer(np.arange(T, dtype=np.float32), inv)
    p["cosk"] = ct(np.cos(f).T.astype(np.float32))
    p["sink"] = ct(np.sin(f).T.astype(np.float32))
    sc = np.float32(1.0 / np.sqrt(128.0))
    p["cosq"] = ct(p["cosk"] * sc)
    p["sinq"] = ct(p["sink"] * sc)
    cm = np.zeros((128, 4 * 512), dtype=np.float32)
    for o in range(4):
        for kk in range(128):
            cm[kk, o * 512 + o * 128 + kk:(o + 1) * 512] = 1.0
    p["cmask"] = cm.astype(BFNP)
    p["ident"] = np.eye(128, dtype=np.float32).astype(BFNP)
    for L in (0, 1):
        p[f"gwT{L}"] = ct(ins[f"L{L}_ffn_gate_w"].T)
        p[f"uwT{L}"] = ct(ins[f"L{L}_ffn_up_w"].T)
        p[f"dwT{L}"] = ct(ins[f"L{L}_ffn_down_w"].T)
        p[f"gs{L}"] = ct(ins[f"L{L}_ffn_gate_s"].reshape(-1, 128).T)
        p[f"us{L}"] = ct(ins[f"L{L}_ffn_up_s"].reshape(-1, 128).T)
        p[f"ds{L}"] = ct(ins[f"L{L}_ffn_down_s"].reshape(-1, 128).T)
    p["qw"] = ct(ins["L1_attn_q_proj_w"].T)
    p["kw"] = ct(ins["L1_attn_k_proj_w"].T)
    p["vw"] = ct(ins["L1_attn_v_proj_w"].T)
    p["bw"] = ct(ins["L1_attn_b_proj_w"].T)
    p["f1w"] = ct(ins["L1_attn_f1_w"].T)
    p["g1w"] = ct(ins["L1_attn_g1_w"].T)
    cv = np.concatenate([ins["L1_attn_q_conv_w"], ins["L1_attn_k_conv_w"],
                         ins["L1_attn_v_conv_w"]], axis=0)
    ncv = cv.shape[0] // 128
    p["cvw"] = ct(cv.reshape(ncv, 128, 4).transpose(1, 0, 2).reshape(128, ncv * 4))
    p["f2wT"] = ct(ins["L1_attn_f2_w"].T)
    p["g2wT"] = ct(ins["L1_attn_g2_w"].T)
    p["dtb"] = ct(ins["L1_attn_dt_bias"].reshape(NH, 128).T)
    p["g2b"] = ct(ins["L1_attn_g2_b"].reshape(NH, 128).T)
    p["alog"] = ct(ins["L1_attn_A_log"].reshape(1, NH))
    p["mlog"] = ct(ins["L1_attn_micro_logits"])
    p["onw"] = ct(ins["L1_attn_o_norm_w"].reshape(128, 1))
    p["opT"] = ct(ins["L1_attn_o_proj_w"].T)
    p["dsone"] = np.ones((128, NH), dtype=np.float32)
    mk = np.zeros((5, 128, 128), np.float32)
    mi = np.arange(128)[:, None]
    mj = np.arange(128)[None, :]
    mk[0] = (mi < mj) & (mi // 16 == mj // 16)
    mk[1] = (mi <= mj) & (mi // 16 == mj // 16)
    mk[2] = (mi // 16 + 1 == mj // 16) & (mi // 32 == mj // 32)
    mk[3] = (mi // 32 + 1 == mj // 32) & (mi // 64 == mj // 64)
    mk[4] = (mi // 64 == 0) & (mj // 64 == 1)
    p["masku"] = np.tile(mk, (1, 1, NH)).reshape(5 * 128, NH * 128).astype(np.uint8)
    oneh = np.zeros((NH, NH * 128), np.float32)
    for h in range(NH):
        oneh[h, h * 128:(h + 1) * 128] = 1.0
    p["oneh16"] = oneh.astype(BFNP)
    pall = np.zeros((128, 32), np.float32)
    pall[np.arange(128), np.arange(128) // 4] = 1.0
    p["pall"] = pall.astype(BFNP)
    i4 = np.zeros((4, 128), np.float32)
    i4[np.arange(128) % 4, np.arange(128)] = 1.0
    p["i4t"] = i4.astype(BFNP)
    return p


def _d(nc, name, arr_shape, dt, kind):
    return nc.dram_tensor(name, list(arr_shape), dt, kind=kind).ap()


def _recycled(f, nc, *args, **kw):
    mark = len(CTR_LOG)
    f(nc, *args, **kw)
    nc.clear_and_free_semaphores([c.h for c in CTR_LOG[mark:]])


def build_A(T, NH=16):
    D = NH * 128
    nc = bass.Bass("TRN2", target_bir_lowering=False)
    aps = {
        "x": _d(nc, "x", (D, T), F32, "ExternalInput"),
        "xn": _d(nc, "xn0", (D, T), BF16, "ExternalOutput"),
        "wqT": _d(nc, "wqT", (D, NH * 128), F32, "ExternalInput"),
        "wkT": _d(nc, "wkT", (D, 128), F32, "ExternalInput"),
        "wvT": _d(nc, "wvT", (D, 128), F32, "ExternalInput"),
        "cosq": _d(nc, "cosq", (64, T), F32, "ExternalInput"),
        "sinq": _d(nc, "sinq", (64, T), F32, "ExternalInput"),
        "cosk": _d(nc, "cosk", (64, T), F32, "ExternalInput"),
        "sink": _d(nc, "sink", (64, T), F32, "ExternalInput"),
        "cmask": _d(nc, "cmask", (128, 2048), BF16, "ExternalInput"),
        "ident": _d(nc, "ident", (128, 128), BF16, "ExternalInput"),
        "x1p": _d(nc, "x1p", (D, T), F32, "ExternalOutput"),
    }
    emit_ln(nc, {"x": aps["x"], "xn": aps["xn"]}, D=D, T=T, tag="ln0")
    _recycled(emit_attn0, nc, aps, NH=NH, T=T, tag="a0")
    return nc


def build_ffn(T, L, D=2048, I=5632):
    nc = bass.Bass("TRN2", target_bir_lowering=False)
    aps = {
        "xn": _d(nc, "xn", (D, T), BF16, "ExternalInput"),
        "resid": _d(nc, "resid", (D, T), F32, "ExternalInput"),
        "gwT": _d(nc, "gwT", (D, I), F32, "ExternalInput"),
        "uwT": _d(nc, "uwT", (D, I), F32, "ExternalInput"),
        "dwT": _d(nc, "dwT", (I, D), F32, "ExternalInput"),
        "gs": _d(nc, "gs", (128, I // 128), F32, "ExternalInput"),
        "us": _d(nc, "us", (128, I // 128), F32, "ExternalInput"),
        "ds": _d(nc, "ds", (128, D // 128), F32, "ExternalInput"),
        "hscr": _d(nc, "hscr", (I, T), BF16, "Internal"),
        "out": _d(nc, "out", (D, T), F32, "ExternalOutput"),
    }
    _recycled(emit_ffn, nc, aps, D=D, I=I, T=T, NQ=4, tag="f0")
    return nc


def build_C1(T, NH=16):
    D = NH * 128
    nc = bass.Bass("TRN2", target_bir_lowering=False)
    aps = {
        "x": _d(nc, "x1", (D, T), F32, "ExternalInput"),
        "xn": _d(nc, "xn1", (D, T), BF16, "ExternalOutput"),
        "qw": _d(nc, "qw", (D, D), F32, "ExternalInput"),
        "kw": _d(nc, "kw", (D, 4 * D), F32, "ExternalInput"),
        "vw": _d(nc, "vw", (D, 4 * D), F32, "ExternalInput"),
        "bw": _d(nc, "bw", (D, 64), F32, "ExternalInput"),
        "f1w": _d(nc, "f1w", (D, 128), F32, "ExternalInput"),
        "g1w": _d(nc, "g1w", (D, 128), F32, "ExternalInput"),
        "cvw": _d(nc, "cvw", (128, (NH + 8 * NH) * 4), F32, "ExternalInput"),
        "f2wT": _d(nc, "f2wT", (128, D), F32, "ExternalInput"),
        "g2wT": _d(nc, "g2wT", (128, D), F32, "ExternalInput"),
        "dtb": _d(nc, "dtb", (128, NH), F32, "ExternalInput"),
        "g2b": _d(nc, "g2b", (128, NH), F32, "ExternalInput"),
        "alog": _d(nc, "alog", (1, NH), F32, "ExternalInput"),
        "q": _d(nc, "q", (D, T), BF16, "ExternalOutput"),
        "k": _d(nc, "k", (4 * D, T), BF16, "ExternalOutput"),
        "v": _d(nc, "v", (4 * D, T), BF16, "ExternalOutput"),
        "bsg": _d(nc, "bsg", (64, T), BF16, "ExternalOutput"),
        "f1s": _d(nc, "f1s", (128, T), BF16, "Internal"),
        "g1s": _d(nc, "g1s", (128, T), BF16, "Internal"),
        "gd": _d(nc, "gd", (D, T), F32, "ExternalOutput"),
        "gsig": _d(nc, "gsig", (D, T), BF16, "ExternalOutput"),
        "ldf": _d(nc, "ldf", (128, NH), F32, "ExternalOutput"),
    }
    emit_ln(nc, {"x": aps["x"], "xn": aps["xn"]}, D=D, T=T, tag="ln1")
    aps["xn1"] = aps["xn"]
    _recycled(emit_l1proj, nc, aps, NH=NH, T=T, tag="c1")
    _recycled(emit_l1bg, nc, aps, NH=NH, T=T, tag="cb")
    return nc


def build_C2(T, NH=16):
    D = NH * 128
    nc = bass.Bass("TRN2", target_bir_lowering=False)
    aps = {
        "q": _d(nc, "q", (D, T), BF16, "ExternalInput"),
        "k": _d(nc, "k", (4 * D, T), BF16, "ExternalInput"),
        "v": _d(nc, "v", (4 * D, T), BF16, "ExternalInput"),
        "gd": _d(nc, "gd", (D, T), F32, "ExternalInput"),
        "bsg": _d(nc, "bsg", (64, T), BF16, "ExternalInput"),
        "gsig": _d(nc, "gsig", (D, T), BF16, "ExternalInput"),
        "ldf": _d(nc, "ldf", (128, NH), F32, "ExternalInput"),
        "mlog": _d(nc, "mlog", (NH, 4), F32, "ExternalInput"),
        "onw": _d(nc, "onw", (128, 1), F32, "ExternalInput"),
        "masku": _d(nc, "masku", (5 * 128, NH * 128), mybir.dt.uint8, "ExternalInput"),
        "oneh16": _d(nc, "oneh16", (NH, NH * 128), BF16, "ExternalInput"),
        "pall": _d(nc, "pall", (128, 32), BF16, "ExternalInput"),
        "i4t": _d(nc, "i4t", (4, 128), BF16, "ExternalInput"),
        "ident": _d(nc, "ident", (128, 128), BF16, "ExternalInput"),
        "wmxs": _d(nc, "wmxs", (NH, 4), F32, "Internal"),
        "osc": _d(nc, "osc", (D, T), BF16, "ExternalOutput"),
    }
    _recycled(emit_scan, nc, aps, NH=NH, T=T, tag="s2")
    return nc


def build_C3(T, D=2048):
    nc = bass.Bass("TRN2", target_bir_lowering=False)
    aps = {
        "resid": _d(nc, "resid", (D, T), F32, "ExternalInput"),
        "dwT": _d(nc, "dwT", (D, D), F32, "ExternalInput"),
        "ds": _d(nc, "ds", (128, D // 128), F32, "ExternalInput"),
        "hscr": _d(nc, "hscr", (D, T), BF16, "ExternalInput"),
        "out": _d(nc, "out", (D, T), F32, "ExternalOutput"),
    }
    _recycled(emit_ffn, nc, aps, D=D, I=D, T=T, NQ=2, tag="c3", down_only=True)
    return nc


def run_pipeline(inputs, runner, T=4096, NH=16, ncores=NCORES):
    D = NH * 128
    prep = host_prep(inputs, T=T, NH=NH)
    B = inputs["L0_attn_x"].shape[0]
    xs = [np.ascontiguousarray(inputs["L0_attn_x"][b].T.astype(np.float32))
          for b in range(B)]

    ncA = build_A(T, NH)
    mapsA = [dict(x=xs[b], wqT=prep["wqT"], wkT=prep["wkT"], wvT=prep["wvT"],
                  cosq=prep["cosq"], sinq=prep["sinq"], cosk=prep["cosk"],
                  sink=prep["sink"], cmask=prep["cmask"], ident=prep["ident"])
             for b in range(B)]
    resA = runner(ncA, mapsA, ["xn0", "x1p"])

    ncB = build_ffn(T, 0)
    mapsB = [dict(xn=resA[b]["xn0"], resid=resA[b]["x1p"], gwT=prep["gwT0"],
                  uwT=prep["uwT0"], dwT=prep["dwT0"], gs=prep["gs0"],
                  us=prep["us0"], ds=prep["ds0"]) for b in range(B)]
    resB = runner(ncB, mapsB, ["out"])

    ncC1 = build_C1(T, NH)
    mapsC1 = [dict(x1=resB[b]["out"], qw=prep["qw"], kw=prep["kw"], vw=prep["vw"],
                   bw=prep["bw"], f1w=prep["f1w"], g1w=prep["g1w"], cvw=prep["cvw"],
                   f2wT=prep["f2wT"], g2wT=prep["g2wT"], dtb=prep["dtb"],
                   g2b=prep["g2b"], alog=prep["alog"]) for b in range(B)]
    resC1 = runner(ncC1, mapsC1, ["xn1", "q", "k", "v", "bsg", "gd", "gsig", "ldf"])

    ncC2 = build_C2(T, NH)
    mapsC2 = [dict(q=resC1[b]["q"], k=resC1[b]["k"], v=resC1[b]["v"],
                   gd=resC1[b]["gd"], bsg=resC1[b]["bsg"], gsig=resC1[b]["gsig"],
                   ldf=resC1[b]["ldf"], mlog=prep["mlog"], onw=prep["onw"],
                   masku=prep["masku"], oneh16=prep["oneh16"],
                   pall=prep["pall"], i4t=prep["i4t"],
                   ident=prep["ident"]) for b in range(B)]
    resC2 = runner(ncC2, mapsC2, ["osc"])

    ncC3 = build_C3(T)
    mapsC3 = [dict(resid=resB[b]["out"], dwT=prep["opT"], ds=prep["dsone"],
                   hscr=resC2[b]["osc"]) for b in range(B)]
    resC3 = runner(ncC3, mapsC3, ["out"])

    ncB1 = build_ffn(T, 1)
    mapsB1 = [dict(xn=resC1[b]["xn1"], resid=resC3[b]["out"], gwT=prep["gwT1"],
                   uwT=prep["uwT1"], dwT=prep["dwT1"], gs=prep["gs1"],
                   us=prep["us1"], ds=prep["ds1"]) for b in range(B)]
    resB1 = runner(ncB1, mapsB1, ["out"])

    out = np.stack([resB1[b]["out"].T for b in range(B)], axis=0)
    return np.ascontiguousarray(out.astype(np.float32))


def kernel_multilaunch(**inputs):
    def runner(nc, in_maps, outnames):
        res = run_bass_kernel_spmd(nc, in_maps, core_ids=list(range(len(in_maps))))
        return res.results
    return run_pipeline(inputs, runner, T=4096, NH=16)


def kernel(**inputs):
    return kernel_fused(**inputs)

def build_full(T=4096, NH=16, I=5632):
    D = NH * 128
    nc = bass.Bass("TRN2", target_bir_lowering=False)
    EI, EO, IN = "ExternalInput", "ExternalOutput", "Internal"
    a = {}
    a["x"] = _d(nc, "x", (D, T), F32, EI)
    for n, sh in (("wqT", (D, D)), ("wkT", (D, 128)), ("wvT", (D, 128)),
                  ("cosq", (64, T)), ("sinq", (64, T)), ("cosk", (64, T)),
                  ("sink", (64, T)),
                  ("gwT0", (D, I)), ("uwT0", (D, I)), ("dwT0", (I, D)),
                  ("gs0", (128, I // 128)), ("us0", (128, I // 128)), ("ds0", (128, D // 128)),
                  ("gwT1", (D, I)), ("uwT1", (D, I)), ("dwT1", (I, D)),
                  ("gs1", (128, I // 128)), ("us1", (128, I // 128)), ("ds1", (128, D // 128)),
                  ("qw", (D, D)), ("kw", (D, 4 * D)), ("vw", (D, 4 * D)),
                  ("bw", (D, 64)), ("f1w", (D, 128)), ("g1w", (D, 128)),
                  ("cvw", (128, (NH + 8 * NH) * 4)), ("f2wT", (128, D)),
                  ("g2wT", (128, D)), ("dtb", (128, NH)), ("g2b", (128, NH)),
                  ("alog", (1, NH)), ("mlog", (NH, 4)), ("onw", (128, 1)),
                  ("opT", (D, D)), ("dsone", (128, NH))):
        a[n] = _d(nc, n, sh, F32, EI)
    for n, sh, dt in (("cmask", (128, 2048), BF16), ("ident", (128, 128), BF16),
                      ("masku", (5 * 128, NH * 128), mybir.dt.uint8),
                      ("oneh16", (NH, NH * 128), BF16),
                      ("pall", (128, 32), BF16), ("i4t", (4, 128), BF16)):
        a[n] = _d(nc, n, sh, dt, EI)
    a["xn0"] = _d(nc, "xn0", (D, T), BF16, IN)
    a["x1p"] = _d(nc, "x1p", (D, T), F32, IN)
    a["x1"] = _d(nc, "x1", (D, T), F32, IN)
    a["h0"] = _d(nc, "h0", (I, T), BF16, IN)
    a["xn1"] = _d(nc, "xn1", (D, T), BF16, IN)
    a["q"] = _d(nc, "q", (D, T), BF16, IN)
    a["k"] = _d(nc, "k", (4 * D, T), BF16, IN)
    a["v"] = _d(nc, "v", (4 * D, T), BF16, IN)
    a["bsg"] = _d(nc, "bsg", (64, T), BF16, IN)
    a["f1s"] = _d(nc, "f1s", (128, T), BF16, IN)
    a["g1s"] = _d(nc, "g1s", (128, T), BF16, IN)
    a["gd"] = _d(nc, "gd", (D, T), F32, IN)
    a["gsig"] = _d(nc, "gsig", (D, T), BF16, IN)
    a["ldf"] = _d(nc, "ldf", (128, NH), F32, IN)
    a["wmxs"] = _d(nc, "wmxs", (NH, 4), F32, IN)
    a["osc"] = _d(nc, "osc", (D, T), BF16, IN)
    a["x2p"] = _d(nc, "x2p", (D, T), F32, IN)
    a["h1"] = _d(nc, "h1", (I, T), BF16, IN)
    a["out"] = _d(nc, "out", (D, T), F32, EO)

    _recycled(emit_ln, nc, {"x": a["x"], "xn": a["xn0"]}, D=D, T=T, tag="ln0")
    _recycled(emit_attn0, nc, {"x": a["x"], "xn": a["xn0"], "wqT": a["wqT"], "wkT": a["wkT"],
                    "wvT": a["wvT"], "cosq": a["cosq"], "sinq": a["sinq"],
                    "cosk": a["cosk"], "sink": a["sink"], "cmask": a["cmask"],
                    "ident": a["ident"], "x1p": a["x1p"]}, NH=NH, T=T, tag="a0")
    _recycled(emit_ffn, nc, {"xn": a["xn0"], "resid": a["x1p"], "gwT": a["gwT0"],
                  "uwT": a["uwT0"], "dwT": a["dwT0"], "gs": a["gs0"],
                  "us": a["us0"], "ds": a["ds0"], "hscr": a["h0"],
                  "out": a["x1"]}, D=D, I=I, T=T, NQ=4, tag="f0")
    _recycled(emit_ln, nc, {"x": a["x1"], "xn": a["xn1"]}, D=D, T=T, tag="ln1")
    _recycled(emit_l1proj, nc, {"xn1": a["xn1"], "q": a["q"], "k": a["k"], "v": a["v"],
                     "bsg": a["bsg"], "f1s": a["f1s"], "g1s": a["g1s"],
                     "qw": a["qw"], "kw": a["kw"], "vw": a["vw"], "bw": a["bw"],
                     "f1w": a["f1w"], "g1w": a["g1w"], "cvw": a["cvw"]},
                NH=NH, T=T, tag="c1")
    _recycled(emit_l1bg, nc, {"f1s": a["f1s"], "g1s": a["g1s"], "f2wT": a["f2wT"],
                   "g2wT": a["g2wT"], "dtb": a["dtb"], "alog": a["alog"],
                   "g2b": a["g2b"], "gd": a["gd"], "gsig": a["gsig"],
                   "ldf": a["ldf"]}, NH=NH, T=T, tag="cb")
    _recycled(emit_scan, nc, {"q": a["q"], "k": a["k"], "v": a["v"], "gd": a["gd"],
                   "bsg": a["bsg"], "gsig": a["gsig"], "ldf": a["ldf"],
                   "mlog": a["mlog"], "onw": a["onw"], "masku": a["masku"],
                   "oneh16": a["oneh16"], "pall": a["pall"], "i4t": a["i4t"],
                   "ident": a["ident"], "wmxs": a["wmxs"], "osc": a["osc"]},
              NH=NH, T=T, tag="s2")
    _recycled(emit_ffn, nc, {"resid": a["x1"], "dwT": a["opT"], "ds": a["dsone"],
                  "hscr": a["osc"], "out": a["x2p"]}, D=D, I=D, T=T, NQ=2,
             tag="c3", down_only=True)
    _recycled(emit_ffn, nc, {"xn": a["xn1"], "resid": a["x2p"], "gwT": a["gwT1"],
                  "uwT": a["uwT1"], "dwT": a["dwT1"], "gs": a["gs1"],
                  "us": a["us1"], "ds": a["ds1"], "hscr": a["h1"],
                  "out": a["out"]}, D=D, I=I, T=T, NQ=4, tag="f1")
    return nc


def kernel_fused(**inputs):
    T, NH = 4096, 16
    prep = host_prep(inputs, T=T, NH=NH)
    B = inputs["L0_attn_x"].shape[0]
    xs = [np.ascontiguousarray(inputs["L0_attn_x"][b].T.astype(np.float32))
          for b in range(B)]
    nc = build_full(T=T, NH=NH)
    names = ("wqT wkT wvT cosq sinq cosk sink cmask ident gwT0 uwT0 dwT0 gs0 us0 "
             "ds0 gwT1 uwT1 dwT1 gs1 us1 ds1 qw kw vw bw f1w g1w cvw f2wT g2wT "
             "dtb g2b alog mlog onw opT dsone masku oneh16 pall i4t").split()
    base = {n: prep[n] for n in names}
    in_maps = [dict(base, x=xs[b]) for b in range(B)]
    res = run_bass_kernel_spmd(nc, in_maps, core_ids=list(range(B)))
    out = np.stack([res.results[b]["out"].T for b in range(B)], axis=0)
    return np.ascontiguousarray(out.astype(np.float32))
